# version 13
# baseline (speedup 1.0000x reference)
import numpy as np
import ml_dtypes
import concourse.bass as bass
import concourse.mybir as mybir
from concourse.bass_utils import run_bass_kernel_spmd
from contextlib import ExitStack

F32 = mybir.dt.float32
BF16 = mybir.dt.bfloat16
F16 = mybir.dt.float16
I32 = mybir.dt.int32
AF = mybir.ActivationFunctionType
ALU = mybir.AluOpType
AX = mybir.AxisListType
BF = ml_dtypes.bfloat16

D = 1024
S = 2048
NSEQ = 2
NC = 8
NH = 16
HS = 64
NCORES = 8
L0I = 4096
L1I = 3072

UPTO = 4
SKIP_L1A = False
NO_ROPE = False
NO_TABLE = False
ROPE_VAR = 0
DBG = None


class Prog:
    ENGS = ("pe", "act", "dve", "pool", "sp")

    def __init__(self, nc, ctx):
        self.nc = nc
        self._ctx = ctx
        self.eng = dict(pe=nc.tensor, act=nc.scalar, dve=nc.vector,
                        pool=nc.gpsimd, sp=nc.sync)
        self.sems = {}
        self.cnt = {}
        for s in ("pe", "act", "dve", "pool", "dma", "dmag", "cc"):
            self.sems[s] = ctx.enter_context(nc.semaphore("S_" + s))
            self.cnt[s] = 0
        self.know = {e: dict.fromkeys(self.sems, 0) for e in self.ENGS}
        self._warm_tok = None

    def stream_sem(self, name):
        if name not in self.sems:
            self.sems[name] = self._ctx.enter_context(self.nc.semaphore("S_" + name))
            self.cnt[name] = 0
            for e in self.know:
                self.know[e][name] = 0
        return name

    def _wait(self, eng, toks):
        best = {}
        for t in toks:
            if t is None:
                continue
            if len(t) == 3:
                s, v, _ = t
                v = self.cnt[s]
            else:
                s, v = t
            if v > best.get(s, 0):
                best[s] = v
        for s, v in best.items():
            if self.know[eng][s] < v:
                self.eng[eng].wait_ge(self.sems[s], v)
                self.know[eng][s] = v

    def op(self, eng, deps, emit):
        self._wait(eng, deps)
        inst = emit(self.eng[eng])
        self.cnt[eng] += 1
        inst.then_inc(self.sems[eng], 1)
        return (eng, self.cnt[eng])

    def dma(self, queue, deps, out, in_, stream=None):
        self._wait(queue, deps)
        if stream is None:
            sem = "dma" if queue == "sp" else "dmag"
        else:
            sem = self.stream_sem(stream)
        inst = self.eng[queue].dma_start(out=out, in_=in_)
        self.cnt[sem] += 16
        inst.then_inc(self.sems[sem], 16)
        if stream is None:
            return (sem, self.cnt[sem], "B")
        return (sem, self.cnt[sem])

    def cc_max(self, deps, in_ap, out_ap):
        self._wait("pool", deps)
        inst = self.eng["pool"].collective_compute(
            "AllReduce", ALU.max, replica_groups=[list(range(NCORES))],
            ins=[in_ap.opt()], outs=[out_ap.opt()])
        self.cnt["cc"] += 1
        inst.then_inc(self.sems["cc"], 1)
        return ("cc", self.cnt["cc"])

    def now(self, eng):
        return (eng, self.cnt[eng])

    def warm(self, lhs, rhs, tgt):
        self._wait("pe", [self._warm_tok])
        inst = self.eng["pe"].matmul(tgt, lhs, rhs, start=True, stop=True)
        self.cnt["pe"] += 1
        inst.then_inc(self.sems["pe"], 1)
        self._warm_tok = ("pe", self.cnt["pe"])

    def fence(self):
        toks = [(s, self.cnt[s]) for s in self.cnt]
        for e in self.ENGS:
            self._wait(e, toks)


class Bank:
    def __init__(self, t):
        self.t = t
        self.free = None


def make_consts(nc):
    c = {}
    c["ident_b"] = nc.inline_tensor(np.eye(128).astype(BF), name="c_ident_b")
    c["ident_f"] = nc.inline_tensor(np.eye(128, dtype=np.float32), name="c_ident_f")
    tri = (np.arange(128)[:, None] <= np.arange(128)[None, :])
    c["tri_b"] = nc.inline_tensor(tri.astype(BF), name="c_tri_b")
    c["ones_row_b"] = nc.inline_tensor(np.ones((1, 128), BF), name="c_ones_row_b")
    c["ones_row_f"] = nc.inline_tensor(np.ones((1, 128), np.float32), name="c_ones_row_f")
    c["ones_col_b"] = nc.inline_tensor(np.ones((128, 1), BF), name="c_ones_col_b")
    c["ones_col_f"] = nc.inline_tensor(np.ones((128, 1), np.float32), name="c_ones_col_f")
    inv_freq = 1.0 / 10000.0 ** (np.arange(0, 32, 2, dtype=np.float64) / 32.0)
    ang = np.arange(S, dtype=np.float64)[None, :] * np.repeat(inv_freq, 2)[:, None]
    cosf = np.ones((128, S), np.float64)
    sinf = np.zeros((128, S), np.float64)
    for base in (0, 64):
        cosf[base:base + 32] = np.cos(ang)
        sinf[base:base + 32] = np.sin(ang)
    c["cosf"] = nc.inline_tensor(cosf.astype(BF), name="c_cosf")
    c["sinf"] = nc.inline_tensor(sinf.astype(BF), name="c_sinf")
    R = np.zeros((128, 128), np.float64)
    for base in (0, 64):
        for i in range(16):
            R[base + 2 * i, base + 2 * i + 1] = -1.0
            R[base + 2 * i + 1, base + 2 * i] = 1.0
    c["rotT_b"] = nc.inline_tensor(np.ascontiguousarray(R.T).astype(BF), name="c_rotT_b")
    return c


def build_program():
    nc = bass.Bass("TRN2", target_bir_lowering=False, debug=False,
                   dynamic_dma_scratch_size=16384)

    x_in = nc.dram_tensor("x_in", [NSEQ, S, D], F32, kind="ExternalInput")
    y_out = nc.dram_tensor("y_out", [NSEQ, S, D], F32, kind="ExternalOutput")
    dbg_kind = "ExternalOutput" if DBG else "Internal"
    dbg16 = nc.dram_tensor("dbg16", [128, 16384], BF16, kind=dbg_kind)
    dbgf = nc.dram_tensor("dbgf", [128, 16, D], F32, kind=dbg_kind)
    ew = {}
    for nm, shape in [
        ("l0_aw", [D, 3 * D]), ("l0_bqk", [128, 16]), ("l0_bv", [1, D]),
        ("l0_ow", [D, D]), ("l0_ob", [1, D]),
        ("l0_wgT", [D, L0I]), ("l0_wuT", [D, L0I]), ("l0_wdT", [L0I, D]),
        ("l1_aw", [D, 3 * D]), ("l1_ow", [D, D]),
        ("l1_wg", [L1I, D]), ("l1_wu", [L1I, D]), ("l1_wd", [D, L1I]),
    ]:
        ew[nm] = nc.dram_tensor(nm, shape, F32, kind="ExternalInput")

    sc = {}
    sc["xcur"] = nc.dram_tensor("xcur", [NSEQ, S, D], F32, kind="Internal")
    for L in (0, 1):
        sc[f"wqk{L}"] = nc.dram_tensor(f"wqk{L}", [16, 8, 128, 128], BF16, kind="Internal")
        sc[f"wv{L}"] = nc.dram_tensor(f"wv{L}", [8, 128, D], BF16, kind="Internal")
        sc[f"wo{L}"] = nc.dram_tensor(f"wo{L}", [8, 128, D], BF16, kind="Internal")
    sc["l0gT"] = nc.dram_tensor("l0gT", [32, 8, 128, 128], BF16, kind="Internal")
    sc["l0uT"] = nc.dram_tensor("l0uT", [32, 8, 128, 128], BF16, kind="Internal")
    sc["l0dT"] = nc.dram_tensor("l0dT", [32, 128, D], BF16, kind="Internal")
    sc["l1gT"] = nc.dram_tensor("l1gT", [24, 8, 128, 128], BF16, kind="Internal")
    sc["l1uT"] = nc.dram_tensor("l1uT", [24, 8, 128, 128], BF16, kind="Internal")
    sc["l1dT"] = nc.dram_tensor("l1dT", [24, 128, D], BF16, kind="Internal")
    sc["sp_up"] = nc.dram_tensor("sp_up", [NSEQ, 24, 128, S], F16, kind="Internal")
    sc["sp_gate"] = nc.dram_tensor("sp_gate", [NSEQ, 24, 128, S], F16, kind="Internal")
    sc["cc_in"] = nc.dram_tensor("cc_in", [1, 8], F32, kind="Internal")
    sc["cc_out"] = nc.dram_tensor("cc_out", [1, 8], F32, kind="Internal")

    consts = make_consts(nc)

    ctx = ExitStack()
    with ctx:
        p = Prog(nc, ctx)
        st = ctx.enter_context

        identb = st(nc.sbuf_tensor("identb", [128, 128], BF16))
        trib = st(nc.sbuf_tensor("trib", [128, 128], BF16))
        identf = st(nc.sbuf_tensor("identf", [128, 128], F32))
        ones_r_b = st(nc.sbuf_tensor("ones_r_b", [1, 128], BF16))
        ones_r_f = st(nc.sbuf_tensor("ones_r_f", [1, 128], F32))
        ones_c_b = st(nc.sbuf_tensor("ones_c_b", [128, 1], BF16))
        ones_c_f = st(nc.sbuf_tensor("ones_c_f", [128, 1], F32))
        rotTb = st(nc.sbuf_tensor("rotTb", [128, 128], BF16))
        epsc = st(nc.sbuf_tensor("epsc", [128, 1], F32))
        bqk = st(nc.sbuf_tensor("bqk", [128, 16], F32))
        bvrow = st(nc.sbuf_tensor("bvrow", [1, D], BF16))
        obrow = st(nc.sbuf_tensor("obrow", [1, D], BF16))
        stat = st(nc.sbuf_tensor("stat", [128, 64], F32))
        srow = st(nc.sbuf_tensor("srow", [1, 64], F32))
        b_qk2 = st(nc.sbuf_tensor("b_qk2", [1, 80], F32))
        bbias = st(nc.sbuf_tensor("bbias", [128, 8], F32))
        mxrow = st(nc.sbuf_tensor("mxrow", [1, 16], F32))
        gb = st(nc.sbuf_tensor("gb", [128, 12], F32))

        xT = st(nc.sbuf_tensor("xT", [128, NC, S], BF16))
        kT = st(nc.sbuf_tensor("kT", [128, NC, S], BF16))
        qq = st(nc.sbuf_tensor("qq", [128, NC, S], BF16))
        vT = st(nc.sbuf_tensor("vT", [128, 16, NH, HS + 1], BF16))
        woB = st(nc.sbuf_tensor("woB", [128, NC, D], BF16))
        wstr = st(nc.sbuf_tensor("wstr", [128, 2, 8, 128], BF16))
        eT = st(nc.sbuf_tensor("eT", [128, 3, 512], BF16))
        aT = st(nc.sbuf_tensor("aT", [128, NC, 512], BF16))
        _aT_addr = nc.lookup_mloc(aT).addr
        cosf = nc.alloc_sbuf_tensor_at("cosf", [128, S], BF16, offset=_aT_addr)
        sinf = nc.alloc_sbuf_tensor_at("sinf", [128, S], BF16, offset=_aT_addr + 4096)
        it2 = nc.alloc_sbuf_tensor_at("it2", [128, 512], I32, offset=_aT_addr)
        ftA = nc.alloc_sbuf_tensor_at("ftA", [128, 512], F32, offset=_aT_addr + 2048)
        ftB = nc.alloc_sbuf_tensor_at("ftB", [128, 512], F32, offset=_aT_addr + 4096)
        sp16b = nc.alloc_sbuf_tensor_at("sp16b", [128, 2, 512], F16, offset=_aT_addr + 6144)
        sqb = st(nc.sbuf_tensor("sqb", [128, 1536], BF16))
        xin = st(nc.sbuf_tensor("xin", [128, 2, D], F32))
        rt = st(nc.sbuf_tensor("rt", [128, D], F32))
        sqf = st(nc.sbuf_tensor("sqf", [128, D], F32))
        xot = st(nc.sbuf_tensor("xot", [128, 2, D], F32))
        ft1 = st(nc.sbuf_tensor("ft1", [128, 512], F32))
        ft2 = st(nc.sbuf_tensor("ft2", [128, 512], F32))
        ft3 = st(nc.sbuf_tensor("ft3", [128, 512], F32))
        it1 = st(nc.sbuf_tensor("it1", [128, 512], I32))
        mtile = st(nc.sbuf_tensor("mtile", [128, 512], F32))
        sxr = st(nc.sbuf_tensor("sxr", [1, 512], F32))
        sp16 = nc.alloc_sbuf_tensor_at("sp16", [128, 2, 512], F16,
                                       offset=nc.lookup_mloc(mtile).addr)
        spu16 = nc.alloc_sbuf_tensor_at("spu16", [128, 2, 512], F16,
                                        offset=nc.lookup_mloc(ft3).addr)
        sxb = st(nc.sbuf_tensor("sxb", [128, 512], F32))
        ft4 = nc.alloc_sbuf_tensor_at("ft4", [128, 512], F32,
                                      offset=nc.lookup_mloc(mtile).addr)
        kT_addr = nc.lookup_mloc(kT).addr
        qq_addr = nc.lookup_mloc(qq).addr
        assert qq_addr == kT_addr + 32768, (kT_addr, qq_addr)
        wf32 = nc.alloc_sbuf_tensor_at("wf32", [128, 2, 3072], F32, offset=kT_addr)
        wb16 = nc.alloc_sbuf_tensor_at("wb16", [128, 3072], BF16, offset=qq_addr)
        wdT0 = nc.alloc_sbuf_tensor_at("wdT0", [128, 32, D], BF16, offset=kT_addr)
        wdT1 = nc.alloc_sbuf_tensor_at("wdT1", [128, 24, D], BF16, offset=kT_addr)
        vT_addr = nc.lookup_mloc(vT).addr
        guh = nc.alloc_sbuf_tensor_at("guh", [128, 32, 512], BF16, offset=vT_addr)
        oq1 = nc.alloc_sbuf_tensor_at("oq1", [128, 24, 512], BF16, offset=vT_addr)

        pb = [Bank(st(nc.psum_tensor(f"pb{i}", [128, 512], F32))) for i in range(6)]
        pt = [Bank(st(nc.psum_tensor(f"pt{i}", [128, 128], BF16))) for i in range(2)]

        for tile, key in ((identb, "ident_b"), (identf, "ident_f"), (trib, "tri_b"),
                          (ones_r_b, "ones_row_b"), (ones_r_f, "ones_row_f"),
                          (ones_c_b, "ones_col_b"), (ones_c_f, "ones_col_f"),
                          (rotTb, "rotT_b")):
            p.dma("sp", [], tile[:], consts[key].ap())
        t_eps = p.op("dve", [], lambda e: e.memset(epsc[:], 1e-5))
        p.op("dve", [], lambda e: e.memset(mxrow[:], 0.0))
        p.dma("sp", [], bqk[:], ew["l0_bqk"].ap())
        t_bv = p.dma("sp", [], rt[0:1, 0:D], ew["l0_bv"].ap())
        t_ob = p.dma("sp", [], sqf[0:1, 0:D], ew["l0_ob"].ap())
        t_bvb = p.op("dve", [t_bv], lambda e: e.tensor_copy(out=bvrow[:], in_=rt[0:1, 0:D]))
        t_obb = p.op("dve", [t_ob], lambda e: e.tensor_copy(out=obrow[:], in_=sqf[0:1, 0:D]))

        def prep_attn(L):
            p.fence()
            aw = ew[f"l{L}_aw"].ap()
            ow = ew[f"l{L}_ow"].ap()
            buf_free = [None, None]
            t_cast_prev = None
            for c in range(NC):
                buf = wf32[:, c % 2]
                tl = p.dma("sp" if c % 2 == 0 else "pool", [buf_free[c % 2]], buf,
                           aw[c * 128:(c + 1) * 128, :],
                           stream=f"wf{c % 2}" if c % 2 == 0 else f"wfp{c % 2}")
                tc = p.op("dve", [tl] + (t_cast_prev if isinstance(t_cast_prev, list) else [t_cast_prev]), lambda e, b=buf: e.tensor_copy(
                    out=wb16[:], in_=b))
                buf_free[c % 2] = tc
                if L == 0:
                    wv5 = wb16[:].rearrange("p (hc h part e) -> p hc h part e",
                                            hc=8, h=2, part=3)
                    qsl = lambda hc: wv5[:, hc, :, 0, :]
                    ksl = lambda hc: wv5[:, hc, :, 1, :]
                    vall = wv5[:, :, :, 2, :]
                else:
                    wv4 = wb16[:].rearrange("p (hc part e) -> p hc part e",
                                            hc=8, part=3)
                    qsl = lambda hc: wv4[:, hc, 0, :]
                    ksl = lambda hc: wv4[:, hc, 1, :]
                    vall = wv4[:, :, 2, :]
                oqv = sc[f"wqk{L}"].ap()[0:8, c].rearrange("hc p e -> p hc e")
                okv = sc[f"wqk{L}"].ap()[8:16, c].rearrange("hc p e -> p hc e")
                if L == 0:
                    for h in range(2):
                        p.dma("sp", [tc], oqv[:, :, h * 64:(h + 1) * 64], wv5[:, :, h, 0, :])
                        p.dma("sp", [tc], okv[:, :, h * 64:(h + 1) * 64], wv5[:, :, h, 1, :])
                else:
                    p.dma("sp", [tc], oqv, wv4[:, :, 0, :])
                    p.dma("sp", [tc], okv, wv4[:, :, 1, :])
                tstores = [p.dma("sp", [tc], sc[f"wv{L}"].ap()[c], vall)]
                t_cast_prev = tstores[-1]
            for c in range(NC):
                buf = wf32[:, c % 2]
                tl = p.dma("sp" if c % 2 == 0 else "pool", [buf_free[c % 2], t_cast_prev],
                           buf[:, 0:D],
                           ow[c * 128:(c + 1) * 128, :],
                           stream=f"wf{c % 2}" if c % 2 == 0 else f"wfp{c % 2}")
                tc = p.op("dve", [tl] + (t_cast_prev if isinstance(t_cast_prev, list) else [t_cast_prev]), lambda e, b=buf: e.tensor_copy(
                    out=wb16[:, 0:D], in_=b[:, 0:D]))
                t_cast_prev = p.dma("sp", [tc], sc[f"wo{L}"].ap()[c], wb16[:, 0:D])
                buf_free[c % 2] = tc

        def prep_tern(nm, src):
            p.fence()
            W = ew[src].ap()
            rows, cols = W.tensor.shape
            nch = rows // 128
            SLAB = min(cols, 2048)
            nsl = cols // SLAB
            acc = stat[:, 0:1]
            tz = p.op("dve", [p.now("dve")], lambda e: e.memset(acc, 0.0))
            buf_free = [None, None]
            tacc = tz
            for c in range(nch):
                for sl in range(nsl):
                    i = (c * nsl + sl) % 2
                    buf = wf32[:, i, 0:SLAB]
                    tl = p.dma("sp" if i == 0 else "pool", [buf_free[i]], buf,
                               W[c * 128:(c + 1) * 128, sl * SLAB:(sl + 1) * SLAB],
                               stream=f"wf{i}" if i == 0 else f"wfp{i}")
                    tr = p.op("dve", [tl, tacc], lambda e, b=buf: e.tensor_reduce(
                        out=stat[:, 1:2], in_=b, axis=AX.X, op=ALU.add,
                        apply_absolute_value=True))
                    tacc = p.op("dve", [tr], lambda e: e.tensor_add(
                        out=acc, in0=acc, in1=stat[:, 1:2]))
                    buf_free[i] = tr
            tsum = p.op("pe", [tacc, pb[0].free], lambda e: e.matmul(
                pb[0].t[0:1, 0:1], ones_c_f[:], acc, start=True, stop=True))
            tm = p.op("dve", [tsum], lambda e: e.tensor_scalar(
                out=srow[0:1, 0:1], in0=pb[0].t[0:1, 0:1], scalar1=1.0 / float(rows * cols),
                scalar2=1e-5, op0=ALU.mult, op1=ALU.max))
            pb[0].free = tm
            trc = p.op("dve", [tm], lambda e: e.reciprocal(
                out=srow[0:1, 1:2], in_=srow[0:1, 0:1]))
            tb = p.op("pe", [trc, pb[1].free], lambda e: e.matmul(
                pb[1].t[:, 0:2], ones_r_f[:], srow[0:1, 0:2], start=True, stop=True))
            sidx = {"l0gT": 2, "l0uT": 4, "l0dT": 6}[nm]
            tbc = p.op("dve", [tb], lambda e: e.tensor_copy(
                out=stat[:, sidx:sidx + 2], in_=pb[1].t[:, 0:2]))
            pb[1].free = tbc
            wb_free = None
            clip_last = [None, None]
            for c in range(nch):
                for sl in range(nsl):
                    i = (c * nsl + sl) % 2
                    buf = wf32[:, i, 0:SLAB]
                    tl = p.dma("sp" if i == 0 else "pool", [buf_free[i]], buf,
                               W[c * 128:(c + 1) * 128, sl * SLAB:(sl + 1) * SLAB],
                               stream=f"wf{i}" if i == 0 else f"wfp{i}")
                    ti_last = None
                    for j in range(SLAB // 512):
                        cs = slice(sl * SLAB + j * 512, sl * SLAB + (j + 1) * 512)
                        jj = (j % 2)
                        IT = it1 if jj == 0 else it2
                        ti = p.op("dve", [tl, tbc, clip_last[jj]], lambda e, b=buf, j=j, IT=IT: e.tensor_scalar(
                            out=IT[:], in0=b[:, j * 512:(j + 1) * 512],
                            scalar1=stat[:, sidx + 1:sidx + 2],
                            scalar2=None, op0=ALU.mult))
                        ti_last = ti
                        clip_last[jj] = p.op("dve", [ti, wb_free], lambda e, cs=cs, IT=IT: e.tensor_scalar(
                            out=wb16[:, cs.start - sl * SLAB:cs.stop - sl * SLAB], in0=IT[:],
                            scalar1=1.0, scalar2=-1.0,
                            op0=ALU.min, op1=ALU.max))
                    tclip = clip_last[0 if (SLAB // 512) % 2 else 1]
                    tclip = p.now("dve")
                    if nm == "l0dT":
                        tst = p.dma("sp", [tclip], sc[nm].ap()[c][:, sl * SLAB:(sl + 1) * SLAB],
                                    wb16[:, 0:SLAB])
                    else:
                        nic = SLAB // 128
                        odst = sc[nm].ap()[sl * nic:(sl + 1) * nic, c].rearrange(
                            "ic p e -> p ic e")
                        tst = p.dma("sp", [tclip], odst,
                                    wb16[:, 0:SLAB].rearrange("p (ic e) -> p ic e", e=128))
                    buf_free[i] = ti_last
                    wb_free = tst

        def prep_lpbq(nm, src):
            p.fence()
            W = ew[src].ap()
            rows, cols = W.tensor.shape
            nch = rows // 128
            ngrp = cols // 32
            buf_free = [None, None]
            sq_slot = [None, None, None, None]
            tlast = None
            for c in range(nch):
                buf = wf32[:, c % 2][:, 0:cols]
                tl = p.dma("sp" if c % 2 == 0 else "pool", [buf_free[c % 2]], buf,
                           W[c * 128:(c + 1) * 128, :],
                           stream=f"wf{c % 2}" if c % 2 == 0 else f"wfp{c % 2}")
                bview = buf.rearrange("p (g k) -> p g k", k=32)
                ts1 = p.op("dve", [tl, tlast], lambda e, b=bview, n=ngrp: e.tensor_reduce(
                    out=ft1[:, 0:n], in_=b, axis=AX.X, op=ALU.max,
                    apply_absolute_value=True))
                ts2 = p.op("dve", [ts1], lambda e, n=ngrp: e.tensor_scalar(
                    out=ft2[:, 0:n], in0=ft1[:, 0:n], scalar1=1.0 / 7.0,
                    scalar2=1e-12, op0=ALU.mult, op1=ALU.max))
                ts3 = p.op("dve", [ts2], lambda e, n=ngrp: e.reciprocal(
                    out=ft3[:, 0:n], in_=ft2[:, 0:n]))
                tq = None
                for j in range(ngrp // 16):
                    gsl = slice(j * 16, (j + 1) * 16)
                    csl = slice(j * 512, (j + 1) * 512)
                    wv = bview[:, gsl, :]
                    rsb = bass.broadcast_tensor_aps(ft3[:, gsl][:, :, None], wv)[0]
                    sb_ = bass.broadcast_tensor_aps(ft2[:, gsl][:, :, None], wv)[0]
                    ti = p.op("dve", [ts3, tq], lambda e, wv=wv, rsb=rsb: e.tensor_tensor(
                        out=it1[:].rearrange("p (g k) -> p g k", k=32),
                        in0=wv, in1=rsb, op=ALU.mult))
                    tc1 = p.op("dve", [ti], lambda e: e.tensor_scalar(
                        out=ft4[:], in0=it1[:], scalar1=7.0, scalar2=-8.0,
                        op0=ALU.min, op1=ALU.max))
                    tq = p.op("dve", [tc1], lambda e, sb_=sb_, csl=csl: e.tensor_tensor(
                        out=wb16[:, csl].rearrange("p (g k) -> p g k", k=32),
                        in0=ft4[:].rearrange("p (g k) -> p g k", k=32),
                        in1=sb_, op=ALU.mult))
                njc = cols // 128
                gsz = min(12, njc)
                tev_grp = []
                for jc in range(njc):
                    b_ = pt[jc % 2]
                    ttr = p.op("pe", [tq, b_.free], lambda e, jc=jc, b_=b_: e.transpose(
                        b_.t[:], wb16[:, jc * 128:(jc + 1) * 128], identb[:]))
                    slot = jc % gsz
                    tev = p.op("dve", [ttr, sq_slot[0] if slot == 0 else None],
                               lambda e, b_=b_, slot=slot: e.tensor_copy(
                                   out=sqb[:, slot * 128:slot * 128 + 128], in_=b_.t[:]))
                    b_.free = tev
                    tev_grp.append(tev)
                    if slot == gsz - 1:
                        j0 = jc - gsz + 1
                        if nm == "l1dT":
                            odst = sc[nm].ap()[j0:j0 + gsz][:, :, c * 128:(c + 1) * 128]                                 .rearrange("t p e -> p t e")
                        else:
                            odst = sc[nm].ap()[c, j0:j0 + gsz].rearrange("t p e -> p t e")
                        tst = p.dma("sp", tev_grp, odst,
                                    sqb[:, 0:gsz * 128].rearrange("p (t e) -> p t e", e=128))
                        sq_slot[0] = tst
                        tev_grp = []
                tlast = tq
                buf_free[c % 2] = tq

        if UPTO >= 0:
            with nc.named_scope("prep_attn0"):
                prep_attn(0)
                p.fence()
        if UPTO >= 2:
            with nc.named_scope("prep_tern"):
                prep_tern("l0gT", "l0_wgT")
                prep_tern("l0uT", "l0_wuT")
                prep_tern("l0dT", "l0_wdT")
                p.fence()
        if UPTO >= 3:
            with nc.named_scope("prep_attn1"):
                prep_attn(1)
                p.fence()
        if UPTO >= 4:
            with nc.named_scope("prep_lpbq"):
                prep_lpbq("l1gT", "l1_wg")
                prep_lpbq("l1uT", "l1_wu")
                prep_lpbq("l1dT", "l1_wd")
                p.fence()

        def emit_ln_store(tt, tdep, L, xdst, prev_tok):
            ts1 = p.op("dve", [tdep], lambda e: e.reduce_sum(
                out=stat[:, 16:17], in_=rt[:], axis=AX.X))
            ts2 = p.op("act", [tdep], lambda e: e.activation(
                out=sqf[:], in_=rt[:], func=AF.Square, accum_out=stat[:, 17:18]))
            tm = p.op("dve", [ts1], lambda e: e.tensor_scalar_mul(
                out=stat[:, 18:19], in0=stat[:, 16:17], scalar1=1.0 / D))
            tv1 = p.op("dve", [ts2], lambda e: e.tensor_scalar_mul(
                out=stat[:, 19:20], in0=stat[:, 17:18], scalar1=1.0 / D))
            tm2 = p.op("dve", [tm], lambda e: e.tensor_mul(
                out=stat[:, 20:21], in0=stat[:, 18:19], in1=stat[:, 18:19]))
            tvv = p.op("dve", [tv1, tm2], lambda e: e.tensor_sub(
                out=stat[:, 21:22], in0=stat[:, 19:20], in1=stat[:, 20:21]))
            tsd = p.op("act", [tvv, t_eps], lambda e: e.activation(
                out=stat[:, 22:23], in_=stat[:, 21:22], func=AF.Sqrt, bias=epsc[:]))
            trs = p.op("dve", [tsd], lambda e: e.reciprocal(
                out=stat[:, 23:24], in_=stat[:, 22:23]))
            buf = xot[:, tt % 2]
            tx = p.op("dve", [trs, prev_tok], lambda e, b=buf: e.tensor_scalar(
                out=b, in0=rt[:], scalar1=stat[:, 18:19], scalar2=stat[:, 23:24],
                op0=ALU.subtract, op1=ALU.mult))
            tst = p.dma("sp", [tx], xdst[tt * 128:(tt + 1) * 128, :], buf, stream=f"xo{tt % 2}")
            tmax = None
            if L == 1:
                tmx = p.op("dve", [tx], lambda e, b=buf: e.tensor_reduce(
                    out=stat[:, 24:25], in_=b, axis=AX.X, op=ALU.max,
                    apply_absolute_value=True))
                tmax = p.op("dve", [tmx], lambda e: e.tensor_tensor(
                    out=stat[:, 25:26], in0=stat[:, 25:26], in1=stat[:, 24:25],
                    op=ALU.max))
            return tst, (tx if tmax is None else tmax)

        def emit_attn(L, seq, x_src, x_dst):
            wqk = sc[f"wqk{L}"].ap()
            wv = sc[f"wv{L}"].ap()
            wo = sc[f"wo{L}"].ap()

            if L == 1:
                if NO_TABLE:
                    p.op("dve", [], lambda e: e.memset(cosf[:], 1.0))
                    p.op("dve", [], lambda e: e.memset(sinf[:], 0.0))
                else:
                    p.dma("sp", [], cosf[:], consts["cosf"].ap())
                    p.dma("sp", [], sinf[:], consts["sinf"].ap())
            sc_a1 = nc.enter_named_scope(f"at{L}{seq}_a1", False)
            xin_free = [None, None]
            t_cast_prev = [None]
            for tt in range(16):
                buf = xin[:, tt % 2]
                tl = p.dma("sp", [xin_free[tt % 2]], buf, x_src[tt * 128:(tt + 1) * 128, :], stream=f"xin{tt % 2}")
                tc = p.op("dve", [tl] + (t_cast_prev if isinstance(t_cast_prev, list) else [t_cast_prev]), lambda e, b=buf: e.tensor_copy(
                    out=wb16[:, 0:D], in_=b))
                xin_free[tt % 2] = tc
                td_cast = None
                if DBG == ("cast", seq):
                    td_cast = p.dma("sp", [tc], dbg16.ap().rearrange(
                        "p (t d) -> p t d", t=16)[:, tt], wb16[:, 0:D], stream="dbgc")
                for c in range(NC):
                    b_ = pt[c % 2]
                    ttr = p.op("pe", [tc, b_.free], lambda e, c=c, b_=b_: e.transpose(
                        b_.t[:], wb16[:, c * 128:(c + 1) * 128], identb[:]))
                    tev = p.op("dve", [ttr], lambda e, c=c, tt=tt, b_=b_: e.tensor_copy(
                        out=xT[:, c, tt * 128:(tt + 1) * 128], in_=b_.t[:]))
                    b_.free = tev
                t_cast_prev = [p.now("pe"), td_cast]
            nc.leave_named_scope(f"at{L}{seq}_a1", sc_a1[0], False)
            sc_a2 = nc.enter_named_scope(f"at{L}{seq}_a2", False)
            t_xT = p.now("dve")
            if DBG == ("xT", seq):
                p.dma("sp", [t_xT], dbg16.ap().rearrange("p (c s) -> p c s", c=NC), xT[:])

            tv_w = [p.dma("sp", [p.now("dve")], woB[:, c, :], wv[c]) for c in range(NC)]
            bank = pb[4]
            for tt in range(16):
                vsl = vT[:, tt]
                tms = p.op("dve", [bank.free], lambda e, v=vsl: e.memset(v, 1.0))
                for half in range(2):
                    tmm = None
                    for c in range(NC):
                        tmm = p.op("pe", [t_xT, tv_w[c], bank.free], lambda e, c=c, tt=tt, h=half, bk=bank: e.matmul(
                            bk.t[:], xT[:, c, tt * 128:(tt + 1) * 128],
                            woB[:, c, h * 512:(h + 1) * 512],
                            start=(c == 0), stop=(c == NC - 1 and L != 0)))
                    if L == 0:
                        tmm = p.op("pe", [tmm, t_bvb], lambda e, h=half, bk=bank: e.matmul(
                            bk.t[:], ones_r_b[:], bvrow[0:1, h * 512:(h + 1) * 512],
                            start=False, stop=True))
                    vout = vsl[:, half * 8:(half + 1) * 8, 0:64]
                    tev = p.op("dve", [tmm, tms], lambda e, vo=vout, bk=bank: e.tensor_copy(
                        out=vo, in_=bk.t[:].rearrange("p (h e) -> p h e", h=8)))
                    bank.free = tev
            t_vdone = p.now("dve")
            nc.leave_named_scope(f"at{L}{seq}_a2", sc_a2[0], False)
            sc_a3 = nc.enter_named_scope(f"at{L}{seq}_a3", False)

            w_free = [None, None]
            rope_war = [None, None]
            norm_war = [None, None]
            norm_war2 = [None]
            rope_ev = [None]

            def qk_prod(which, dst):
                hbase = 0 if which == "q" else 8
                for hc in range(8):
                    wbuf = wstr[:, hc % 2]
                    tw = p.dma("sp", [w_free[hc % 2]], wbuf[:, 0:8, :],
                               wqk[hbase + hc].rearrange("c p e -> p c e"),
                               stream=f"ws{hc % 2}")
                    for c in range(NC):
                        for g in range(4):
                            bk = pb[g]
                            p.op("pe", [tw, t_xT, bk.free], lambda e, c=c, g=g, w=wbuf, bk=bk: e.matmul(
                                bk.t[:], w[:, c, :], xT[:, c, g * 512:(g + 1) * 512],
                                start=(c == 0), stop=(c == NC - 1)))
                    for g in range(4):
                        bk = pb[g]
                        tmm = p.now("pe")
                        if L == 0 or NO_ROPE:
                            bias_ap = bqk[:, hbase + hc:hbase + hc + 1]
                            if L == 0:
                                tev = p.op("act", [tmm], lambda e, g=g, hc=hc, b=bias_ap, bk=bk: e.activation(
                                    out=dst[:, hc, g * 512:(g + 1) * 512], in_=bk.t[:],
                                    func=AF.Identity, bias=b))
                            else:
                                tev = p.op("act", [tmm], lambda e, g=g, hc=hc, bk=bk: e.activation(
                                    out=dst[:, hc, g * 512:(g + 1) * 512], in_=bk.t[:],
                                    func=AF.Identity))
                            bk.free = tev
                        else:
                            tst_ = p.op("act", [tmm, rope_war[g % 2]], lambda e, g=g, bk=bk: e.activation(
                                out=sqb[:, (g % 2) * 512:(g % 2 + 1) * 512], in_=bk.t[:],
                                func=AF.Identity))
                            if ROPE_VAR == 1:
                                tt1 = p.op("dve", [tmm, rope_ev[0]], lambda e, g=g, bk=bk: e.tensor_mul(
                                    out=ft1[:], in0=bk.t[:],
                                    in1=cosf[:, g * 512:(g + 1) * 512]))
                                bk.free = tt1
                                tev = p.op("dve", [tt1], lambda e, g=g, hc=hc: e.tensor_copy(
                                    out=dst[:, hc, g * 512:(g + 1) * 512], in_=ft1[:]))
                                rope_ev[0] = tev
                                continue_flag = True
                            else:
                                continue_flag = False
                            if continue_flag:
                                toks_done = None
                                continue
                            tt1 = p.op("dve", [tmm, tst_, rope_ev[0]], lambda e, g=g, bk=bk: e.tensor_mul(
                                out=ft1[:], in0=bk.t[:],
                                in1=cosf[:, g * 512:(g + 1) * 512]))
                            bk.free = tt1
                            rbk = pb[4]
                            trot = p.op("pe", [tst_, rbk.free], lambda e, g=g, rbk=rbk: e.matmul(
                                rbk.t[:], rotTb[:], sqb[:, (g % 2) * 512:(g % 2 + 1) * 512],
                                start=True, stop=True))
                            rope_war[g % 2] = trot
                            tt2 = p.op("dve", [trot, rope_ev[0]], lambda e, g=g, rbk=rbk: e.tensor_mul(
                                out=ft2[:], in0=rbk.t[:],
                                in1=sinf[:, g * 512:(g + 1) * 512]))
                            rbk.free = tt2
                            tev = p.op("dve", [tt1, tt2], lambda e, g=g, hc=hc: e.tensor_add(
                                out=dst[:, hc, g * 512:(g + 1) * 512],
                                in0=ft1[:], in1=ft2[:]))
                            rope_ev[0] = tev
                    w_free[hc % 2] = p.now("pe")

            qk_prod("k", kT)
            t_k = [p.now("dve"), p.now("act"), p.now("pe"), p.now("pool")]
            qk_prod("q", qq)
            t_qk = [p.now("dve"), p.now("act"), p.now("pe"), p.now("pool")]

            nc.leave_named_scope(f"at{L}{seq}_a3", sc_a3[0], False)
            sc_b0 = nc.enter_named_scope(f"at{L}{seq}_b0", False)
            tzb = p.op("dve", t_k, lambda e: e.memset(b_qk2[0:1, 0:80], 0.0))
            sq_par = [None, None]
            red_prev = [tzb, tzb]
            for which, src_t, base in (("k", kT, 8), ("q", qq, 0)):
                tdep = t_k if which == "k" else t_qk
                for c in range(NC):
                    for g in range(4):
                        par = g % 2
                        tsq = p.op("act", tdep + [sq_par[par]], lambda e, c=c, g=g, s=src_t, par=par: e.activation(
                            out=sqb[:, par * 512:(par + 1) * 512],
                            in_=s[:, c, g * 512:(g + 1) * 512], func=AF.Square))
                        bk = pb[4 + par]
                        tmm = p.op("pe", [tsq, bk.free], lambda e, par=par, bk=bk: e.matmul(
                            bk.t[0:1, :], ones_c_b[:], sqb[:, par * 512:(par + 1) * 512],
                            start=True, stop=True))
                        sq_par[par] = tmm
                        idx = (base + c) * 4 + g
                        trd = p.op("dve", [tmm, red_prev[par]], lambda e, idx=idx, bk=bk: e.tensor_reduce(
                            out=b_qk2[0:1, idx:idx + 1], in_=bk.t[0:1, :], axis=AX.X,
                            op=ALU.max))
                        bk.free = trd
                        red_prev[par] = trd
            tprev = p.op("dve", [red_prev[0], red_prev[1]], lambda e: e.tensor_reduce(
                out=b_qk2[0:1, 64:80], in_=b_qk2[0:1, 0:64].rearrange("a (c g) -> a c g", g=4),
                axis=AX.X, op=ALU.max))
            tb1 = p.op("dve", [tprev], lambda e: e.tensor_mul(
                out=srow[0:1, 8:16], in0=b_qk2[0:1, 64:72], in1=b_qk2[0:1, 72:80]))
            tb2 = p.op("act", [tb1], lambda e: e.activation(
                out=srow[0:1, 16:24], in_=srow[0:1, 8:16], func=AF.Sqrt))
            tb3 = p.op("dve", [tb2], lambda e: e.tensor_scalar_mul(
                out=srow[0:1, 24:32], in0=srow[0:1, 16:24], scalar1=-0.125))
            tb4 = p.op("pe", [tb3, pb[5].free], lambda e: e.matmul(
                pb[5].t[:, 0:8], ones_r_f[:], srow[0:1, 24:32], start=True, stop=True))
            tb5 = p.op("dve", [tb4], lambda e: e.tensor_copy(out=bbias[:], in_=pb[5].t[:, 0:8]))
            pb[5].free = tb5

            nc.leave_named_scope(f"at{L}{seq}_b0", sc_b0[0], False)
            sc_b2 = nc.enter_named_scope(f"at{L}{seq}_b2", False)
            t_wo = [p.dma("sp", [t_vdone], woB[:, c, :], wo[c]) for c in range(NC)]

            xo_prev = [None, None]
            rt_prev = None
            tst_last = None
            for qt in range(4):
                for hd in range(NH):
                    hc2 = hd // 2
                    rb = (hd % 2) * 64
                    kmax = 4 * (qt + 1)
                    abk = pb[2 + (hd % 2)]
                    t_pv = None
                    texs = [None] * (kmax + 2)
                    for kc in range(kmax + 2):
                      if kc < kmax:
                        o = max(0, kc * 128 - qt * 512)
                        n = 512 - o
                        sbk = pb[[0, 1, 4][kc % 3]]
                        tsc = p.op("pe", [sbk.free, tb5], lambda e, kc=kc, hc2=hc2, rb=rb, qt=qt, o=o, n=n, sbk=sbk: e.matmul(
                            sbk.t[:, o:o + n],
                            kT[rb:rb + 64, hc2, kc * 128:(kc + 1) * 128],
                            qq[rb:rb + 64, hc2, qt * 512 + o:qt * 512 + o + n],
                            start=True, stop=True))
                        slot = kc % 3
                        tex = p.op("act", [tsc, t_pv], lambda e, sbk=sbk, o=o, n=n, slot=slot, hc2=hc2: e.activation(
                            out=eT[:, slot, o:o + n], in_=sbk.t[:, o:o + n], func=AF.Exp,
                            bias=bbias[:, hc2:hc2 + 1], scale=0.125))
                        sbk.free = tex
                        if kc >= 4 * qt:
                            tex = p.op("pool", [tex], lambda e, slot=slot, o=o: e.tensor_mul(
                                out=eT[:, slot, o:o + 128], in0=eT[:, slot, o:o + 128],
                                in1=trib[:]))
                        texs[kc] = tex
                      if kc >= 2:
                        kp = kc - 2
                        o = max(0, kp * 128 - qt * 512)
                        n = 512 - o
                        slot = kp % 3
                        t_pv = p.op("pe", [texs[kp], t_vdone] + ([t_pv] if t_pv else [abk.free]),
                                    lambda e, kp=kp, hd=hd, o=o, n=n, slot=slot, kmax=kmax, abk=abk: e.matmul(
                                        abk.t[0:65, o:o + n], vT[:, kp, hd, 0:65],
                                        eT[:, slot, o:o + n],
                                        start=(kp == 0), stop=(kp == kmax - 1)))
                    trb = p.op("dve", [t_pv, norm_war[hd % 2]], lambda e, hd=hd, abk=abk: e.tensor_copy(
                        out=sqb[0:1, (hd % 2) * 512:(hd % 2) * 512 + 512],
                        in_=abk.t[64:65, 0:512]))
                    tbc = p.op("pe", [trb, pb[5].free], lambda e, hd=hd: e.matmul(
                        pb[5].t[0:64, :], ones_r_b[0:1, 0:64],
                        sqb[0:1, (hd % 2) * 512:(hd % 2) * 512 + 512],
                        start=True, stop=True))
                    trc = p.op("dve", [tbc, norm_war2[0]], lambda e: e.reciprocal(
                        out=ft1[0:64, :], in_=pb[5].t[0:64, :]))
                    pb[5].free = trc
                    tev = p.op("dve", [t_pv, trc], lambda e, abk=abk: e.tensor_copy(
                        out=sqb[0:64, 1024:1536], in_=abk.t[0:64, 0:512]))
                    abk.free = tev
                    tn = p.op("dve", [trc, tev], lambda e, hc2=hc2, rb=rb: e.tensor_tensor(
                        out=aT[rb:rb + 64, hc2, :], in0=sqb[0:64, 1024:1536],
                        in1=ft1[0:64, :], op=ALU.mult))
                    norm_war2[0] = tn
                    norm_war[hd % 2] = tn
                t_aT = p.now("dve")
                if DBG == ("aT", seq):
                    p.dma("sp", [t_aT], dbg16.ap().rearrange("p (c q s) -> p c q s", c=NC, q=4)[:, :, qt], aT[:])
                for tti in range(4):
                    tt = qt * 4 + tti
                    txl = p.dma("pool", [xin_free[tt % 2]], xin[:, tt % 2],
                                x_src[tt * 128:(tt + 1) * 128, :], stream=f"xp{tt % 2}")
                    tmm_h = []
                    for half in range(2):
                        bk = pb[0 + half]
                        tmm = None
                        for c in range(NC):
                            tmm = p.op("pe", [t_aT, t_wo[c], bk.free], lambda e, c=c, tti=tti, h=half, bk=bk: e.matmul(
                                bk.t[:], aT[:, c, tti * 128:(tti + 1) * 128],
                                woB[:, c, h * 512:(h + 1) * 512],
                                start=(c == 0), stop=(c == NC - 1 and L != 0)))
                        if L == 0:
                            tmm = p.op("pe", [tmm, t_obb], lambda e, h=half, bk=bk: e.matmul(
                                bk.t[:], ones_r_b[:], obrow[0:1, h * 512:(h + 1) * 512],
                                start=False, stop=True))
                        tmm_h.append(tmm)
                    tr1 = p.op("dve", [tmm_h[0], txl, rt_prev], lambda e, tt=tt: e.tensor_add(
                        out=rt[:, 0:512], in0=xin[:, tt % 2, 0:512], in1=pb[0].t[:]))
                    pb[0].free = tr1
                    tr2 = p.op("dve", [tmm_h[1], tr1], lambda e, tt=tt: e.tensor_add(
                        out=rt[:, 512:1024], in0=xin[:, tt % 2, 512:1024], in1=pb[1].t[:]))
                    pb[1].free = tr2
                    xin_free[tt % 2] = tr2
                    if DBG == ("rt", seq):
                        tr2 = p.dma("sp", [tr2], dbgf.ap()[:, tt], rt[:])
                    tst_last, tln = emit_ln_store(tt, tr2, L, x_dst,
                                                  xo_prev[tt % 2])
                    rt_prev = tln
                    xo_prev[tt % 2] = tst_last
            nc.leave_named_scope(f"at{L}{seq}_b2", sc_b2[0], False)
            return tst_last

        def emit_ffn0(seq, x_src, x_dst):
            l0g = sc["l0gT"].ap()
            l0u = sc["l0uT"].ap()
            l0d = sc["l0dT"].ap()
            rt_prev = [None]
            xo_prev2 = [None, None]

            sc_f0 = nc.enter_named_scope(f"f0{seq}_xq", False)
            xin_free = [None, None]
            t_cast_prev = [None]
            for tt in range(16):
                buf = xin[:, tt % 2]
                tl = p.dma("sp", [xin_free[tt % 2]], buf,
                           x_src[tt * 128:(tt + 1) * 128, :], stream=f"xin{tt % 2}")
                tam = p.op("dve", [tl], lambda e, b=buf: e.tensor_reduce(
                    out=stat[:, 26:27], in_=b, axis=AX.X, op=ALU.max,
                    apply_absolute_value=True))
                tcl = p.op("dve", [tam], lambda e: e.tensor_scalar_max(
                    out=stat[:, 27:28], in0=stat[:, 26:27], scalar1=1e-5))
                trc = p.op("dve", [tcl], lambda e: e.reciprocal(
                    out=stat[:, 28:29], in_=stat[:, 27:28]))
                tsc_ = p.op("dve", [trc], lambda e: e.tensor_scalar_mul(
                    out=stat[:, 29:30], in0=stat[:, 28:29], scalar1=127.0))
                tsx = p.op("dve", [tcl], lambda e, tt=tt: e.tensor_scalar_mul(
                    out=stat[:, 32 + tt:33 + tt], in0=stat[:, 27:28],
                    scalar1=1.0 / 127.0))
                tq = t_cast_prev
                for h in range(2):
                    ti = p.op("dve", [tsc_, tl] + (tq if isinstance(tq, list) else [tq]),
                              lambda e, b=buf, h=h: e.tensor_scalar(
                                  out=it1[:], in0=b[:, h * 512:(h + 1) * 512],
                                  scalar1=stat[:, 29:30], scalar2=None, op0=ALU.mult))
                    tq = p.op("dve", [ti], lambda e, h=h: e.tensor_copy(
                        out=sqb[:, h * 512:(h + 1) * 512], in_=it1[:]))
                xin_free[tt % 2] = ("dve", p.cnt["dve"])
                for c in range(NC):
                    b_ = pt[c % 2]
                    ttr = p.op("pe", [tq, b_.free], lambda e, c=c, b_=b_: e.transpose(
                        b_.t[:], sqb[:, c * 128:(c + 1) * 128], identb[:]))
                    tev = p.op("dve", [ttr], lambda e, c=c, tt=tt, b_=b_: e.tensor_copy(
                        out=xT[:, c, tt * 128:(tt + 1) * 128], in_=b_.t[:]))
                    b_.free = tev
                t_cast_prev = [p.now("pe")]
            t_xq = p.now("dve")

            nc.leave_named_scope(f"f0{seq}_xq", sc_f0[0], False)
            sc_f1 = nc.enter_named_scope(f"f0{seq}_main", False)
            t_wd = []
            for ic in range(32):
                t_wd.append(p.dma("sp", [t_xq], wdT0[:, ic, :], l0d[ic]))

            mean_g = stat[:, 2:3]
            mean_u = stat[:, 4:5]
            mean_d = stat[:, 6:7]
            for g in range(4):
                tsxr = []
                for j in range(4):
                    tt = g * 4 + j
                    b_ = pb[5]
                    ttr = p.op("pe", [t_xq, b_.free], lambda e, tt=tt, b_=b_: e.transpose(
                        b_.t[0:1, 0:128], stat[:, 32 + tt:33 + tt], identf[:]))
                    tcp = p.op("dve", [ttr], lambda e, j=j, b_=b_: e.tensor_copy(
                        out=sxr[0:1, j * 128:(j + 1) * 128], in_=b_.t[0:1, 0:128]))
                    b_.free = tcp
                    tsxr.append(tcp)
                tbc = p.op("pe", [pb[4].free] + tsxr, lambda e: e.matmul(
                    pb[4].t[:], ones_r_f[:], sxr[0:1, :], start=True, stop=True))
                tsxb = p.op("dve", [tbc], lambda e: e.tensor_copy(
                    out=sxb[:], in_=pb[4].t[:]))
                pb[4].free = tsxb
                tmz = p.op("dve", [pb[0].free, pb[1].free], lambda e: e.memset(mtile[:], 0.0))
                w_free2 = [None, None]
                tgu_prev = None
                for ic in range(32):
                    twu = p.dma("sp", [w_free2[0]], wstr[:, 0],
                                l0u[ic].rearrange("c p e -> p c e"), stream="fu")
                    twg = p.dma("pool", [w_free2[1]], wstr[:, 1],
                                l0g[ic].rearrange("c p e -> p c e"), stream="fgp")
                    for c in range(NC):
                        p.op("pe", [twu, t_xq, pb[0].free if c == 0 else None],
                             lambda e, c=c, g=g: e.matmul(
                                 pb[0].t[:], wstr[:, 0, c, :], xT[:, c, g * 512:(g + 1) * 512],
                                 start=(c == 0), stop=(c == NC - 1)))
                    tmmu = p.now("pe")
                    w_free2[0] = tmmu
                    for c in range(NC):
                        p.op("pe", [twg, t_xq, pb[1].free if c == 0 else None],
                             lambda e, c=c, g=g: e.matmul(
                                 pb[1].t[:], wstr[:, 1, c, :], xT[:, c, g * 512:(g + 1) * 512],
                                 start=(c == 0), stop=(c == NC - 1)))
                    tmmg = p.now("pe")
                    w_free2[1] = tmmg
                    tup = p.op("dve", [tmmu, tsxb], lambda e: e.scalar_tensor_tensor(
                        out=ft1[:], in0=pb[0].t[:], scalar=mean_u, in1=sxb[:],
                        op0=ALU.mult, op1=ALU.mult))
                    pb[0].free = tup
                    tga = p.op("dve", [tmmg, tsxb], lambda e: e.scalar_tensor_tensor(
                        out=ft2[:], in0=pb[1].t[:], scalar=mean_g, in1=sxb[:],
                        op0=ALU.mult, op1=ALU.mult))
                    pb[1].free = tga
                    tsi = p.op("act", [tga], lambda e: e.activation(
                        out=ft3[:], in_=ft2[:], func=AF.Silu))
                    tgu = p.op("dve", [tsi, tup, tgu_prev], lambda e, ic=ic: e.tensor_mul(
                        out=guh[:, ic, :], in0=ft1[:], in1=ft3[:]))
                    tm1 = p.op("dve", [tgu, tmz], lambda e, ic=ic: e.scalar_tensor_tensor(
                        out=mtile[:], in0=guh[:, ic, :], scalar=1.0, in1=mtile[:],
                        op0=ALU.mult, op1=ALU.max))
                    tgu_prev = p.op("dve", [tm1], lambda e, ic=ic: e.scalar_tensor_tensor(
                        out=mtile[:], in0=guh[:, ic, :], scalar=-1.0, in1=mtile[:],
                        op0=ALU.mult, op1=ALU.max))
                tscq = []
                for j in range(4):
                    tt = g * 4 + j
                    b_ = pb[5]
                    ttr = p.op("pe", [tgu_prev, b_.free], lambda e, j=j, b_=b_: e.transpose(
                        b_.t[0:128, 0:128], mtile[:, j * 128:(j + 1) * 128], identf[:]))
                    trd = p.op("dve", [ttr], lambda e, b_=b_: e.tensor_reduce(
                        out=stat[:, 26:27], in_=b_.t[0:128, 0:128], axis=AX.X, op=ALU.max))
                    b_.free = trd
                    tcl = p.op("dve", [trd], lambda e: e.tensor_scalar_max(
                        out=stat[:, 27:28], in0=stat[:, 26:27], scalar1=1e-5))
                    tds = p.op("dve", [tcl], lambda e, j=j: e.tensor_scalar_mul(
                        out=stat[:, 48 + j:49 + j], in0=stat[:, 27:28],
                        scalar1=1.0 / 127.0))
                    tds2 = p.op("dve", [tds], lambda e, j=j: e.tensor_mul(
                        out=stat[:, 48 + j:49 + j], in0=stat[:, 48 + j:49 + j],
                        in1=mean_d))
                    trc = p.op("dve", [tcl], lambda e: e.reciprocal(
                        out=stat[:, 28:29], in_=stat[:, 27:28]))
                    tsc_ = p.op("dve", [trc], lambda e: e.tensor_scalar_mul(
                        out=stat[:, 29:30], in0=stat[:, 28:29], scalar1=127.0))
                    b2 = pb[5]
                    ttr2 = p.op("pe", [tsc_, b2.free], lambda e, b2=b2: e.transpose(
                        b2.t[0:1, 0:128], stat[:, 29:30], identf[:]))
                    tcp = p.op("dve", [ttr2, tds2], lambda e, j=j, b2=b2: e.tensor_copy(
                        out=sxr[0:1, j * 128:(j + 1) * 128], in_=b2.t[0:1, 0:128]))
                    b2.free = tcp
                    tscq.append(tcp)
                tbc2 = p.op("pe", [pb[4].free] + tscq, lambda e: e.matmul(
                    pb[4].t[:], ones_r_f[:], sxr[0:1, :], start=True, stop=True))
                tsqb2 = p.op("dve", [tbc2], lambda e: e.tensor_copy(
                    out=sxb[:], in_=pb[4].t[:]))
                pb[4].free = tsqb2
                tqz = None
                for ic in range(32):
                    ti = p.op("dve", [tsqb2, tqz], lambda e, ic=ic: e.tensor_tensor(
                        out=it1[:], in0=guh[:, ic, :], in1=sxb[:], op=ALU.mult))
                    tqz = p.op("dve", [ti], lambda e, ic=ic: e.tensor_copy(
                        out=guh[:, ic, :], in_=it1[:]))
                for half_pass in range(2):
                    tmm = None
                    for ic in range(32):
                        for j2 in range(2):
                            tti = half_pass * 2 + j2
                            tt = g * 4 + tti
                            for h in range(2):
                                bk = pb[j2 * 2 + h]
                                tmm = p.op("pe", [tqz, t_wd[ic], bk.free],
                                           lambda e, ic=ic, tt=tt, h=h, bk=bk, g=g, tti=tti: e.matmul(
                                               bk.t[:], guh[:, ic, tti * 128:(tti + 1) * 128],
                                               wdT0[:, ic, h * 512:(h + 1) * 512],
                                               start=(ic == 0), stop=(ic == 31)))
                    for j2 in range(2):
                        tti = half_pass * 2 + j2
                        tt = g * 4 + tti
                        txl = p.dma("pool", [xin_free[tt % 2]], xin[:, tt % 2],
                                    x_src[tt * 128:(tt + 1) * 128, :], stream=f"xp{tt % 2}")
                        tr1 = p.op("dve", [tmm, txl, rt_prev[0]], lambda e, tt=tt, j2=j2, tti=tti: e.scalar_tensor_tensor(
                            out=rt[:, 0:512], in0=pb[j2 * 2].t[:],
                            scalar=stat[:, 48 + tti:49 + tti],
                            in1=xin[:, tt % 2, 0:512], op0=ALU.mult, op1=ALU.add))
                        pb[j2 * 2].free = tr1
                        tr2 = p.op("dve", [tr1], lambda e, tt=tt, j2=j2, tti=tti: e.scalar_tensor_tensor(
                            out=rt[:, 512:1024], in0=pb[j2 * 2 + 1].t[:],
                            scalar=stat[:, 48 + tti:49 + tti],
                            in1=xin[:, tt % 2, 512:1024], op0=ALU.mult, op1=ALU.add))
                        pb[j2 * 2 + 1].free = tr2
                        xin_free[tt % 2] = tr2
                        tst, tln = emit_ln_store(tt, tr2, 0, x_dst, xo_prev2[tt % 2])
                        rt_prev[0] = tln
                        xo_prev2[tt % 2] = tst
            nc.leave_named_scope(f"f0{seq}_main", sc_f1[0], False)

        def finish_gmax(col, slot, deps):
            b_ = pb[5]
            ttr = p.op("pe", deps + [b_.free], lambda e, col=col, b_=b_: e.transpose(
                b_.t[0:1, 0:128], stat[:, col:col + 1], identf[:]))
            trd = p.op("dve", [ttr], lambda e, slot=slot, b_=b_: e.tensor_reduce(
                out=mxrow[0:1, slot:slot + 1], in_=b_.t[0:1, 0:128], axis=AX.X,
                op=ALU.max))
            b_.free = trd
            return trd

        def do_cc(deps):
            tin = p.dma("pool", deps, sc["cc_in"].ap(), mxrow[0:1, 0:8], stream="ccin")
            tcc = p.cc_max([tin], sc["cc_in"].ap(), sc["cc_out"].ap())
            return p.dma("pool", [tcc], mxrow[0:1, 8:16], sc["cc_out"].ap(), stream="ccout")

        def derive_scale(tdep, gslot, gbcol):
            if not isinstance(tdep, list):
                tdep = [tdep]
            t1 = p.op("dve", tdep, lambda e: e.tensor_scalar(
                out=srow[0:1, 40:41], in0=mxrow[0:1, 8 + gslot:9 + gslot],
                scalar1=1.0 / 32767.0, scalar2=1e-12, op0=ALU.mult, op1=ALU.max))
            t2 = p.op("dve", [t1], lambda e: e.reciprocal(
                out=srow[0:1, 41:42], in_=srow[0:1, 40:41]))
            t3 = p.op("pe", [t2, pb[4].free], lambda e: e.matmul(
                pb[4].t[:, 0:2], ones_r_f[:], srow[0:1, 40:42], start=True, stop=True))
            t4 = p.op("dve", [t3], lambda e, gbcol=gbcol: e.tensor_copy(
                out=gb[:, gbcol:gbcol + 2], in_=pb[4].t[:, 0:2]))
            pb[4].free = t4
            return t4

        def emit_ffn1_AB(seq, x_src):
            l1g = sc["l1gT"].ap()
            l1u = sc["l1uT"].ap()
            spu = sc["sp_up"].ap()
            spg = sc["sp_gate"].ap()
            xin_free = [None, None]
            t_cast_prev = [None]
            for tt in range(16):
                buf = xin[:, tt % 2]
                tl = p.dma("sp", [xin_free[tt % 2]], buf,
                           x_src[tt * 128:(tt + 1) * 128, :], stream=f"xin{tt % 2}")
                tq = t_cast_prev
                for h in range(2):
                    ti = p.op("dve", [tl] + (tq if isinstance(tq, list) else [tq]),
                              lambda e, b=buf, h=h: e.tensor_scalar(
                                  out=it1[:], in0=b[:, h * 512:(h + 1) * 512],
                                  scalar1=gb[:, 1:2], scalar2=None, op0=ALU.mult))
                    tq = p.op("dve", [ti], lambda e, h=h: e.tensor_scalar(
                        out=sqb[:, h * 512:(h + 1) * 512], in0=it1[:],
                        scalar1=gb[:, 0:1], scalar2=None, op0=ALU.mult))
                xin_free[tt % 2] = ("dve", p.cnt["dve"])
                for c in range(NC):
                    b_ = pt[c % 2]
                    ttr = p.op("pe", [tq, b_.free], lambda e, c=c, b_=b_: e.transpose(
                        b_.t[:], sqb[:, c * 128:(c + 1) * 128], identb[:]))
                    tev = p.op("dve", [ttr], lambda e, c=c, tt=tt, b_=b_: e.tensor_copy(
                        out=xT[:, c, tt * 128:(tt + 1) * 128], in_=b_.t[:]))
                    b_.free = tev
                t_cast_prev = [p.now("pe")]
            t_xq = p.now("dve")
            tmz = p.op("dve", [t_xq], lambda e: e.memset(stat[:, 52:54], 0.0))
            w_free2 = [None, None]
            sp_free = [None, None]
            tmax_prev = tmz
            for mat, (wsrc, spdst, mcol, wslot) in enumerate(
                    [(l1u, spu, 52, 0), (l1g, spg, 53, 1)]):
                for ic in range(24):
                    tw = p.dma("sp", [w_free2[wslot]], wstr[:, wslot],
                               wsrc[ic].rearrange("c p e -> p c e"), stream=f"f{wslot}")
                    for g in range(4):
                        bk = pb[g]
                        tmm = None
                        for c in range(NC):
                            tmm = p.op("pe", [tw, t_xq, bk.free], lambda e, c=c, g=g, wslot=wslot: e.matmul(
                                bk.t[:], wstr[:, wslot, c, :], xT[:, c, g * 512:(g + 1) * 512],
                                start=(c == 0), stop=(c == NC - 1)))
                        sl = g % 2
                        tcp = p.op("act", [tmm, sp_free[sl]], lambda e, sl=sl, bk=bk: e.activation(
                            out=sp16[:, sl], in_=bk.t[:], func=AF.Identity))
                        bk.free = tcp
                        trd = p.op("dve", [tcp, tmax_prev], lambda e, sl=sl: e.tensor_reduce(
                            out=stat[:, 56:57], in_=sp16[:, sl], axis=AX.X, op=ALU.max,
                            apply_absolute_value=True))
                        tmax_prev = p.op("dve", [trd, tmax_prev], lambda e, mcol=mcol: e.tensor_tensor(
                            out=stat[:, mcol:mcol + 1], in0=stat[:, mcol:mcol + 1],
                            in1=stat[:, 56:57], op=ALU.max))
                        sp_free[sl] = p.dma("pool", [trd], spdst[seq, ic][:, g * 512:(g + 1) * 512],
                                            sp16[:, sl], stream=f"sp{sl}")
                    w_free2[wslot] = p.now("pe")
            return tmax_prev

        def emit_ffn1_C1(seq):
            spg = sc["sp_gate"].ap()
            sp_free = [None, None]
            tmax_prev = p.op("dve", [], lambda e: e.memset(stat[:, 54:55], 0.0))                 if seq == 0 else None
            for ic in range(24):
                for g in range(4):
                    sl = g % 2
                    SP = sp16 if sl == 0 else sp16b
                    IT = it1 if sl == 0 else it2
                    FA = ft1 if sl == 0 else ftA
                    FB = ft2 if sl == 0 else ftB
                    tl = p.dma("sp" if sl == 0 else "pool", [sp_free[sl]], SP[:, 0],
                               spg[seq, ic][:, g * 512:(g + 1) * 512],
                               stream="cg0" if sl == 0 else "cgp")
                    t1 = p.op("dve", [tl], lambda e, SP=SP, IT=IT: e.tensor_scalar(
                        out=IT[:], in0=SP[:, 0], scalar1=gb[:, 5:6], scalar2=None,
                        op0=ALU.mult))
                    t2 = p.op("dve", [t1], lambda e, IT=IT, FA=FA: e.tensor_scalar(
                        out=FA[:], in0=IT[:], scalar1=gb[:, 4:5], scalar2=None,
                        op0=ALU.mult))
                    pass
                    t3 = p.op("act", [t2], lambda e, FA=FA, FB=FB: e.activation(
                        out=FB[:], in_=FA[:], func=AF.Sigmoid))
                    t4 = p.op("dve", [t3, t2], lambda e, FB=FB, IT=IT: e.tensor_scalar(
                        out=IT[:], in0=FB[:], scalar1=65536.0, scalar2=None,
                        op0=ALU.mult))
                    t5 = p.op("dve", [t4], lambda e, IT=IT, FB=FB: e.tensor_scalar(
                        out=FB[:], in0=IT[:], scalar1=65535.0, scalar2=1.0 / 65536.0,
                        op0=ALU.min, op1=ALU.mult))
                    t6 = p.op("dve", [t5], lambda e, SP=SP, FA=FA, FB=FB: e.tensor_mul(
                        out=SP[:, 1], in0=FA[:], in1=FB[:]))
                    trd = p.op("dve", [t6, tmax_prev], lambda e, SP=SP: e.tensor_reduce(
                        out=stat[:, 56:57], in_=SP[:, 1], axis=AX.X, op=ALU.max,
                        apply_absolute_value=True))
                    tmax_prev = p.op("dve", [trd], lambda e: e.tensor_tensor(
                        out=stat[:, 54:55], in0=stat[:, 54:55], in1=stat[:, 56:57],
                        op=ALU.max))
                    sp_free[sl] = p.dma("pool", [trd], spg[seq, ic][:, g * 512:(g + 1) * 512],
                                        SP[:, 1], stream=f"sp{sl}")
            return tmax_prev

        def emit_ffn1_C2(seq):
            spg = sc["sp_gate"].ap()
            spu = sc["sp_up"].ap()
            sp_free = [None, None]
            spu_free = [None, None]
            tmax_prev = None
            if seq == 0:
                tmax_prev = p.op("dve", [], lambda e: e.memset(stat[:, 55:56], 0.0))
            for ic in range(24):
                for g in range(4):
                    sl = g % 2
                    SP = sp16 if sl == 0 else sp16b
                    IT = it1 if sl == 0 else it2
                    FA = ft1 if sl == 0 else ftA
                    FB = ft2 if sl == 0 else ftB
                    US = spu16[:, sl]
                    tlg = p.dma("sp" if sl == 0 else "pool", [sp_free[sl]], SP[:, 0],
                                spg[seq, ic][:, g * 512:(g + 1) * 512],
                                stream="cg0" if sl == 0 else "cgp")
                    tlu = p.dma("pool" if sl == 0 else "sp", [spu_free[sl]], US,
                                spu[seq, ic][:, g * 512:(g + 1) * 512],
                                stream="sup" if sl == 0 else "su1")
                    t1 = p.op("dve", [tlg], lambda e, SP=SP, IT=IT: e.tensor_scalar(
                        out=IT[:], in0=SP[:, 0], scalar1=gb[:, 7:8], scalar2=None,
                        op0=ALU.mult))
                    t2 = p.op("dve", [t1], lambda e, IT=IT, FA=FA: e.tensor_scalar(
                        out=FA[:], in0=IT[:], scalar1=gb[:, 6:7], scalar2=None,
                        op0=ALU.mult))
                    pass
                    t3 = p.op("dve", [tlu, t2], lambda e, US=US, IT=IT: e.tensor_scalar(
                        out=IT[:], in0=US, scalar1=gb[:, 3:4], scalar2=None,
                        op0=ALU.mult))
                    t4 = p.op("dve", [t3], lambda e, IT=IT, FB=FB: e.tensor_scalar(
                        out=FB[:], in0=IT[:], scalar1=gb[:, 2:3], scalar2=None,
                        op0=ALU.mult))
                    t5 = p.op("dve", [t4], lambda e, SP=SP, FA=FA, FB=FB: e.tensor_mul(
                        out=SP[:, 1], in0=FA[:], in1=FB[:]))
                    trd = p.op("dve", [t5, tmax_prev], lambda e, SP=SP: e.tensor_reduce(
                        out=stat[:, 56:57], in_=SP[:, 1], axis=AX.X, op=ALU.max,
                        apply_absolute_value=True))
                    tmax_prev = p.op("dve", [trd], lambda e: e.tensor_tensor(
                        out=stat[:, 55:56], in0=stat[:, 55:56], in1=stat[:, 56:57],
                        op=ALU.max))
                    spu_free[sl] = t3
                    sp_free[sl] = p.dma("pool", [trd],
                                        spu[seq, ic][:, g * 512:(g + 1) * 512],
                                        SP[:, 1], stream=f"sp{sl}")
            return tmax_prev

        def emit_ffn1_C3(seq, x_src, y_dst, t_wd1):
            spu = sc["sp_up"].ap()
            sp_free = [None, None]
            rt_prev = [None]
            xo_prev2 = [None, None]
            xin_free = [None, None]
            for g in range(4):
                toq_all = []
                toq = None
                for ic in range(24):
                    sl = ic % 2
                    SP = sp16 if sl == 0 else sp16b
                    IT = it1 if sl == 0 else it2
                    tl = p.dma("sp" if sl == 0 else "pool", [sp_free[sl]], SP[:, 0],
                               spu[seq, ic][:, g * 512:(g + 1) * 512],
                               stream="cg0" if sl == 0 else "cgp")
                    t1 = p.op("dve", [tl, toq], lambda e, SP=SP, IT=IT: e.tensor_scalar(
                        out=IT[:], in0=SP[:, 0], scalar1=gb[:, 9:10], scalar2=None,
                        op0=ALU.mult))
                    sp_free[sl] = t1
                    toq = p.op("dve", [t1], lambda e, ic=ic, IT=IT: e.tensor_scalar(
                        out=oq1[:, ic, :], in0=IT[:], scalar1=gb[:, 8:9], scalar2=None,
                        op0=ALU.mult))
                    toq_all.append(toq)
                for half_pass in range(2):
                    tmm = None
                    for ic in range(24):
                        for j2 in range(2):
                            tti = half_pass * 2 + j2
                            for h in range(2):
                                bk = pb[j2 * 2 + h]
                                tmm = p.op("pe", [toq_all[ic], t_wd1[ic], bk.free],
                                           lambda e, ic=ic, tti=tti, h=h, bk=bk: e.matmul(
                                               bk.t[:], oq1[:, ic, tti * 128:(tti + 1) * 128],
                                               wdT1[:, ic, h * 512:(h + 1) * 512],
                                               start=(ic == 0), stop=(ic == 23)))
                    for j2 in range(2):
                        tti = half_pass * 2 + j2
                        tt = g * 4 + tti
                        txl = p.dma("sp", [xin_free[tt % 2]], xin[:, tt % 2],
                                    x_src[tt * 128:(tt + 1) * 128, :], stream=f"xin{tt % 2}")
                        tr1 = p.op("dve", [tmm, txl, rt_prev[0]], lambda e, tt=tt, j2=j2: e.tensor_add(
                            out=rt[:, 0:512], in0=pb[j2 * 2].t[:], in1=xin[:, tt % 2, 0:512]))
                        pb[j2 * 2].free = tr1
                        tr2 = p.op("dve", [tr1], lambda e, tt=tt, j2=j2: e.tensor_add(
                            out=rt[:, 512:1024], in0=pb[j2 * 2 + 1].t[:],
                            in1=xin[:, tt % 2, 512:1024]))
                        pb[j2 * 2 + 1].free = tr2
                        xin_free[tt % 2] = tr2
                        tst, tln = emit_ln_store(tt, tr2, 0, y_dst, xo_prev2[tt % 2])
                        rt_prev[0] = tln
                        xo_prev2[tt % 2] = tst

        x0 = x_in.ap()
        xc = sc["xcur"].ap()
        t_mx_init = p.op("dve", [], lambda e: e.memset(stat[:, 25:26], 0.0))

        if UPTO >= 1:
            for seq in range(NSEQ):
                with nc.named_scope(f"l0a_s{seq}"):
                    emit_attn(0, seq, x0[seq], xc[seq])
                    p.fence()
                if UPTO >= 2:
                    with nc.named_scope(f"l0f_s{seq}"):
                        emit_ffn0(seq, xc[seq], xc[seq])
                        p.fence()
                if UPTO >= 3 and not SKIP_L1A:
                    with nc.named_scope(f"l1a_s{seq}"):
                        emit_attn(1, seq, xc[seq], xc[seq])
                        p.fence()
        if UPTO >= 4:
            with nc.named_scope("cc1"):
                tg = finish_gmax(25, 0, [p.now("dve")])
                tcc = do_cc([tg])
                derive_scale(tcc, 0, 0)
                p.fence()
            for seq in range(NSEQ):
                with nc.named_scope(f"l1fAB_s{seq}"):
                    emit_ffn1_AB(seq, xc[seq])
                    p.fence()
            with nc.named_scope("cc2"):
                tg1 = finish_gmax(52, 1, [])
                tg2 = finish_gmax(53, 2, [tg1])
                tcc = do_cc([tg1, tg2])
                td1 = derive_scale(tcc, 1, 2)
                derive_scale([tcc, td1], 2, 4)
                p.fence()
            for seq in range(NSEQ):
                with nc.named_scope(f"l1fC1_s{seq}"):
                    emit_ffn1_C1(seq)
                    p.fence()
            with nc.named_scope("cc3"):
                tg3 = finish_gmax(54, 3, [])
                tcc = do_cc([tg3])
                derive_scale(tcc, 3, 6)
                p.fence()
            for seq in range(NSEQ):
                with nc.named_scope(f"l1fC2_s{seq}"):
                    emit_ffn1_C2(seq)
                    p.fence()
            with nc.named_scope("cc4"):
                tg4 = finish_gmax(55, 4, [])
                tcc = do_cc([tg4])
                derive_scale(tcc, 4, 8)
                p.fence()
            t_wd1 = [p.dma("sp", [], wdT1[:, ic, :], sc["l1dT"].ap()[ic])
                     for ic in range(24)]
            for seq in range(NSEQ):
                with nc.named_scope(f"l1fC3_s{seq}"):
                    emit_ffn1_C3(seq, xc[seq], y_out.ap()[seq], t_wd1)
                    p.fence()

        p.fence()
        if UPTO < 4:
            tlast = None
            for seq in range(NSEQ):
                for tt in range(16):
                    tlast = p.dma("sp", [tlast], y_out.ap()[seq][tt * 128:(tt + 1) * 128, :],
                                  xc[seq][tt * 128:(tt + 1) * 128, :])
            p._wait("sp", [tlast])

    return nc


def _host_arrange(inputs):
    f = np.float32
    A = np.ascontiguousarray
    b = np.asarray(inputs["L0_attn_attn_b"], f)
    bqk = np.zeros((128, 16), f)
    for hc in range(8):
        qcols = np.concatenate([np.arange(384 * hc, 384 * hc + 64),
                                np.arange(384 * hc + 192, 384 * hc + 256)])
        bqk[:, hc] = b[qcols]
        bqk[:, 8 + hc] = b[qcols + 64]
    vcols = np.concatenate([np.arange(192 * h + 128, 192 * h + 192) for h in range(16)])
    return {
        "l0_aw": A(np.asarray(inputs["L0_attn_attn_W"], f)),
        "l0_bqk": bqk,
        "l0_bv": A(b[vcols][None, :]),
        "l0_ow": A(np.asarray(inputs["L0_attn_out_W"], f)),
        "l0_ob": A(np.asarray(inputs["L0_attn_out_b"], f)[None, :]),
        "l0_wgT": A(np.asarray(inputs["L0_ffn_w_gate"], f).T),
        "l0_wuT": A(np.asarray(inputs["L0_ffn_w_up"], f).T),
        "l0_wdT": A(np.asarray(inputs["L0_ffn_w_down"], f).T),
        "l1_aw": A(np.asarray(inputs["L1_attn_w_qkv"], f)),
        "l1_ow": A(np.asarray(inputs["L1_attn_w_out"], f)),
        "l1_wg": A(np.asarray(inputs["L1_ffn_w_gate"], f)),
        "l1_wu": A(np.asarray(inputs["L1_ffn_w_up"], f)),
        "l1_wd": A(np.asarray(inputs["L1_ffn_w_down"], f)),
    }


def kernel(**inputs):
    x = np.asarray(inputs["L0_attn_encodings"], dtype=np.float32)
    w = _host_arrange(inputs)
    nc = build_program()
    in_maps = []
    for c in range(NCORES):
        m = dict(w)
        m["x_in"] = np.ascontiguousarray(x[c * NSEQ:(c + 1) * NSEQ])
        in_maps.append(m)
    res = run_bass_kernel_spmd(nc, in_maps, core_ids=list(range(NCORES)))
    return np.concatenate([r["y_out"] for r in res.results], axis=0)


# version 14
# speedup vs baseline: 1.0861x; 1.0861x over previous
import numpy as np
import ml_dtypes
import concourse.bass as bass
import concourse.mybir as mybir
from concourse.bass_utils import run_bass_kernel_spmd
from contextlib import ExitStack

F32 = mybir.dt.float32
BF16 = mybir.dt.bfloat16
F16 = mybir.dt.float16
I32 = mybir.dt.int32
AF = mybir.ActivationFunctionType
ALU = mybir.AluOpType
AX = mybir.AxisListType
BF = ml_dtypes.bfloat16

D = 1024
S = 2048
NSEQ = 2
NC = 8
NH = 16
HS = 64
NCORES = 8
L0I = 4096
L1I = 3072

UPTO = 4
SKIP_L1A = False
NO_ROPE = False
NO_TABLE = False
ROPE_VAR = 0
DBG = None


class Prog:
    ENGS = ("pe", "act", "dve", "pool", "sp")

    def __init__(self, nc, ctx):
        self.nc = nc
        self._ctx = ctx
        self.eng = dict(pe=nc.tensor, act=nc.scalar, dve=nc.vector,
                        pool=nc.gpsimd, sp=nc.sync)
        self.sems = {}
        self.cnt = {}
        for s in ("pe", "act", "dve", "pool", "dma", "dmag", "cc"):
            self.sems[s] = ctx.enter_context(nc.semaphore("S_" + s))
            self.cnt[s] = 0
        self.know = {e: dict.fromkeys(self.sems, 0) for e in self.ENGS}
        self._warm_tok = None

    def stream_sem(self, name):
        if name not in self.sems:
            self.sems[name] = self._ctx.enter_context(self.nc.semaphore("S_" + name))
            self.cnt[name] = 0
            for e in self.know:
                self.know[e][name] = 0
        return name

    def _wait(self, eng, toks):
        best = {}
        for t in toks:
            if t is None:
                continue
            if len(t) == 3:
                s, v, _ = t
                v = self.cnt[s]
            else:
                s, v = t
            if v > best.get(s, 0):
                best[s] = v
        for s, v in best.items():
            if self.know[eng][s] < v:
                self.eng[eng].wait_ge(self.sems[s], v)
                self.know[eng][s] = v

    def op(self, eng, deps, emit):
        self._wait(eng, deps)
        inst = emit(self.eng[eng])
        self.cnt[eng] += 1
        inst.then_inc(self.sems[eng], 1)
        return (eng, self.cnt[eng])

    def dma(self, queue, deps, out, in_, stream=None):
        self._wait(queue, deps)
        if stream is None:
            sem = "dma" if queue == "sp" else "dmag"
        else:
            sem = self.stream_sem(stream)
        inst = self.eng[queue].dma_start(out=out, in_=in_)
        self.cnt[sem] += 16
        inst.then_inc(self.sems[sem], 16)
        if stream is None:
            return (sem, self.cnt[sem], "B")
        return (sem, self.cnt[sem])

    def cc_max(self, deps, in_ap, out_ap):
        self._wait("pool", deps)
        inst = self.eng["pool"].collective_compute(
            "AllReduce", ALU.max, replica_groups=[list(range(NCORES))],
            ins=[in_ap.opt()], outs=[out_ap.opt()])
        self.cnt["cc"] += 1
        inst.then_inc(self.sems["cc"], 1)
        return ("cc", self.cnt["cc"])

    def now(self, eng):
        return (eng, self.cnt[eng])

    def warm(self, lhs, rhs, tgt):
        self._wait("pe", [self._warm_tok])
        inst = self.eng["pe"].matmul(tgt, lhs, rhs, start=True, stop=True)
        self.cnt["pe"] += 1
        inst.then_inc(self.sems["pe"], 1)
        self._warm_tok = ("pe", self.cnt["pe"])

    def fence(self):
        toks = [(s, self.cnt[s]) for s in self.cnt]
        for e in self.ENGS:
            self._wait(e, toks)


class Bank:
    def __init__(self, t):
        self.t = t
        self.free = None


def make_consts(nc):
    c = {}
    c["ident_b"] = nc.inline_tensor(np.eye(128).astype(BF), name="c_ident_b")
    c["ident_f"] = nc.inline_tensor(np.eye(128, dtype=np.float32), name="c_ident_f")
    tri = (np.arange(128)[:, None] <= np.arange(128)[None, :])
    c["tri_b"] = nc.inline_tensor(tri.astype(BF), name="c_tri_b")
    c["ones_row_b"] = nc.inline_tensor(np.ones((1, 128), BF), name="c_ones_row_b")
    c["ones_row_f"] = nc.inline_tensor(np.ones((1, 128), np.float32), name="c_ones_row_f")
    c["ones_col_b"] = nc.inline_tensor(np.ones((128, 1), BF), name="c_ones_col_b")
    c["ones_col_f"] = nc.inline_tensor(np.ones((128, 1), np.float32), name="c_ones_col_f")
    inv_freq = 1.0 / 10000.0 ** (np.arange(0, 32, 2, dtype=np.float64) / 32.0)
    ang = np.arange(S, dtype=np.float64)[None, :] * np.repeat(inv_freq, 2)[:, None]
    cosf = np.ones((128, S), np.float64)
    sinf = np.zeros((128, S), np.float64)
    for base in (0, 64):
        cosf[base:base + 32] = np.cos(ang)
        sinf[base:base + 32] = np.sin(ang)
    c["cosf"] = nc.inline_tensor(cosf.astype(BF), name="c_cosf")
    c["sinf"] = nc.inline_tensor(sinf.astype(BF), name="c_sinf")
    R = np.zeros((128, 128), np.float64)
    for base in (0, 64):
        for i in range(16):
            R[base + 2 * i, base + 2 * i + 1] = -1.0
            R[base + 2 * i + 1, base + 2 * i] = 1.0
    c["rotT_b"] = nc.inline_tensor(np.ascontiguousarray(R.T).astype(BF), name="c_rotT_b")
    return c


def build_program():
    nc = bass.Bass("TRN2", target_bir_lowering=False, debug=False,
                   dynamic_dma_scratch_size=16384)

    x_in = nc.dram_tensor("x_in", [NSEQ, S, D], F32, kind="ExternalInput")
    y_out = nc.dram_tensor("y_out", [NSEQ, S, D], F32, kind="ExternalOutput")
    dbg_kind = "ExternalOutput" if DBG else "Internal"
    dbg16 = nc.dram_tensor("dbg16", [128, 16384], BF16, kind=dbg_kind)
    dbgf = nc.dram_tensor("dbgf", [128, 16, D], F32, kind=dbg_kind)
    ew = {}
    for nm, shape in [
        ("l0_aw", [D, 3 * D]), ("l0_bqk", [128, 16]), ("l0_bv", [1, D]),
        ("l0_ow", [D, D]), ("l0_ob", [1, D]),
        ("l0_wgT", [D, L0I]), ("l0_wuT", [D, L0I]), ("l0_wdT", [L0I, D]),
        ("l1_aw", [D, 3 * D]), ("l1_ow", [D, D]),
        ("l1_wg", [L1I, D]), ("l1_wu", [L1I, D]), ("l1_wd", [D, L1I]),
    ]:
        ew[nm] = nc.dram_tensor(nm, shape, F32, kind="ExternalInput")

    sc = {}
    sc["xcur"] = nc.dram_tensor("xcur", [NSEQ, S, D], F32, kind="Internal")
    for L in (0, 1):
        sc[f"wqk{L}"] = nc.dram_tensor(f"wqk{L}", [16, 8, 128, 128], BF16, kind="Internal")
        sc[f"wv{L}"] = nc.dram_tensor(f"wv{L}", [8, 128, D], BF16, kind="Internal")
        sc[f"wo{L}"] = nc.dram_tensor(f"wo{L}", [8, 128, D], BF16, kind="Internal")
    sc["l0gT"] = nc.dram_tensor("l0gT", [32, 8, 128, 128], BF16, kind="Internal")
    sc["l0uT"] = nc.dram_tensor("l0uT", [32, 8, 128, 128], BF16, kind="Internal")
    sc["l0dT"] = nc.dram_tensor("l0dT", [32, 128, D], BF16, kind="Internal")
    sc["l1gT"] = nc.dram_tensor("l1gT", [24, 8, 128, 128], BF16, kind="Internal")
    sc["l1uT"] = nc.dram_tensor("l1uT", [24, 8, 128, 128], BF16, kind="Internal")
    sc["l1dT"] = nc.dram_tensor("l1dT", [24, 128, D], BF16, kind="Internal")
    sc["sp_up"] = nc.dram_tensor("sp_up", [NSEQ, 24, 128, S], F16, kind="Internal")
    sc["sp_gate"] = nc.dram_tensor("sp_gate", [NSEQ, 24, 128, S], F16, kind="Internal")
    sc["cc_in"] = nc.dram_tensor("cc_in", [1, 8], F32, kind="Internal")
    sc["cc_out"] = nc.dram_tensor("cc_out", [1, 8], F32, kind="Internal")

    consts = make_consts(nc)

    ctx = ExitStack()
    with ctx:
        p = Prog(nc, ctx)
        st = ctx.enter_context

        identb = st(nc.sbuf_tensor("identb", [128, 128], BF16))
        trib = st(nc.sbuf_tensor("trib", [128, 128], BF16))
        identf = st(nc.sbuf_tensor("identf", [128, 128], F32))
        ones_r_b = st(nc.sbuf_tensor("ones_r_b", [1, 128], BF16))
        ones_r_f = st(nc.sbuf_tensor("ones_r_f", [1, 128], F32))
        ones_c_b = st(nc.sbuf_tensor("ones_c_b", [128, 1], BF16))
        ones_c_f = st(nc.sbuf_tensor("ones_c_f", [128, 1], F32))
        rotTb = st(nc.sbuf_tensor("rotTb", [128, 128], BF16))
        epsc = st(nc.sbuf_tensor("epsc", [128, 1], F32))
        bqk = st(nc.sbuf_tensor("bqk", [128, 16], F32))
        bvrow = st(nc.sbuf_tensor("bvrow", [1, D], BF16))
        obrow = st(nc.sbuf_tensor("obrow", [1, D], BF16))
        stat = st(nc.sbuf_tensor("stat", [128, 64], F32))
        srow = st(nc.sbuf_tensor("srow", [1, 64], F32))
        b_qk2 = st(nc.sbuf_tensor("b_qk2", [1, 80], F32))
        bbias = st(nc.sbuf_tensor("bbias", [128, 8], F32))
        mxrow = st(nc.sbuf_tensor("mxrow", [1, 16], F32))
        gb = st(nc.sbuf_tensor("gb", [128, 12], F32))

        xT = st(nc.sbuf_tensor("xT", [128, NC, S], BF16))
        kT = st(nc.sbuf_tensor("kT", [128, NC, S], BF16))
        qq = st(nc.sbuf_tensor("qq", [128, NC, S], BF16))
        vT = st(nc.sbuf_tensor("vT", [128, 16, NH, HS + 1], BF16))
        woB = st(nc.sbuf_tensor("woB", [128, NC, D], BF16))
        wstr = st(nc.sbuf_tensor("wstr", [128, 2, 8, 128], BF16))
        eT = st(nc.sbuf_tensor("eT", [128, 3, 512], BF16))
        aT = st(nc.sbuf_tensor("aT", [128, NC, 512], BF16))
        _aT_addr = nc.lookup_mloc(aT).addr
        cosf = nc.alloc_sbuf_tensor_at("cosf", [128, S], BF16, offset=_aT_addr)
        sinf = nc.alloc_sbuf_tensor_at("sinf", [128, S], BF16, offset=_aT_addr + 4096)
        it2 = nc.alloc_sbuf_tensor_at("it2", [128, 512], I32, offset=_aT_addr)
        ftA = nc.alloc_sbuf_tensor_at("ftA", [128, 512], F32, offset=_aT_addr + 2048)
        ftB = nc.alloc_sbuf_tensor_at("ftB", [128, 512], F32, offset=_aT_addr + 4096)
        sp16b = nc.alloc_sbuf_tensor_at("sp16b", [128, 2, 512], F16, offset=_aT_addr + 6144)
        sqb = st(nc.sbuf_tensor("sqb", [128, 1536], BF16))
        xin = st(nc.sbuf_tensor("xin", [128, 2, D], F32))
        rt = st(nc.sbuf_tensor("rt", [128, D], F32))
        sqf = st(nc.sbuf_tensor("sqf", [128, D], F32))
        xot = st(nc.sbuf_tensor("xot", [128, 2, D], F32))
        ft1 = st(nc.sbuf_tensor("ft1", [128, 512], F32))
        ft2 = st(nc.sbuf_tensor("ft2", [128, 512], F32))
        ft3 = st(nc.sbuf_tensor("ft3", [128, 512], F32))
        it1 = st(nc.sbuf_tensor("it1", [128, 512], I32))
        mtile = st(nc.sbuf_tensor("mtile", [128, 512], F32))
        sxr = st(nc.sbuf_tensor("sxr", [1, 512], F32))
        sp16 = nc.alloc_sbuf_tensor_at("sp16", [128, 2, 512], F16,
                                       offset=nc.lookup_mloc(mtile).addr)
        spu16 = nc.alloc_sbuf_tensor_at("spu16", [128, 2, 512], F16,
                                        offset=nc.lookup_mloc(ft3).addr)
        sxb = st(nc.sbuf_tensor("sxb", [128, 512], F32))
        ft4 = nc.alloc_sbuf_tensor_at("ft4", [128, 512], F32,
                                      offset=nc.lookup_mloc(mtile).addr)
        kT_addr = nc.lookup_mloc(kT).addr
        qq_addr = nc.lookup_mloc(qq).addr
        assert qq_addr == kT_addr + 32768, (kT_addr, qq_addr)
        wf32 = nc.alloc_sbuf_tensor_at("wf32", [128, 2, 3072], F32, offset=kT_addr)
        wb16 = nc.alloc_sbuf_tensor_at("wb16", [128, 3072], BF16, offset=qq_addr)
        wdT0 = nc.alloc_sbuf_tensor_at("wdT0", [128, 32, D], BF16, offset=kT_addr)
        wdT1 = nc.alloc_sbuf_tensor_at("wdT1", [128, 24, D], BF16, offset=kT_addr)
        vT_addr = nc.lookup_mloc(vT).addr
        guh = nc.alloc_sbuf_tensor_at("guh", [128, 32, 512], BF16, offset=vT_addr)
        oq1 = nc.alloc_sbuf_tensor_at("oq1", [128, 24, 512], BF16, offset=vT_addr)

        pb = [Bank(st(nc.psum_tensor(f"pb{i}", [128, 512], F32))) for i in range(6)]
        pt = [Bank(st(nc.psum_tensor(f"pt{i}", [128, 128], BF16))) for i in range(2)]

        for tile, key in ((identb, "ident_b"), (identf, "ident_f"), (trib, "tri_b"),
                          (ones_r_b, "ones_row_b"), (ones_r_f, "ones_row_f"),
                          (ones_c_b, "ones_col_b"), (ones_c_f, "ones_col_f"),
                          (rotTb, "rotT_b")):
            p.dma("sp", [], tile[:], consts[key].ap())
        t_eps = p.op("dve", [], lambda e: e.memset(epsc[:], 1e-5))
        p.op("dve", [], lambda e: e.memset(mxrow[:], 0.0))
        p.dma("sp", [], bqk[:], ew["l0_bqk"].ap())
        t_bv = p.dma("sp", [], rt[0:1, 0:D], ew["l0_bv"].ap())
        t_ob = p.dma("sp", [], sqf[0:1, 0:D], ew["l0_ob"].ap())
        t_bvb = p.op("dve", [t_bv], lambda e: e.tensor_copy(out=bvrow[:], in_=rt[0:1, 0:D]))
        t_obb = p.op("dve", [t_ob], lambda e: e.tensor_copy(out=obrow[:], in_=sqf[0:1, 0:D]))

        def prep_attn(L):
            p.fence()
            aw = ew[f"l{L}_aw"].ap()
            ow = ew[f"l{L}_ow"].ap()
            buf_free = [None, None]
            t_cast_prev = None
            for c in range(NC):
                buf = wf32[:, c % 2]
                tl = p.dma("sp" if c % 2 == 0 else "pool", [buf_free[c % 2]], buf,
                           aw[c * 128:(c + 1) * 128, :],
                           stream=f"wf{c % 2}" if c % 2 == 0 else f"wfp{c % 2}")
                tc = p.op("dve", [tl] + (t_cast_prev if isinstance(t_cast_prev, list) else [t_cast_prev]), lambda e, b=buf: e.tensor_copy(
                    out=wb16[:], in_=b))
                buf_free[c % 2] = tc
                if L == 0:
                    wv5 = wb16[:].rearrange("p (hc h part e) -> p hc h part e",
                                            hc=8, h=2, part=3)
                    qsl = lambda hc: wv5[:, hc, :, 0, :]
                    ksl = lambda hc: wv5[:, hc, :, 1, :]
                    vall = wv5[:, :, :, 2, :]
                else:
                    wv4 = wb16[:].rearrange("p (hc part e) -> p hc part e",
                                            hc=8, part=3)
                    qsl = lambda hc: wv4[:, hc, 0, :]
                    ksl = lambda hc: wv4[:, hc, 1, :]
                    vall = wv4[:, :, 2, :]
                oqv = sc[f"wqk{L}"].ap()[0:8, c].rearrange("hc p e -> p hc e")
                okv = sc[f"wqk{L}"].ap()[8:16, c].rearrange("hc p e -> p hc e")
                if L == 0:
                    for h in range(2):
                        p.dma("sp", [tc], oqv[:, :, h * 64:(h + 1) * 64], wv5[:, :, h, 0, :])
                        p.dma("sp", [tc], okv[:, :, h * 64:(h + 1) * 64], wv5[:, :, h, 1, :])
                else:
                    p.dma("sp", [tc], oqv, wv4[:, :, 0, :])
                    p.dma("sp", [tc], okv, wv4[:, :, 1, :])
                tstores = [p.dma("sp", [tc], sc[f"wv{L}"].ap()[c], vall)]
                t_cast_prev = tstores[-1]
            for c in range(NC):
                buf = wf32[:, c % 2]
                tl = p.dma("sp" if c % 2 == 0 else "pool", [buf_free[c % 2], t_cast_prev],
                           buf[:, 0:D],
                           ow[c * 128:(c + 1) * 128, :],
                           stream=f"wf{c % 2}" if c % 2 == 0 else f"wfp{c % 2}")
                tc = p.op("dve", [tl] + (t_cast_prev if isinstance(t_cast_prev, list) else [t_cast_prev]), lambda e, b=buf: e.tensor_copy(
                    out=wb16[:, 0:D], in_=b[:, 0:D]))
                t_cast_prev = p.dma("sp", [tc], sc[f"wo{L}"].ap()[c], wb16[:, 0:D])
                buf_free[c % 2] = tc

        def prep_tern(nm, src):
            p.fence()
            W = ew[src].ap()
            rows, cols = W.tensor.shape
            nch = rows // 128
            SLAB = min(cols, 2048)
            nsl = cols // SLAB
            acc = stat[:, 0:1]
            tz = p.op("dve", [p.now("dve")], lambda e: e.memset(acc, 0.0))
            buf_free = [None, None]
            tacc = tz
            for c in range(nch):
                for sl in range(nsl):
                    i = (c * nsl + sl) % 2
                    buf = wf32[:, i, 0:SLAB]
                    tl = p.dma("sp" if i == 0 else "pool", [buf_free[i]], buf,
                               W[c * 128:(c + 1) * 128, sl * SLAB:(sl + 1) * SLAB],
                               stream=f"wf{i}" if i == 0 else f"wfp{i}")
                    tr = p.op("dve", [tl, tacc], lambda e, b=buf: e.tensor_reduce(
                        out=stat[:, 1:2], in_=b, axis=AX.X, op=ALU.add,
                        apply_absolute_value=True))
                    tacc = p.op("dve", [tr], lambda e: e.tensor_add(
                        out=acc, in0=acc, in1=stat[:, 1:2]))
                    buf_free[i] = tr
            tsum = p.op("pe", [tacc, pb[0].free], lambda e: e.matmul(
                pb[0].t[0:1, 0:1], ones_c_f[:], acc, start=True, stop=True))
            tm = p.op("dve", [tsum], lambda e: e.tensor_scalar(
                out=srow[0:1, 0:1], in0=pb[0].t[0:1, 0:1], scalar1=1.0 / float(rows * cols),
                scalar2=1e-5, op0=ALU.mult, op1=ALU.max))
            pb[0].free = tm
            trc = p.op("dve", [tm], lambda e: e.reciprocal(
                out=srow[0:1, 1:2], in_=srow[0:1, 0:1]))
            tb = p.op("pe", [trc, pb[1].free], lambda e: e.matmul(
                pb[1].t[:, 0:2], ones_r_f[:], srow[0:1, 0:2], start=True, stop=True))
            sidx = {"l0gT": 2, "l0uT": 4, "l0dT": 6}[nm]
            tbc = p.op("dve", [tb], lambda e: e.tensor_copy(
                out=stat[:, sidx:sidx + 2], in_=pb[1].t[:, 0:2]))
            pb[1].free = tbc
            wb_free = None
            clip_last = [None, None]
            for c in range(nch):
                for sl in range(nsl):
                    i = (c * nsl + sl) % 2
                    buf = wf32[:, i, 0:SLAB]
                    tl = p.dma("sp" if i == 0 else "pool", [buf_free[i]], buf,
                               W[c * 128:(c + 1) * 128, sl * SLAB:(sl + 1) * SLAB],
                               stream=f"wf{i}" if i == 0 else f"wfp{i}")
                    ti_last = None
                    for j in range(SLAB // 512):
                        cs = slice(sl * SLAB + j * 512, sl * SLAB + (j + 1) * 512)
                        jj = (j % 2)
                        IT = it1 if jj == 0 else it2
                        ti = p.op("dve", [tl, tbc, clip_last[jj]], lambda e, b=buf, j=j, IT=IT: e.tensor_scalar(
                            out=IT[:], in0=b[:, j * 512:(j + 1) * 512],
                            scalar1=stat[:, sidx + 1:sidx + 2],
                            scalar2=None, op0=ALU.mult))
                        ti_last = ti
                        clip_last[jj] = p.op("dve", [ti, wb_free], lambda e, cs=cs, IT=IT: e.tensor_scalar(
                            out=wb16[:, cs.start - sl * SLAB:cs.stop - sl * SLAB], in0=IT[:],
                            scalar1=1.0, scalar2=-1.0,
                            op0=ALU.min, op1=ALU.max))
                    tclip = clip_last[0 if (SLAB // 512) % 2 else 1]
                    tclip = p.now("dve")
                    if nm == "l0dT":
                        tst = p.dma("sp", [tclip], sc[nm].ap()[c][:, sl * SLAB:(sl + 1) * SLAB],
                                    wb16[:, 0:SLAB])
                    else:
                        nic = SLAB // 128
                        odst = sc[nm].ap()[sl * nic:(sl + 1) * nic, c].rearrange(
                            "ic p e -> p ic e")
                        tst = p.dma("sp", [tclip], odst,
                                    wb16[:, 0:SLAB].rearrange("p (ic e) -> p ic e", e=128))
                    buf_free[i] = ti_last
                    wb_free = tst

        def prep_lpbq(nm, src):
            p.fence()
            W = ew[src].ap()
            rows, cols = W.tensor.shape
            nch = rows // 128
            ngrp = cols // 32
            buf_free = [None, None]
            sq_slot = [None, None, None, None]
            tlast = None
            for c in range(nch):
                buf = wf32[:, c % 2][:, 0:cols]
                tl = p.dma("sp" if c % 2 == 0 else "pool", [buf_free[c % 2]], buf,
                           W[c * 128:(c + 1) * 128, :],
                           stream=f"wf{c % 2}" if c % 2 == 0 else f"wfp{c % 2}")
                bview = buf.rearrange("p (g k) -> p g k", k=32)
                ts1 = p.op("dve", [tl, tlast], lambda e, b=bview, n=ngrp: e.tensor_reduce(
                    out=ft1[:, 0:n], in_=b, axis=AX.X, op=ALU.max,
                    apply_absolute_value=True))
                ts2 = p.op("dve", [ts1], lambda e, n=ngrp: e.tensor_scalar(
                    out=ft2[:, 0:n], in0=ft1[:, 0:n], scalar1=1.0 / 7.0,
                    scalar2=1e-12, op0=ALU.mult, op1=ALU.max))
                ts3 = p.op("dve", [ts2], lambda e, n=ngrp: e.reciprocal(
                    out=ft3[:, 0:n], in_=ft2[:, 0:n]))
                tq = None
                for j in range(ngrp // 16):
                    gsl = slice(j * 16, (j + 1) * 16)
                    csl = slice(j * 512, (j + 1) * 512)
                    wv = bview[:, gsl, :]
                    rsb = bass.broadcast_tensor_aps(ft3[:, gsl][:, :, None], wv)[0]
                    sb_ = bass.broadcast_tensor_aps(ft2[:, gsl][:, :, None], wv)[0]
                    ti = p.op("dve", [ts3, tq], lambda e, wv=wv, rsb=rsb: e.tensor_tensor(
                        out=it1[:].rearrange("p (g k) -> p g k", k=32),
                        in0=wv, in1=rsb, op=ALU.mult))
                    tc1 = p.op("dve", [ti], lambda e: e.tensor_scalar(
                        out=ft4[:], in0=it1[:], scalar1=7.0, scalar2=-8.0,
                        op0=ALU.min, op1=ALU.max))
                    tq = p.op("dve", [tc1], lambda e, sb_=sb_, csl=csl: e.tensor_tensor(
                        out=wb16[:, csl].rearrange("p (g k) -> p g k", k=32),
                        in0=ft4[:].rearrange("p (g k) -> p g k", k=32),
                        in1=sb_, op=ALU.mult))
                njc = cols // 128
                gsz = min(12, njc)
                tev_grp = []
                for jc in range(njc):
                    b_ = pt[jc % 2]
                    ttr = p.op("pe", [tq, b_.free], lambda e, jc=jc, b_=b_: e.transpose(
                        b_.t[:], wb16[:, jc * 128:(jc + 1) * 128], identb[:]))
                    slot = jc % gsz
                    tev = p.op("dve", [ttr, sq_slot[0] if slot == 0 else None],
                               lambda e, b_=b_, slot=slot: e.tensor_copy(
                                   out=sqb[:, slot * 128:slot * 128 + 128], in_=b_.t[:]))
                    b_.free = tev
                    tev_grp.append(tev)
                    if slot == gsz - 1:
                        j0 = jc - gsz + 1
                        if nm == "l1dT":
                            odst = sc[nm].ap()[j0:j0 + gsz][:, :, c * 128:(c + 1) * 128]                                 .rearrange("t p e -> p t e")
                        else:
                            odst = sc[nm].ap()[c, j0:j0 + gsz].rearrange("t p e -> p t e")
                        tst = p.dma("sp", tev_grp, odst,
                                    sqb[:, 0:gsz * 128].rearrange("p (t e) -> p t e", e=128))
                        sq_slot[0] = tst
                        tev_grp = []
                tlast = tq
                buf_free[c % 2] = tq

        if UPTO >= 0:
            with nc.named_scope("prep_attn0"):
                prep_attn(0)
                p.fence()
        if UPTO >= 2:
            with nc.named_scope("prep_tern"):
                prep_tern("l0gT", "l0_wgT")
                prep_tern("l0uT", "l0_wuT")
                prep_tern("l0dT", "l0_wdT")
                p.fence()
        if UPTO >= 3:
            with nc.named_scope("prep_attn1"):
                prep_attn(1)
                p.fence()
        if UPTO >= 4:
            with nc.named_scope("prep_lpbq"):
                prep_lpbq("l1gT", "l1_wg")
                prep_lpbq("l1uT", "l1_wu")
                prep_lpbq("l1dT", "l1_wd")
                p.fence()

        def emit_ln_store(tt, tdep, L, xdst, prev_tok):
            ts1 = p.op("dve", [tdep], lambda e: e.reduce_sum(
                out=stat[:, 16:17], in_=rt[:], axis=AX.X))
            ts2 = p.op("act", [tdep], lambda e: e.activation(
                out=sqf[:], in_=rt[:], func=AF.Square, accum_out=stat[:, 17:18]))
            tm = p.op("dve", [ts1], lambda e: e.tensor_scalar_mul(
                out=stat[:, 18:19], in0=stat[:, 16:17], scalar1=1.0 / D))
            tv1 = p.op("dve", [ts2], lambda e: e.tensor_scalar_mul(
                out=stat[:, 19:20], in0=stat[:, 17:18], scalar1=1.0 / D))
            tm2 = p.op("dve", [tm], lambda e: e.tensor_mul(
                out=stat[:, 20:21], in0=stat[:, 18:19], in1=stat[:, 18:19]))
            tvv = p.op("dve", [tv1, tm2], lambda e: e.tensor_sub(
                out=stat[:, 21:22], in0=stat[:, 19:20], in1=stat[:, 20:21]))
            tsd = p.op("act", [tvv, t_eps], lambda e: e.activation(
                out=stat[:, 22:23], in_=stat[:, 21:22], func=AF.Sqrt, bias=epsc[:]))
            trs = p.op("dve", [tsd], lambda e: e.reciprocal(
                out=stat[:, 23:24], in_=stat[:, 22:23]))
            buf = xot[:, tt % 2]
            tx = p.op("dve", [trs, prev_tok], lambda e, b=buf: e.tensor_scalar(
                out=b, in0=rt[:], scalar1=stat[:, 18:19], scalar2=stat[:, 23:24],
                op0=ALU.subtract, op1=ALU.mult))
            tst = p.dma("sp", [tx], xdst[tt * 128:(tt + 1) * 128, :], buf, stream=f"xo{tt % 2}")
            tmax = None
            if L == 1:
                tmx = p.op("dve", [tx], lambda e, b=buf: e.tensor_reduce(
                    out=stat[:, 24:25], in_=b, axis=AX.X, op=ALU.max,
                    apply_absolute_value=True))
                tmax = p.op("dve", [tmx], lambda e: e.tensor_tensor(
                    out=stat[:, 25:26], in0=stat[:, 25:26], in1=stat[:, 24:25],
                    op=ALU.max))
            return tst, (tx if tmax is None else tmax)

        def emit_attn(L, seq, x_src, x_dst):
            wqk = sc[f"wqk{L}"].ap()
            wv = sc[f"wv{L}"].ap()
            wo = sc[f"wo{L}"].ap()

            if L == 1:
                if NO_TABLE:
                    p.op("dve", [], lambda e: e.memset(cosf[:], 1.0))
                    p.op("dve", [], lambda e: e.memset(sinf[:], 0.0))
                else:
                    p.dma("sp", [], cosf[:], consts["cosf"].ap())
                    p.dma("sp", [], sinf[:], consts["sinf"].ap())
            sc_a1 = nc.enter_named_scope(f"at{L}{seq}_a1", False)
            xin_free = [None, None]
            t_cast_prev = [None]
            for tt in range(16):
                buf = xin[:, tt % 2]
                tl = p.dma("sp", [xin_free[tt % 2]], buf, x_src[tt * 128:(tt + 1) * 128, :], stream=f"xin{tt % 2}")
                tc = p.op("dve", [tl] + (t_cast_prev if isinstance(t_cast_prev, list) else [t_cast_prev]), lambda e, b=buf: e.tensor_copy(
                    out=wb16[:, 0:D], in_=b))
                xin_free[tt % 2] = tc
                td_cast = None
                if DBG == ("cast", seq):
                    td_cast = p.dma("sp", [tc], dbg16.ap().rearrange(
                        "p (t d) -> p t d", t=16)[:, tt], wb16[:, 0:D], stream="dbgc")
                for c in range(NC):
                    b_ = pt[c % 2]
                    ttr = p.op("pe", [tc, b_.free], lambda e, c=c, b_=b_: e.transpose(
                        b_.t[:], wb16[:, c * 128:(c + 1) * 128], identb[:]))
                    tev = p.op("dve", [ttr], lambda e, c=c, tt=tt, b_=b_: e.tensor_copy(
                        out=xT[:, c, tt * 128:(tt + 1) * 128], in_=b_.t[:]))
                    b_.free = tev
                t_cast_prev = [p.now("pe"), td_cast]
            nc.leave_named_scope(f"at{L}{seq}_a1", sc_a1[0], False)
            sc_a2 = nc.enter_named_scope(f"at{L}{seq}_a2", False)
            t_xT = p.now("dve")
            if DBG == ("xT", seq):
                p.dma("sp", [t_xT], dbg16.ap().rearrange("p (c s) -> p c s", c=NC), xT[:])

            tv_w = [p.dma("sp", [p.now("dve")], woB[:, c, :], wv[c]) for c in range(NC)]
            bank = pb[4]
            for tt in range(16):
                vsl = vT[:, tt]
                tms = p.op("dve", [bank.free], lambda e, v=vsl: e.memset(v, 1.0))
                for half in range(2):
                    tmm = None
                    for c in range(NC):
                        tmm = p.op("pe", [t_xT, tv_w[c], bank.free], lambda e, c=c, tt=tt, h=half, bk=bank: e.matmul(
                            bk.t[:], xT[:, c, tt * 128:(tt + 1) * 128],
                            woB[:, c, h * 512:(h + 1) * 512],
                            start=(c == 0), stop=(c == NC - 1 and L != 0)))
                    if L == 0:
                        tmm = p.op("pe", [tmm, t_bvb], lambda e, h=half, bk=bank: e.matmul(
                            bk.t[:], ones_r_b[:], bvrow[0:1, h * 512:(h + 1) * 512],
                            start=False, stop=True))
                    vout = vsl[:, half * 8:(half + 1) * 8, 0:64]
                    tev = p.op("dve", [tmm, tms], lambda e, vo=vout, bk=bank: e.tensor_copy(
                        out=vo, in_=bk.t[:].rearrange("p (h e) -> p h e", h=8)))
                    bank.free = tev
            t_vdone = p.now("dve")
            nc.leave_named_scope(f"at{L}{seq}_a2", sc_a2[0], False)
            sc_a3 = nc.enter_named_scope(f"at{L}{seq}_a3", False)

            w_free = [None, None]
            rope_war = [None, None]
            norm_war = [None, None]
            norm_war2 = [None]
            rope_ev = [None]

            def qk_prod(which, dst):
                hbase = 0 if which == "q" else 8
                for hc in range(8):
                    wbuf = wstr[:, hc % 2]
                    tw = p.dma("sp", [w_free[hc % 2]], wbuf[:, 0:8, :],
                               wqk[hbase + hc].rearrange("c p e -> p c e"),
                               stream=f"ws{hc % 2}")
                    for c in range(NC):
                        for g in range(4):
                            bk = pb[g]
                            p.op("pe", [tw, t_xT, bk.free], lambda e, c=c, g=g, w=wbuf, bk=bk: e.matmul(
                                bk.t[:], w[:, c, :], xT[:, c, g * 512:(g + 1) * 512],
                                start=(c == 0), stop=(c == NC - 1)))
                    for g in range(4):
                        bk = pb[g]
                        tmm = p.now("pe")
                        if L == 0 or NO_ROPE:
                            bias_ap = bqk[:, hbase + hc:hbase + hc + 1]
                            if L == 0:
                                tev = p.op("act", [tmm], lambda e, g=g, hc=hc, b=bias_ap, bk=bk: e.activation(
                                    out=dst[:, hc, g * 512:(g + 1) * 512], in_=bk.t[:],
                                    func=AF.Identity, bias=b))
                            else:
                                tev = p.op("act", [tmm], lambda e, g=g, hc=hc, bk=bk: e.activation(
                                    out=dst[:, hc, g * 512:(g + 1) * 512], in_=bk.t[:],
                                    func=AF.Identity))
                            bk.free = tev
                        else:
                            tst_ = p.op("act", [tmm, rope_war[g % 2]], lambda e, g=g, bk=bk: e.activation(
                                out=sqb[:, (g % 2) * 512:(g % 2 + 1) * 512], in_=bk.t[:],
                                func=AF.Identity))
                            if ROPE_VAR == 1:
                                tt1 = p.op("dve", [tmm, rope_ev[0]], lambda e, g=g, bk=bk: e.tensor_mul(
                                    out=ft1[:], in0=bk.t[:],
                                    in1=cosf[:, g * 512:(g + 1) * 512]))
                                bk.free = tt1
                                tev = p.op("dve", [tt1], lambda e, g=g, hc=hc: e.tensor_copy(
                                    out=dst[:, hc, g * 512:(g + 1) * 512], in_=ft1[:]))
                                rope_ev[0] = tev
                                continue_flag = True
                            else:
                                continue_flag = False
                            if continue_flag:
                                toks_done = None
                                continue
                            tt1 = p.op("dve", [tmm, tst_, rope_ev[0]], lambda e, g=g, bk=bk: e.tensor_mul(
                                out=ft1[:], in0=bk.t[:],
                                in1=cosf[:, g * 512:(g + 1) * 512]))
                            bk.free = tt1
                            rbk = pb[4]
                            trot = p.op("pe", [tst_, rbk.free], lambda e, g=g, rbk=rbk: e.matmul(
                                rbk.t[:], rotTb[:], sqb[:, (g % 2) * 512:(g % 2 + 1) * 512],
                                start=True, stop=True))
                            rope_war[g % 2] = trot
                            tt2 = p.op("dve", [trot, rope_ev[0]], lambda e, g=g, rbk=rbk: e.tensor_mul(
                                out=ft2[:], in0=rbk.t[:],
                                in1=sinf[:, g * 512:(g + 1) * 512]))
                            rbk.free = tt2
                            tev = p.op("dve", [tt1, tt2], lambda e, g=g, hc=hc: e.tensor_add(
                                out=dst[:, hc, g * 512:(g + 1) * 512],
                                in0=ft1[:], in1=ft2[:]))
                            rope_ev[0] = tev
                    w_free[hc % 2] = p.now("pe")

            qk_prod("k", kT)
            t_k = [p.now("dve"), p.now("act"), p.now("pe"), p.now("pool")]
            qk_prod("q", qq)
            t_qk = [p.now("dve"), p.now("act"), p.now("pe"), p.now("pool")]

            nc.leave_named_scope(f"at{L}{seq}_a3", sc_a3[0], False)
            sc_b0 = nc.enter_named_scope(f"at{L}{seq}_b0", False)
            tzb = p.op("dve", t_k, lambda e: e.memset(b_qk2[0:1, 0:80], 0.0))
            sq_par = [None, None]
            red_prev = [tzb, tzb]
            for which, src_t, base in (("k", kT, 8), ("q", qq, 0)):
                tdep = t_k if which == "k" else t_qk
                for c in range(NC):
                    for g in range(4):
                        par = g % 2
                        tsq = p.op("act", tdep + [sq_par[par]], lambda e, c=c, g=g, s=src_t, par=par: e.activation(
                            out=sqb[:, par * 512:(par + 1) * 512],
                            in_=s[:, c, g * 512:(g + 1) * 512], func=AF.Square))
                        bk = pb[4 + par]
                        tmm = p.op("pe", [tsq, bk.free], lambda e, par=par, bk=bk: e.matmul(
                            bk.t[0:1, :], ones_c_b[:], sqb[:, par * 512:(par + 1) * 512],
                            start=True, stop=True))
                        sq_par[par] = tmm
                        idx = (base + c) * 4 + g
                        trd = p.op("dve", [tmm, red_prev[par]], lambda e, idx=idx, bk=bk: e.tensor_reduce(
                            out=b_qk2[0:1, idx:idx + 1], in_=bk.t[0:1, :], axis=AX.X,
                            op=ALU.max))
                        bk.free = trd
                        red_prev[par] = trd
            tprev = p.op("dve", [red_prev[0], red_prev[1]], lambda e: e.tensor_reduce(
                out=b_qk2[0:1, 64:80], in_=b_qk2[0:1, 0:64].rearrange("a (c g) -> a c g", g=4),
                axis=AX.X, op=ALU.max))
            tb1 = p.op("dve", [tprev], lambda e: e.tensor_mul(
                out=srow[0:1, 8:16], in0=b_qk2[0:1, 64:72], in1=b_qk2[0:1, 72:80]))
            tb2 = p.op("act", [tb1], lambda e: e.activation(
                out=srow[0:1, 16:24], in_=srow[0:1, 8:16], func=AF.Sqrt))
            tb3 = p.op("dve", [tb2], lambda e: e.tensor_scalar_mul(
                out=srow[0:1, 24:32], in0=srow[0:1, 16:24], scalar1=-0.125))
            tb4 = p.op("pe", [tb3, pb[5].free], lambda e: e.matmul(
                pb[5].t[:, 0:8], ones_r_f[:], srow[0:1, 24:32], start=True, stop=True))
            tb5 = p.op("dve", [tb4], lambda e: e.tensor_copy(out=bbias[:], in_=pb[5].t[:, 0:8]))
            pb[5].free = tb5

            nc.leave_named_scope(f"at{L}{seq}_b0", sc_b0[0], False)
            sc_b2 = nc.enter_named_scope(f"at{L}{seq}_b2", False)
            t_wo = [p.dma("sp", [t_vdone], woB[:, c, :], wo[c]) for c in range(NC)]

            xo_prev = [None, None]
            rt_prev = None
            tst_last = None
            for qt in range(4):
                for hd in range(NH):
                    hc2 = hd // 2
                    rb = (hd % 2) * 64
                    kmax = 4 * (qt + 1)
                    abk = pb[2 + (hd % 2)]
                    t_pv = None
                    texs = [None] * (kmax + 2)
                    for kc in range(kmax + 2):
                      if kc < kmax:
                        o = max(0, kc * 128 - qt * 512)
                        n = 512 - o
                        sbk = pb[[0, 1, 4][kc % 3]]
                        tsc = p.op("pe", [sbk.free, tb5], lambda e, kc=kc, hc2=hc2, rb=rb, qt=qt, o=o, n=n, sbk=sbk: e.matmul(
                            sbk.t[:, o:o + n],
                            kT[rb:rb + 64, hc2, kc * 128:(kc + 1) * 128],
                            qq[rb:rb + 64, hc2, qt * 512 + o:qt * 512 + o + n],
                            start=True, stop=True))
                        slot = kc % 3
                        tex = p.op("act", [tsc, t_pv], lambda e, sbk=sbk, o=o, n=n, slot=slot, hc2=hc2: e.activation(
                            out=eT[:, slot, o:o + n], in_=sbk.t[:, o:o + n], func=AF.Exp,
                            bias=bbias[:, hc2:hc2 + 1], scale=0.125))
                        sbk.free = tex
                        if kc >= 4 * qt:
                            tex = p.op("pool", [tex], lambda e, slot=slot, o=o: e.tensor_mul(
                                out=eT[:, slot, o:o + 128], in0=eT[:, slot, o:o + 128],
                                in1=trib[:]))
                        texs[kc] = tex
                      if kc >= 2:
                        kp = kc - 2
                        o = max(0, kp * 128 - qt * 512)
                        n = 512 - o
                        slot = kp % 3
                        t_pv = p.op("pe", [texs[kp], t_vdone] + ([t_pv] if t_pv else [abk.free]),
                                    lambda e, kp=kp, hd=hd, o=o, n=n, slot=slot, kmax=kmax, abk=abk: e.matmul(
                                        abk.t[0:65, o:o + n], vT[:, kp, hd, 0:65],
                                        eT[:, slot, o:o + n],
                                        start=(kp == 0), stop=(kp == kmax - 1)))
                    trb = p.op("dve", [t_pv, norm_war[hd % 2]], lambda e, hd=hd, abk=abk: e.tensor_copy(
                        out=sqb[0:1, (hd % 2) * 512:(hd % 2) * 512 + 512],
                        in_=abk.t[64:65, 0:512]))
                    tbc = p.op("pe", [trb, pb[5].free], lambda e, hd=hd: e.matmul(
                        pb[5].t[0:64, :], ones_r_b[0:1, 0:64],
                        sqb[0:1, (hd % 2) * 512:(hd % 2) * 512 + 512],
                        start=True, stop=True))
                    trc = p.op("dve", [tbc, norm_war2[0]], lambda e: e.reciprocal(
                        out=ft1[0:64, :], in_=pb[5].t[0:64, :]))
                    pb[5].free = trc
                    tn = p.op("dve", [trc, t_pv], lambda e, hc2=hc2, rb=rb, abk=abk: e.tensor_tensor(
                        out=aT[rb:rb + 64, hc2, :], in0=abk.t[0:64, 0:512],
                        in1=ft1[0:64, :], op=ALU.mult))
                    abk.free = tn
                    norm_war2[0] = tn
                    norm_war[hd % 2] = tn
                t_aT = p.now("dve")
                if DBG == ("aT", seq):
                    p.dma("sp", [t_aT], dbg16.ap().rearrange("p (c q s) -> p c q s", c=NC, q=4)[:, :, qt], aT[:])
                for tti in range(4):
                    tt = qt * 4 + tti
                    txl = p.dma("pool", [xin_free[tt % 2]], xin[:, tt % 2],
                                x_src[tt * 128:(tt + 1) * 128, :], stream=f"xp{tt % 2}")
                    tmm_h = []
                    for half in range(2):
                        bk = pb[0 + half]
                        tmm = None
                        for c in range(NC):
                            tmm = p.op("pe", [t_aT, t_wo[c], bk.free], lambda e, c=c, tti=tti, h=half, bk=bk: e.matmul(
                                bk.t[:], aT[:, c, tti * 128:(tti + 1) * 128],
                                woB[:, c, h * 512:(h + 1) * 512],
                                start=(c == 0), stop=(c == NC - 1 and L != 0)))
                        if L == 0:
                            tmm = p.op("pe", [tmm, t_obb], lambda e, h=half, bk=bk: e.matmul(
                                bk.t[:], ones_r_b[:], obrow[0:1, h * 512:(h + 1) * 512],
                                start=False, stop=True))
                        tmm_h.append(tmm)
                    tr1 = p.op("dve", [tmm_h[0], txl, rt_prev], lambda e, tt=tt: e.tensor_add(
                        out=rt[:, 0:512], in0=xin[:, tt % 2, 0:512], in1=pb[0].t[:]))
                    pb[0].free = tr1
                    tr2 = p.op("dve", [tmm_h[1], tr1], lambda e, tt=tt: e.tensor_add(
                        out=rt[:, 512:1024], in0=xin[:, tt % 2, 512:1024], in1=pb[1].t[:]))
                    pb[1].free = tr2
                    xin_free[tt % 2] = tr2
                    if DBG == ("rt", seq):
                        tr2 = p.dma("sp", [tr2], dbgf.ap()[:, tt], rt[:])
                    tst_last, tln = emit_ln_store(tt, tr2, L, x_dst,
                                                  xo_prev[tt % 2])
                    rt_prev = tln
                    xo_prev[tt % 2] = tst_last
            nc.leave_named_scope(f"at{L}{seq}_b2", sc_b2[0], False)
            return tst_last

        def emit_ffn0(seq, x_src, x_dst):
            l0g = sc["l0gT"].ap()
            l0u = sc["l0uT"].ap()
            l0d = sc["l0dT"].ap()
            rt_prev = [None]
            xo_prev2 = [None, None]

            sc_f0 = nc.enter_named_scope(f"f0{seq}_xq", False)
            xin_free = [None, None]
            t_cast_prev = [None]
            for tt in range(16):
                buf = xin[:, tt % 2]
                tl = p.dma("sp", [xin_free[tt % 2]], buf,
                           x_src[tt * 128:(tt + 1) * 128, :], stream=f"xin{tt % 2}")
                tam = p.op("dve", [tl], lambda e, b=buf: e.tensor_reduce(
                    out=stat[:, 26:27], in_=b, axis=AX.X, op=ALU.max,
                    apply_absolute_value=True))
                tcl = p.op("dve", [tam], lambda e: e.tensor_scalar_max(
                    out=stat[:, 27:28], in0=stat[:, 26:27], scalar1=1e-5))
                trc = p.op("dve", [tcl], lambda e: e.reciprocal(
                    out=stat[:, 28:29], in_=stat[:, 27:28]))
                tsc_ = p.op("dve", [trc], lambda e: e.tensor_scalar_mul(
                    out=stat[:, 29:30], in0=stat[:, 28:29], scalar1=127.0))
                tsx = p.op("dve", [tcl], lambda e, tt=tt: e.tensor_scalar_mul(
                    out=stat[:, 32 + tt:33 + tt], in0=stat[:, 27:28],
                    scalar1=1.0 / 127.0))
                tq = t_cast_prev
                for h in range(2):
                    ti = p.op("dve", [tsc_, tl] + (tq if isinstance(tq, list) else [tq]),
                              lambda e, b=buf, h=h: e.tensor_scalar(
                                  out=it1[:], in0=b[:, h * 512:(h + 1) * 512],
                                  scalar1=stat[:, 29:30], scalar2=None, op0=ALU.mult))
                    tq = p.op("dve", [ti], lambda e, h=h: e.tensor_copy(
                        out=sqb[:, h * 512:(h + 1) * 512], in_=it1[:]))
                xin_free[tt % 2] = ("dve", p.cnt["dve"])
                for c in range(NC):
                    b_ = pt[c % 2]
                    ttr = p.op("pe", [tq, b_.free], lambda e, c=c, b_=b_: e.transpose(
                        b_.t[:], sqb[:, c * 128:(c + 1) * 128], identb[:]))
                    tev = p.op("dve", [ttr], lambda e, c=c, tt=tt, b_=b_: e.tensor_copy(
                        out=xT[:, c, tt * 128:(tt + 1) * 128], in_=b_.t[:]))
                    b_.free = tev
                t_cast_prev = [p.now("pe")]
            t_xq = p.now("dve")

            nc.leave_named_scope(f"f0{seq}_xq", sc_f0[0], False)
            sc_f1 = nc.enter_named_scope(f"f0{seq}_main", False)
            t_wd = []
            for ic in range(32):
                t_wd.append(p.dma("sp", [t_xq], wdT0[:, ic, :], l0d[ic]))

            mean_g = stat[:, 2:3]
            mean_u = stat[:, 4:5]
            mean_d = stat[:, 6:7]
            for g in range(4):
                tsxr = []
                for j in range(4):
                    tt = g * 4 + j
                    b_ = pb[5]
                    ttr = p.op("pe", [t_xq, b_.free], lambda e, tt=tt, b_=b_: e.transpose(
                        b_.t[0:1, 0:128], stat[:, 32 + tt:33 + tt], identf[:]))
                    tcp = p.op("dve", [ttr], lambda e, j=j, b_=b_: e.tensor_copy(
                        out=sxr[0:1, j * 128:(j + 1) * 128], in_=b_.t[0:1, 0:128]))
                    b_.free = tcp
                    tsxr.append(tcp)
                tbc = p.op("pe", [pb[4].free] + tsxr, lambda e: e.matmul(
                    pb[4].t[:], ones_r_f[:], sxr[0:1, :], start=True, stop=True))
                tsxb = p.op("dve", [tbc], lambda e: e.tensor_copy(
                    out=sxb[:], in_=pb[4].t[:]))
                pb[4].free = tsxb
                tmz = p.op("dve", [pb[0].free, pb[1].free], lambda e: e.memset(mtile[:], 0.0))
                w_free2 = [None, None]
                tgu_prev = None
                for ic in range(32):
                    twu = p.dma("sp", [w_free2[0]], wstr[:, 0],
                                l0u[ic].rearrange("c p e -> p c e"), stream="fu")
                    twg = p.dma("sp", [w_free2[1]], wstr[:, 1],
                                l0g[ic].rearrange("c p e -> p c e"), stream="fg")
                    for c in range(NC):
                        p.op("pe", [twu, t_xq, pb[0].free if c == 0 else None],
                             lambda e, c=c, g=g: e.matmul(
                                 pb[0].t[:], wstr[:, 0, c, :], xT[:, c, g * 512:(g + 1) * 512],
                                 start=(c == 0), stop=(c == NC - 1)))
                    tmmu = p.now("pe")
                    w_free2[0] = tmmu
                    for c in range(NC):
                        p.op("pe", [twg, t_xq, pb[1].free if c == 0 else None],
                             lambda e, c=c, g=g: e.matmul(
                                 pb[1].t[:], wstr[:, 1, c, :], xT[:, c, g * 512:(g + 1) * 512],
                                 start=(c == 0), stop=(c == NC - 1)))
                    tmmg = p.now("pe")
                    w_free2[1] = tmmg
                    tup = p.op("dve", [tmmu, tsxb], lambda e: e.scalar_tensor_tensor(
                        out=ft1[:], in0=pb[0].t[:], scalar=mean_u, in1=sxb[:],
                        op0=ALU.mult, op1=ALU.mult))
                    pb[0].free = tup
                    tga = p.op("dve", [tmmg, tsxb], lambda e: e.scalar_tensor_tensor(
                        out=ft2[:], in0=pb[1].t[:], scalar=mean_g, in1=sxb[:],
                        op0=ALU.mult, op1=ALU.mult))
                    pb[1].free = tga
                    tsi = p.op("act", [tga], lambda e: e.activation(
                        out=ft3[:], in_=ft2[:], func=AF.Silu))
                    tgu = p.op("dve", [tsi, tup, tgu_prev], lambda e, ic=ic: e.tensor_mul(
                        out=guh[:, ic, :], in0=ft1[:], in1=ft3[:]))
                    tm1 = p.op("dve", [tgu, tmz], lambda e, ic=ic: e.scalar_tensor_tensor(
                        out=mtile[:], in0=guh[:, ic, :], scalar=1.0, in1=mtile[:],
                        op0=ALU.mult, op1=ALU.max))
                    tgu_prev = p.op("dve", [tm1], lambda e, ic=ic: e.scalar_tensor_tensor(
                        out=mtile[:], in0=guh[:, ic, :], scalar=-1.0, in1=mtile[:],
                        op0=ALU.mult, op1=ALU.max))
                tscq = []
                for j in range(4):
                    tt = g * 4 + j
                    b_ = pb[5]
                    ttr = p.op("pe", [tgu_prev, b_.free], lambda e, j=j, b_=b_: e.transpose(
                        b_.t[0:128, 0:128], mtile[:, j * 128:(j + 1) * 128], identf[:]))
                    trd = p.op("dve", [ttr], lambda e, b_=b_: e.tensor_reduce(
                        out=stat[:, 26:27], in_=b_.t[0:128, 0:128], axis=AX.X, op=ALU.max))
                    b_.free = trd
                    tcl = p.op("dve", [trd], lambda e: e.tensor_scalar_max(
                        out=stat[:, 27:28], in0=stat[:, 26:27], scalar1=1e-5))
                    tds = p.op("dve", [tcl], lambda e, j=j: e.tensor_scalar_mul(
                        out=stat[:, 48 + j:49 + j], in0=stat[:, 27:28],
                        scalar1=1.0 / 127.0))
                    tds2 = p.op("dve", [tds], lambda e, j=j: e.tensor_mul(
                        out=stat[:, 48 + j:49 + j], in0=stat[:, 48 + j:49 + j],
                        in1=mean_d))
                    trc = p.op("dve", [tcl], lambda e: e.reciprocal(
                        out=stat[:, 28:29], in_=stat[:, 27:28]))
                    tsc_ = p.op("dve", [trc], lambda e: e.tensor_scalar_mul(
                        out=stat[:, 29:30], in0=stat[:, 28:29], scalar1=127.0))
                    b2 = pb[5]
                    ttr2 = p.op("pe", [tsc_, b2.free], lambda e, b2=b2: e.transpose(
                        b2.t[0:1, 0:128], stat[:, 29:30], identf[:]))
                    tcp = p.op("dve", [ttr2, tds2], lambda e, j=j, b2=b2: e.tensor_copy(
                        out=sxr[0:1, j * 128:(j + 1) * 128], in_=b2.t[0:1, 0:128]))
                    b2.free = tcp
                    tscq.append(tcp)
                tbc2 = p.op("pe", [pb[4].free] + tscq, lambda e: e.matmul(
                    pb[4].t[:], ones_r_f[:], sxr[0:1, :], start=True, stop=True))
                tsqb2 = p.op("dve", [tbc2], lambda e: e.tensor_copy(
                    out=sxb[:], in_=pb[4].t[:]))
                pb[4].free = tsqb2
                tqz = None
                for ic in range(32):
                    ti = p.op("dve", [tsqb2, tqz], lambda e, ic=ic: e.tensor_tensor(
                        out=it1[:], in0=guh[:, ic, :], in1=sxb[:], op=ALU.mult))
                    tqz = p.op("dve", [ti], lambda e, ic=ic: e.tensor_copy(
                        out=guh[:, ic, :], in_=it1[:]))
                for half_pass in range(2):
                    tmm = None
                    for ic in range(32):
                        for j2 in range(2):
                            tti = half_pass * 2 + j2
                            tt = g * 4 + tti
                            for h in range(2):
                                bk = pb[j2 * 2 + h]
                                tmm = p.op("pe", [tqz, t_wd[ic], bk.free],
                                           lambda e, ic=ic, tt=tt, h=h, bk=bk, g=g, tti=tti: e.matmul(
                                               bk.t[:], guh[:, ic, tti * 128:(tti + 1) * 128],
                                               wdT0[:, ic, h * 512:(h + 1) * 512],
                                               start=(ic == 0), stop=(ic == 31)))
                    for j2 in range(2):
                        tti = half_pass * 2 + j2
                        tt = g * 4 + tti
                        txl = p.dma("pool", [xin_free[tt % 2]], xin[:, tt % 2],
                                    x_src[tt * 128:(tt + 1) * 128, :], stream=f"xp{tt % 2}")
                        tr1 = p.op("dve", [tmm, txl, rt_prev[0]], lambda e, tt=tt, j2=j2, tti=tti: e.scalar_tensor_tensor(
                            out=rt[:, 0:512], in0=pb[j2 * 2].t[:],
                            scalar=stat[:, 48 + tti:49 + tti],
                            in1=xin[:, tt % 2, 0:512], op0=ALU.mult, op1=ALU.add))
                        pb[j2 * 2].free = tr1
                        tr2 = p.op("dve", [tr1], lambda e, tt=tt, j2=j2, tti=tti: e.scalar_tensor_tensor(
                            out=rt[:, 512:1024], in0=pb[j2 * 2 + 1].t[:],
                            scalar=stat[:, 48 + tti:49 + tti],
                            in1=xin[:, tt % 2, 512:1024], op0=ALU.mult, op1=ALU.add))
                        pb[j2 * 2 + 1].free = tr2
                        xin_free[tt % 2] = tr2
                        tst, tln = emit_ln_store(tt, tr2, 0, x_dst, xo_prev2[tt % 2])
                        rt_prev[0] = tln
                        xo_prev2[tt % 2] = tst
            nc.leave_named_scope(f"f0{seq}_main", sc_f1[0], False)

        def finish_gmax(col, slot, deps):
            b_ = pb[5]
            ttr = p.op("pe", deps + [b_.free], lambda e, col=col, b_=b_: e.transpose(
                b_.t[0:1, 0:128], stat[:, col:col + 1], identf[:]))
            trd = p.op("dve", [ttr], lambda e, slot=slot, b_=b_: e.tensor_reduce(
                out=mxrow[0:1, slot:slot + 1], in_=b_.t[0:1, 0:128], axis=AX.X,
                op=ALU.max))
            b_.free = trd
            return trd

        def do_cc(deps):
            tin = p.dma("pool", deps, sc["cc_in"].ap(), mxrow[0:1, 0:8], stream="ccin")
            tcc = p.cc_max([tin], sc["cc_in"].ap(), sc["cc_out"].ap())
            return p.dma("pool", [tcc], mxrow[0:1, 8:16], sc["cc_out"].ap(), stream="ccout")

        def derive_scale(tdep, gslot, gbcol):
            if not isinstance(tdep, list):
                tdep = [tdep]
            t1 = p.op("dve", tdep, lambda e: e.tensor_scalar(
                out=srow[0:1, 40:41], in0=mxrow[0:1, 8 + gslot:9 + gslot],
                scalar1=1.0 / 32767.0, scalar2=1e-12, op0=ALU.mult, op1=ALU.max))
            t2 = p.op("dve", [t1], lambda e: e.reciprocal(
                out=srow[0:1, 41:42], in_=srow[0:1, 40:41]))
            t3 = p.op("pe", [t2, pb[4].free], lambda e: e.matmul(
                pb[4].t[:, 0:2], ones_r_f[:], srow[0:1, 40:42], start=True, stop=True))
            t4 = p.op("dve", [t3], lambda e, gbcol=gbcol: e.tensor_copy(
                out=gb[:, gbcol:gbcol + 2], in_=pb[4].t[:, 0:2]))
            pb[4].free = t4
            return t4

        def emit_ffn1_AB(seq, x_src):
            l1g = sc["l1gT"].ap()
            l1u = sc["l1uT"].ap()
            spu = sc["sp_up"].ap()
            spg = sc["sp_gate"].ap()
            xin_free = [None, None]
            t_cast_prev = [None]
            for tt in range(16):
                buf = xin[:, tt % 2]
                tl = p.dma("sp", [xin_free[tt % 2]], buf,
                           x_src[tt * 128:(tt + 1) * 128, :], stream=f"xin{tt % 2}")
                tq = t_cast_prev
                for h in range(2):
                    ti = p.op("dve", [tl] + (tq if isinstance(tq, list) else [tq]),
                              lambda e, b=buf, h=h: e.tensor_scalar(
                                  out=it1[:], in0=b[:, h * 512:(h + 1) * 512],
                                  scalar1=gb[:, 1:2], scalar2=None, op0=ALU.mult))
                    tq = p.op("dve", [ti], lambda e, h=h: e.tensor_scalar(
                        out=sqb[:, h * 512:(h + 1) * 512], in0=it1[:],
                        scalar1=gb[:, 0:1], scalar2=None, op0=ALU.mult))
                xin_free[tt % 2] = ("dve", p.cnt["dve"])
                for c in range(NC):
                    b_ = pt[c % 2]
                    ttr = p.op("pe", [tq, b_.free], lambda e, c=c, b_=b_: e.transpose(
                        b_.t[:], sqb[:, c * 128:(c + 1) * 128], identb[:]))
                    tev = p.op("dve", [ttr], lambda e, c=c, tt=tt, b_=b_: e.tensor_copy(
                        out=xT[:, c, tt * 128:(tt + 1) * 128], in_=b_.t[:]))
                    b_.free = tev
                t_cast_prev = [p.now("pe")]
            t_xq = p.now("dve")
            tmz = p.op("dve", [t_xq], lambda e: e.memset(stat[:, 52:54], 0.0))
            w_free2 = [None, None]
            sp_free = [None, None]
            tmax_prev = tmz
            for mat, (wsrc, spdst, mcol, wslot) in enumerate(
                    [(l1u, spu, 52, 0), (l1g, spg, 53, 1)]):
                for ic in range(24):
                    tw = p.dma("sp", [w_free2[wslot]], wstr[:, wslot],
                               wsrc[ic].rearrange("c p e -> p c e"), stream=f"f{wslot}")
                    for g in range(4):
                        bk = pb[g]
                        tmm = None
                        for c in range(NC):
                            tmm = p.op("pe", [tw, t_xq, bk.free], lambda e, c=c, g=g, wslot=wslot: e.matmul(
                                bk.t[:], wstr[:, wslot, c, :], xT[:, c, g * 512:(g + 1) * 512],
                                start=(c == 0), stop=(c == NC - 1)))
                        sl = g % 2
                        tcp = p.op("act", [tmm, sp_free[sl]], lambda e, sl=sl, bk=bk: e.activation(
                            out=sp16[:, sl], in_=bk.t[:], func=AF.Identity))
                        bk.free = tcp
                        trd = p.op("dve", [tcp, tmax_prev], lambda e, sl=sl: e.tensor_reduce(
                            out=stat[:, 56:57], in_=sp16[:, sl], axis=AX.X, op=ALU.max,
                            apply_absolute_value=True))
                        tmax_prev = p.op("dve", [trd, tmax_prev], lambda e, mcol=mcol: e.tensor_tensor(
                            out=stat[:, mcol:mcol + 1], in0=stat[:, mcol:mcol + 1],
                            in1=stat[:, 56:57], op=ALU.max))
                        sp_free[sl] = p.dma("pool", [trd], spdst[seq, ic][:, g * 512:(g + 1) * 512],
                                            sp16[:, sl], stream=f"sp{sl}")
                    w_free2[wslot] = p.now("pe")
            return tmax_prev

        def emit_ffn1_C1(seq):
            spg = sc["sp_gate"].ap()
            sp_free = [None, None]
            tmax_prev = p.op("dve", [], lambda e: e.memset(stat[:, 54:55], 0.0))                 if seq == 0 else None
            for ic in range(24):
                for g in range(4):
                    sl = g % 2
                    SP = sp16 if sl == 0 else sp16b
                    IT = it1 if sl == 0 else it2
                    FA = ft1 if sl == 0 else ftA
                    FB = ft2 if sl == 0 else ftB
                    tl = p.dma("sp", [sp_free[sl]], SP[:, 0],
                               spg[seq, ic][:, g * 512:(g + 1) * 512], stream=f"cg{sl}")
                    t1 = p.op("dve", [tl], lambda e, SP=SP, IT=IT: e.tensor_scalar(
                        out=IT[:], in0=SP[:, 0], scalar1=gb[:, 5:6], scalar2=None,
                        op0=ALU.mult))
                    t2 = p.op("dve", [t1], lambda e, IT=IT, FA=FA: e.tensor_scalar(
                        out=FA[:], in0=IT[:], scalar1=gb[:, 4:5], scalar2=None,
                        op0=ALU.mult))
                    pass
                    t3 = p.op("act", [t2], lambda e, FA=FA, FB=FB: e.activation(
                        out=FB[:], in_=FA[:], func=AF.Sigmoid))
                    t4 = p.op("dve", [t3, t2], lambda e, FB=FB, IT=IT: e.tensor_scalar(
                        out=IT[:], in0=FB[:], scalar1=65536.0, scalar2=None,
                        op0=ALU.mult))
                    t5 = p.op("dve", [t4], lambda e, IT=IT, FB=FB: e.tensor_scalar(
                        out=FB[:], in0=IT[:], scalar1=65535.0, scalar2=1.0 / 65536.0,
                        op0=ALU.min, op1=ALU.mult))
                    t6 = p.op("dve", [t5], lambda e, SP=SP, FA=FA, FB=FB: e.tensor_mul(
                        out=SP[:, 1], in0=FA[:], in1=FB[:]))
                    trd = p.op("dve", [t6, tmax_prev], lambda e, SP=SP: e.tensor_reduce(
                        out=stat[:, 56:57], in_=SP[:, 1], axis=AX.X, op=ALU.max,
                        apply_absolute_value=True))
                    tmax_prev = p.op("dve", [trd], lambda e: e.tensor_tensor(
                        out=stat[:, 54:55], in0=stat[:, 54:55], in1=stat[:, 56:57],
                        op=ALU.max))
                    sp_free[sl] = p.dma("pool", [trd], spg[seq, ic][:, g * 512:(g + 1) * 512],
                                        SP[:, 1], stream=f"sp{sl}")
            return tmax_prev

        def emit_ffn1_C2(seq):
            spg = sc["sp_gate"].ap()
            spu = sc["sp_up"].ap()
            sp_free = [None, None]
            spu_free = [None, None]
            tmax_prev = None
            if seq == 0:
                tmax_prev = p.op("dve", [], lambda e: e.memset(stat[:, 55:56], 0.0))
            for ic in range(24):
                for g in range(4):
                    sl = g % 2
                    SP = sp16 if sl == 0 else sp16b
                    IT = it1 if sl == 0 else it2
                    FA = ft1 if sl == 0 else ftA
                    FB = ft2 if sl == 0 else ftB
                    US = spu16[:, sl]
                    tlg = p.dma("sp", [sp_free[sl]], SP[:, 0],
                                spg[seq, ic][:, g * 512:(g + 1) * 512], stream=f"cg{sl}")
                    tlu = p.dma("sp", [spu_free[sl]], US,
                                spu[seq, ic][:, g * 512:(g + 1) * 512], stream=f"su{sl}")
                    t1 = p.op("dve", [tlg], lambda e, SP=SP, IT=IT: e.tensor_scalar(
                        out=IT[:], in0=SP[:, 0], scalar1=gb[:, 7:8], scalar2=None,
                        op0=ALU.mult))
                    t2 = p.op("dve", [t1], lambda e, IT=IT, FA=FA: e.tensor_scalar(
                        out=FA[:], in0=IT[:], scalar1=gb[:, 6:7], scalar2=None,
                        op0=ALU.mult))
                    pass
                    t3 = p.op("dve", [tlu, t2], lambda e, US=US, IT=IT: e.tensor_scalar(
                        out=IT[:], in0=US, scalar1=gb[:, 3:4], scalar2=None,
                        op0=ALU.mult))
                    t4 = p.op("dve", [t3], lambda e, IT=IT, FB=FB: e.tensor_scalar(
                        out=FB[:], in0=IT[:], scalar1=gb[:, 2:3], scalar2=None,
                        op0=ALU.mult))
                    t5 = p.op("dve", [t4], lambda e, SP=SP, FA=FA, FB=FB: e.tensor_mul(
                        out=SP[:, 1], in0=FA[:], in1=FB[:]))
                    trd = p.op("dve", [t5, tmax_prev], lambda e, SP=SP: e.tensor_reduce(
                        out=stat[:, 56:57], in_=SP[:, 1], axis=AX.X, op=ALU.max,
                        apply_absolute_value=True))
                    tmax_prev = p.op("dve", [trd], lambda e: e.tensor_tensor(
                        out=stat[:, 55:56], in0=stat[:, 55:56], in1=stat[:, 56:57],
                        op=ALU.max))
                    spu_free[sl] = t3
                    sp_free[sl] = p.dma("pool", [trd],
                                        spu[seq, ic][:, g * 512:(g + 1) * 512],
                                        SP[:, 1], stream=f"sp{sl}")
            return tmax_prev

        def emit_ffn1_C3(seq, x_src, y_dst, t_wd1):
            spu = sc["sp_up"].ap()
            sp_free = [None, None]
            rt_prev = [None]
            xo_prev2 = [None, None]
            xin_free = [None, None]
            for g in range(4):
                toq_all = []
                toq = None
                for ic in range(24):
                    sl = ic % 2
                    SP = sp16 if sl == 0 else sp16b
                    IT = it1 if sl == 0 else it2
                    tl = p.dma("sp", [sp_free[sl]], SP[:, 0],
                               spu[seq, ic][:, g * 512:(g + 1) * 512], stream=f"cg{sl}")
                    t1 = p.op("dve", [tl, toq], lambda e, SP=SP, IT=IT: e.tensor_scalar(
                        out=IT[:], in0=SP[:, 0], scalar1=gb[:, 9:10], scalar2=None,
                        op0=ALU.mult))
                    sp_free[sl] = t1
                    toq = p.op("dve", [t1], lambda e, ic=ic, IT=IT: e.tensor_scalar(
                        out=oq1[:, ic, :], in0=IT[:], scalar1=gb[:, 8:9], scalar2=None,
                        op0=ALU.mult))
                    toq_all.append(toq)
                for half_pass in range(2):
                    tmm = None
                    for ic in range(24):
                        for j2 in range(2):
                            tti = half_pass * 2 + j2
                            for h in range(2):
                                bk = pb[j2 * 2 + h]
                                tmm = p.op("pe", [toq_all[ic], t_wd1[ic], bk.free],
                                           lambda e, ic=ic, tti=tti, h=h, bk=bk: e.matmul(
                                               bk.t[:], oq1[:, ic, tti * 128:(tti + 1) * 128],
                                               wdT1[:, ic, h * 512:(h + 1) * 512],
                                               start=(ic == 0), stop=(ic == 23)))
                    for j2 in range(2):
                        tti = half_pass * 2 + j2
                        tt = g * 4 + tti
                        txl = p.dma("sp", [xin_free[tt % 2]], xin[:, tt % 2],
                                    x_src[tt * 128:(tt + 1) * 128, :], stream=f"xin{tt % 2}")
                        tr1 = p.op("dve", [tmm, txl, rt_prev[0]], lambda e, tt=tt, j2=j2: e.tensor_add(
                            out=rt[:, 0:512], in0=pb[j2 * 2].t[:], in1=xin[:, tt % 2, 0:512]))
                        pb[j2 * 2].free = tr1
                        tr2 = p.op("dve", [tr1], lambda e, tt=tt, j2=j2: e.tensor_add(
                            out=rt[:, 512:1024], in0=pb[j2 * 2 + 1].t[:],
                            in1=xin[:, tt % 2, 512:1024]))
                        pb[j2 * 2 + 1].free = tr2
                        xin_free[tt % 2] = tr2
                        tst, tln = emit_ln_store(tt, tr2, 0, y_dst, xo_prev2[tt % 2])
                        rt_prev[0] = tln
                        xo_prev2[tt % 2] = tst

        x0 = x_in.ap()
        xc = sc["xcur"].ap()
        t_mx_init = p.op("dve", [], lambda e: e.memset(stat[:, 25:26], 0.0))

        if UPTO >= 1:
            for seq in range(NSEQ):
                with nc.named_scope(f"l0a_s{seq}"):
                    emit_attn(0, seq, x0[seq], xc[seq])
                    p.fence()
                if UPTO >= 2:
                    with nc.named_scope(f"l0f_s{seq}"):
                        emit_ffn0(seq, xc[seq], xc[seq])
                        p.fence()
                if UPTO >= 3 and not SKIP_L1A:
                    with nc.named_scope(f"l1a_s{seq}"):
                        emit_attn(1, seq, xc[seq], xc[seq])
                        p.fence()
        if UPTO >= 4:
            with nc.named_scope("cc1"):
                tg = finish_gmax(25, 0, [p.now("dve")])
                tcc = do_cc([tg])
                derive_scale(tcc, 0, 0)
                p.fence()
            for seq in range(NSEQ):
                with nc.named_scope(f"l1fAB_s{seq}"):
                    emit_ffn1_AB(seq, xc[seq])
                    p.fence()
            with nc.named_scope("cc2"):
                tg1 = finish_gmax(52, 1, [])
                tg2 = finish_gmax(53, 2, [tg1])
                tcc = do_cc([tg1, tg2])
                td1 = derive_scale(tcc, 1, 2)
                derive_scale([tcc, td1], 2, 4)
                p.fence()
            for seq in range(NSEQ):
                with nc.named_scope(f"l1fC1_s{seq}"):
                    emit_ffn1_C1(seq)
                    p.fence()
            with nc.named_scope("cc3"):
                tg3 = finish_gmax(54, 3, [])
                tcc = do_cc([tg3])
                derive_scale(tcc, 3, 6)
                p.fence()
            for seq in range(NSEQ):
                with nc.named_scope(f"l1fC2_s{seq}"):
                    emit_ffn1_C2(seq)
                    p.fence()
            with nc.named_scope("cc4"):
                tg4 = finish_gmax(55, 4, [])
                tcc = do_cc([tg4])
                derive_scale(tcc, 4, 8)
                p.fence()
            t_wd1 = [p.dma("sp", [], wdT1[:, ic, :], sc["l1dT"].ap()[ic])
                     for ic in range(24)]
            for seq in range(NSEQ):
                with nc.named_scope(f"l1fC3_s{seq}"):
                    emit_ffn1_C3(seq, xc[seq], y_out.ap()[seq], t_wd1)
                    p.fence()

        p.fence()
        if UPTO < 4:
            tlast = None
            for seq in range(NSEQ):
                for tt in range(16):
                    tlast = p.dma("sp", [tlast], y_out.ap()[seq][tt * 128:(tt + 1) * 128, :],
                                  xc[seq][tt * 128:(tt + 1) * 128, :])
            p._wait("sp", [tlast])

    return nc


def _host_arrange(inputs):
    f = np.float32
    A = np.ascontiguousarray
    b = np.asarray(inputs["L0_attn_attn_b"], f)
    bqk = np.zeros((128, 16), f)
    for hc in range(8):
        qcols = np.concatenate([np.arange(384 * hc, 384 * hc + 64),
                                np.arange(384 * hc + 192, 384 * hc + 256)])
        bqk[:, hc] = b[qcols]
        bqk[:, 8 + hc] = b[qcols + 64]
    vcols = np.concatenate([np.arange(192 * h + 128, 192 * h + 192) for h in range(16)])
    return {
        "l0_aw": A(np.asarray(inputs["L0_attn_attn_W"], f)),
        "l0_bqk": bqk,
        "l0_bv": A(b[vcols][None, :]),
        "l0_ow": A(np.asarray(inputs["L0_attn_out_W"], f)),
        "l0_ob": A(np.asarray(inputs["L0_attn_out_b"], f)[None, :]),
        "l0_wgT": A(np.asarray(inputs["L0_ffn_w_gate"], f).T),
        "l0_wuT": A(np.asarray(inputs["L0_ffn_w_up"], f).T),
        "l0_wdT": A(np.asarray(inputs["L0_ffn_w_down"], f).T),
        "l1_aw": A(np.asarray(inputs["L1_attn_w_qkv"], f)),
        "l1_ow": A(np.asarray(inputs["L1_attn_w_out"], f)),
        "l1_wg": A(np.asarray(inputs["L1_ffn_w_gate"], f)),
        "l1_wu": A(np.asarray(inputs["L1_ffn_w_up"], f)),
        "l1_wd": A(np.asarray(inputs["L1_ffn_w_down"], f)),
    }


def kernel(**inputs):
    x = np.asarray(inputs["L0_attn_encodings"], dtype=np.float32)
    w = _host_arrange(inputs)
    nc = build_program()
    in_maps = []
    for c in range(NCORES):
        m = dict(w)
        m["x_in"] = np.ascontiguousarray(x[c * NSEQ:(c + 1) * NSEQ])
        in_maps.append(m)
    res = run_bass_kernel_spmd(nc, in_maps, core_ids=list(range(NCORES)))
    return np.concatenate([r["y_out"] for r in res.results], axis=0)


# version 15 (final)
# speedup vs baseline: 1.0909x; 1.0044x over previous
import numpy as np
import ml_dtypes
import concourse.bass as bass
import concourse.mybir as mybir
from concourse.bass_utils import run_bass_kernel_spmd
from contextlib import ExitStack

F32 = mybir.dt.float32
BF16 = mybir.dt.bfloat16
F16 = mybir.dt.float16
I32 = mybir.dt.int32
AF = mybir.ActivationFunctionType
ALU = mybir.AluOpType
AX = mybir.AxisListType
BF = ml_dtypes.bfloat16

D = 1024
S = 2048
NSEQ = 2
NC = 8
NH = 16
HS = 64
NCORES = 8
L0I = 4096
L1I = 3072

UPTO = 4
SKIP_L1A = False
NO_ROPE = False
NO_TABLE = False
ROPE_VAR = 0
DBG = None


class Prog:
    ENGS = ("pe", "act", "dve", "pool", "sp")

    def __init__(self, nc, ctx):
        self.nc = nc
        self._ctx = ctx
        self.eng = dict(pe=nc.tensor, act=nc.scalar, dve=nc.vector,
                        pool=nc.gpsimd, sp=nc.sync)
        self.sems = {}
        self.cnt = {}
        for s in ("pe", "act", "dve", "pool", "dma", "dmag", "cc"):
            self.sems[s] = ctx.enter_context(nc.semaphore("S_" + s))
            self.cnt[s] = 0
        self.know = {e: dict.fromkeys(self.sems, 0) for e in self.ENGS}
        self._warm_tok = None

    def stream_sem(self, name):
        if name not in self.sems:
            self.sems[name] = self._ctx.enter_context(self.nc.semaphore("S_" + name))
            self.cnt[name] = 0
            for e in self.know:
                self.know[e][name] = 0
        return name

    def _wait(self, eng, toks):
        best = {}
        for t in toks:
            if t is None:
                continue
            if len(t) == 3:
                s, v, _ = t
                v = self.cnt[s]
            else:
                s, v = t
            if v > best.get(s, 0):
                best[s] = v
        for s, v in best.items():
            if self.know[eng][s] < v:
                self.eng[eng].wait_ge(self.sems[s], v)
                self.know[eng][s] = v

    def op(self, eng, deps, emit):
        self._wait(eng, deps)
        inst = emit(self.eng[eng])
        self.cnt[eng] += 1
        inst.then_inc(self.sems[eng], 1)
        return (eng, self.cnt[eng])

    def dma(self, queue, deps, out, in_, stream=None):
        self._wait(queue, deps)
        if stream is None:
            sem = "dma" if queue == "sp" else "dmag"
        else:
            sem = self.stream_sem(stream)
        inst = self.eng[queue].dma_start(out=out, in_=in_)
        self.cnt[sem] += 16
        inst.then_inc(self.sems[sem], 16)
        if stream is None:
            return (sem, self.cnt[sem], "B")
        return (sem, self.cnt[sem])

    def cc_max(self, deps, in_ap, out_ap):
        self._wait("pool", deps)
        inst = self.eng["pool"].collective_compute(
            "AllReduce", ALU.max, replica_groups=[list(range(NCORES))],
            ins=[in_ap.opt()], outs=[out_ap.opt()])
        self.cnt["cc"] += 1
        inst.then_inc(self.sems["cc"], 1)
        return ("cc", self.cnt["cc"])

    def now(self, eng):
        return (eng, self.cnt[eng])

    def warm(self, lhs, rhs, tgt):
        self._wait("pe", [self._warm_tok])
        inst = self.eng["pe"].matmul(tgt, lhs, rhs, start=True, stop=True)
        self.cnt["pe"] += 1
        inst.then_inc(self.sems["pe"], 1)
        self._warm_tok = ("pe", self.cnt["pe"])

    def fence(self):
        toks = [(s, self.cnt[s]) for s in self.cnt]
        for e in self.ENGS:
            self._wait(e, toks)


class Bank:
    def __init__(self, t):
        self.t = t
        self.free = None


def make_consts(nc):
    c = {}
    c["ident_b"] = nc.inline_tensor(np.eye(128).astype(BF), name="c_ident_b")
    c["ident_f"] = nc.inline_tensor(np.eye(128, dtype=np.float32), name="c_ident_f")
    tri = (np.arange(128)[:, None] <= np.arange(128)[None, :])
    c["tri_b"] = nc.inline_tensor(tri.astype(BF), name="c_tri_b")
    c["ones_row_b"] = nc.inline_tensor(np.ones((1, 128), BF), name="c_ones_row_b")
    c["ones_row_f"] = nc.inline_tensor(np.ones((1, 128), np.float32), name="c_ones_row_f")
    c["ones_col_b"] = nc.inline_tensor(np.ones((128, 1), BF), name="c_ones_col_b")
    c["ones_col_f"] = nc.inline_tensor(np.ones((128, 1), np.float32), name="c_ones_col_f")
    inv_freq = 1.0 / 10000.0 ** (np.arange(0, 32, 2, dtype=np.float64) / 32.0)
    ang = np.arange(S, dtype=np.float64)[None, :] * np.repeat(inv_freq, 2)[:, None]
    cosf = np.ones((128, S), np.float64)
    sinf = np.zeros((128, S), np.float64)
    for base in (0, 64):
        cosf[base:base + 32] = np.cos(ang)
        sinf[base:base + 32] = np.sin(ang)
    c["cosf"] = nc.inline_tensor(cosf.astype(BF), name="c_cosf")
    c["sinf"] = nc.inline_tensor(sinf.astype(BF), name="c_sinf")
    R = np.zeros((128, 128), np.float64)
    for base in (0, 64):
        for i in range(16):
            R[base + 2 * i, base + 2 * i + 1] = -1.0
            R[base + 2 * i + 1, base + 2 * i] = 1.0
    c["rotT_b"] = nc.inline_tensor(np.ascontiguousarray(R.T).astype(BF), name="c_rotT_b")
    return c


def build_program():
    nc = bass.Bass("TRN2", target_bir_lowering=False, debug=False,
                   dynamic_dma_scratch_size=16384)

    x_in = nc.dram_tensor("x_in", [NSEQ, S, D], F32, kind="ExternalInput")
    y_out = nc.dram_tensor("y_out", [NSEQ, S, D], F32, kind="ExternalOutput")
    dbg_kind = "ExternalOutput" if DBG else "Internal"
    dbg16 = nc.dram_tensor("dbg16", [128, 16384], BF16, kind=dbg_kind)
    dbgf = nc.dram_tensor("dbgf", [128, 16, D], F32, kind=dbg_kind)
    ew = {}
    for nm, shape in [
        ("l0_aw", [D, 3 * D]), ("l0_bqk", [128, 16]), ("l0_bv", [1, D]),
        ("l0_ow", [D, D]), ("l0_ob", [1, D]),
        ("l0_wgT", [D, L0I]), ("l0_wuT", [D, L0I]), ("l0_wdT", [L0I, D]),
        ("l1_aw", [D, 3 * D]), ("l1_ow", [D, D]),
        ("l1_wg", [L1I, D]), ("l1_wu", [L1I, D]), ("l1_wd", [D, L1I]),
    ]:
        ew[nm] = nc.dram_tensor(nm, shape, F32, kind="ExternalInput")

    sc = {}
    sc["xcur"] = nc.dram_tensor("xcur", [NSEQ, S, D], F32, kind="Internal")
    for L in (0, 1):
        sc[f"wqk{L}"] = nc.dram_tensor(f"wqk{L}", [16, 8, 128, 128], BF16, kind="Internal")
        sc[f"wv{L}"] = nc.dram_tensor(f"wv{L}", [8, 128, D], BF16, kind="Internal")
        sc[f"wo{L}"] = nc.dram_tensor(f"wo{L}", [8, 128, D], BF16, kind="Internal")
    sc["l0gT"] = nc.dram_tensor("l0gT", [32, 8, 128, 128], BF16, kind="Internal")
    sc["l0uT"] = nc.dram_tensor("l0uT", [32, 8, 128, 128], BF16, kind="Internal")
    sc["l0dT"] = nc.dram_tensor("l0dT", [32, 128, D], BF16, kind="Internal")
    sc["l1gT"] = nc.dram_tensor("l1gT", [24, 8, 128, 128], BF16, kind="Internal")
    sc["l1uT"] = nc.dram_tensor("l1uT", [24, 8, 128, 128], BF16, kind="Internal")
    sc["l1dT"] = nc.dram_tensor("l1dT", [24, 128, D], BF16, kind="Internal")
    sc["sp_up"] = nc.dram_tensor("sp_up", [NSEQ, 24, 128, S], F16, kind="Internal")
    sc["sp_gate"] = nc.dram_tensor("sp_gate", [NSEQ, 24, 128, S], F16, kind="Internal")
    sc["cc_in"] = nc.dram_tensor("cc_in", [1, 8], F32, kind="Internal")
    sc["cc_out"] = nc.dram_tensor("cc_out", [1, 8], F32, kind="Internal")

    consts = make_consts(nc)

    ctx = ExitStack()
    with ctx:
        p = Prog(nc, ctx)
        st = ctx.enter_context

        identb = st(nc.sbuf_tensor("identb", [128, 128], BF16))
        trib = st(nc.sbuf_tensor("trib", [128, 128], BF16))
        identf = st(nc.sbuf_tensor("identf", [128, 128], F32))
        ones_r_b = st(nc.sbuf_tensor("ones_r_b", [1, 128], BF16))
        ones_r_f = st(nc.sbuf_tensor("ones_r_f", [1, 128], F32))
        ones_c_b = st(nc.sbuf_tensor("ones_c_b", [128, 1], BF16))
        ones_c_f = st(nc.sbuf_tensor("ones_c_f", [128, 1], F32))
        rotTb = st(nc.sbuf_tensor("rotTb", [128, 128], BF16))
        epsc = st(nc.sbuf_tensor("epsc", [128, 1], F32))
        bqk = st(nc.sbuf_tensor("bqk", [128, 16], F32))
        bvrow = st(nc.sbuf_tensor("bvrow", [1, D], BF16))
        obrow = st(nc.sbuf_tensor("obrow", [1, D], BF16))
        stat = st(nc.sbuf_tensor("stat", [128, 64], F32))
        srow = st(nc.sbuf_tensor("srow", [1, 64], F32))
        b_qk2 = st(nc.sbuf_tensor("b_qk2", [1, 80], F32))
        bbias = st(nc.sbuf_tensor("bbias", [128, 8], F32))
        mxrow = st(nc.sbuf_tensor("mxrow", [1, 16], F32))
        gb = st(nc.sbuf_tensor("gb", [128, 12], F32))

        xT = st(nc.sbuf_tensor("xT", [128, NC, S], BF16))
        kT = st(nc.sbuf_tensor("kT", [128, NC, S], BF16))
        qq = st(nc.sbuf_tensor("qq", [128, NC, S], BF16))
        vT = st(nc.sbuf_tensor("vT", [128, 16, NH, HS + 1], BF16))
        woB = st(nc.sbuf_tensor("woB", [128, NC, D], BF16))
        wstr = st(nc.sbuf_tensor("wstr", [128, 2, 8, 128], BF16))
        eT = st(nc.sbuf_tensor("eT", [128, 3, 512], BF16))
        aT = st(nc.sbuf_tensor("aT", [128, NC, 512], BF16))
        _aT_addr = nc.lookup_mloc(aT).addr
        cosf = nc.alloc_sbuf_tensor_at("cosf", [128, S], BF16, offset=_aT_addr)
        sinf = nc.alloc_sbuf_tensor_at("sinf", [128, S], BF16, offset=_aT_addr + 4096)
        it2 = nc.alloc_sbuf_tensor_at("it2", [128, 512], I32, offset=_aT_addr)
        ftA = nc.alloc_sbuf_tensor_at("ftA", [128, 512], F32, offset=_aT_addr + 2048)
        ftB = nc.alloc_sbuf_tensor_at("ftB", [128, 512], F32, offset=_aT_addr + 4096)
        sp16b = nc.alloc_sbuf_tensor_at("sp16b", [128, 2, 512], F16, offset=_aT_addr + 6144)
        sqb = st(nc.sbuf_tensor("sqb", [128, 1536], BF16))
        xin = st(nc.sbuf_tensor("xin", [128, 2, D], F32))
        rt = st(nc.sbuf_tensor("rt", [128, D], F32))
        sqf = st(nc.sbuf_tensor("sqf", [128, D], F32))
        xot = st(nc.sbuf_tensor("xot", [128, 2, D], F32))
        ft1 = st(nc.sbuf_tensor("ft1", [128, 512], F32))
        ft2 = st(nc.sbuf_tensor("ft2", [128, 512], F32))
        ft3 = st(nc.sbuf_tensor("ft3", [128, 512], F32))
        it1 = st(nc.sbuf_tensor("it1", [128, 512], I32))
        mtile = st(nc.sbuf_tensor("mtile", [128, 512], F32))
        sxr = st(nc.sbuf_tensor("sxr", [1, 512], F32))
        sp16 = nc.alloc_sbuf_tensor_at("sp16", [128, 2, 512], F16,
                                       offset=nc.lookup_mloc(mtile).addr)
        spu16 = nc.alloc_sbuf_tensor_at("spu16", [128, 2, 512], F16,
                                        offset=nc.lookup_mloc(ft3).addr)
        sxb = st(nc.sbuf_tensor("sxb", [128, 512], F32))
        ft4 = nc.alloc_sbuf_tensor_at("ft4", [128, 512], F32,
                                      offset=nc.lookup_mloc(mtile).addr)
        kT_addr = nc.lookup_mloc(kT).addr
        qq_addr = nc.lookup_mloc(qq).addr
        assert qq_addr == kT_addr + 32768, (kT_addr, qq_addr)
        wf32 = nc.alloc_sbuf_tensor_at("wf32", [128, 2, 3072], F32, offset=kT_addr)
        wb16 = nc.alloc_sbuf_tensor_at("wb16", [128, 3072], BF16, offset=qq_addr)
        wdT0 = nc.alloc_sbuf_tensor_at("wdT0", [128, 32, D], BF16, offset=kT_addr)
        wdT1 = nc.alloc_sbuf_tensor_at("wdT1", [128, 24, D], BF16, offset=kT_addr)
        vT_addr = nc.lookup_mloc(vT).addr
        guh = nc.alloc_sbuf_tensor_at("guh", [128, 32, 512], BF16, offset=vT_addr)
        oq1 = nc.alloc_sbuf_tensor_at("oq1", [128, 24, 512], BF16, offset=vT_addr)

        pb = [Bank(st(nc.psum_tensor(f"pb{i}", [128, 512], F32))) for i in range(6)]
        pt = [Bank(st(nc.psum_tensor(f"pt{i}", [128, 128], BF16))) for i in range(2)]

        for tile, key in ((identb, "ident_b"), (identf, "ident_f"), (trib, "tri_b"),
                          (ones_r_b, "ones_row_b"), (ones_r_f, "ones_row_f"),
                          (ones_c_b, "ones_col_b"), (ones_c_f, "ones_col_f"),
                          (rotTb, "rotT_b")):
            p.dma("sp", [], tile[:], consts[key].ap())
        t_eps = p.op("dve", [], lambda e: e.memset(epsc[:], 1e-5))
        p.op("dve", [], lambda e: e.memset(mxrow[:], 0.0))
        p.dma("sp", [], bqk[:], ew["l0_bqk"].ap())
        t_bv = p.dma("sp", [], rt[0:1, 0:D], ew["l0_bv"].ap())
        t_ob = p.dma("sp", [], sqf[0:1, 0:D], ew["l0_ob"].ap())
        t_bvb = p.op("dve", [t_bv], lambda e: e.tensor_copy(out=bvrow[:], in_=rt[0:1, 0:D]))
        t_obb = p.op("dve", [t_ob], lambda e: e.tensor_copy(out=obrow[:], in_=sqf[0:1, 0:D]))

        def prep_attn(L):
            p.fence()
            aw = ew[f"l{L}_aw"].ap()
            ow = ew[f"l{L}_ow"].ap()
            buf_free = [None, None]
            t_cast_prev = None
            for c in range(NC):
                buf = wf32[:, c % 2]
                tl = p.dma("sp" if c % 2 == 0 else "pool", [buf_free[c % 2]], buf,
                           aw[c * 128:(c + 1) * 128, :],
                           stream=f"wf{c % 2}" if c % 2 == 0 else f"wfp{c % 2}")
                tc = p.op("dve", [tl] + (t_cast_prev if isinstance(t_cast_prev, list) else [t_cast_prev]), lambda e, b=buf: e.tensor_copy(
                    out=wb16[:], in_=b))
                buf_free[c % 2] = tc
                if L == 0:
                    wv5 = wb16[:].rearrange("p (hc h part e) -> p hc h part e",
                                            hc=8, h=2, part=3)
                    qsl = lambda hc: wv5[:, hc, :, 0, :]
                    ksl = lambda hc: wv5[:, hc, :, 1, :]
                    vall = wv5[:, :, :, 2, :]
                else:
                    wv4 = wb16[:].rearrange("p (hc part e) -> p hc part e",
                                            hc=8, part=3)
                    qsl = lambda hc: wv4[:, hc, 0, :]
                    ksl = lambda hc: wv4[:, hc, 1, :]
                    vall = wv4[:, :, 2, :]
                oqv = sc[f"wqk{L}"].ap()[0:8, c].rearrange("hc p e -> p hc e")
                okv = sc[f"wqk{L}"].ap()[8:16, c].rearrange("hc p e -> p hc e")
                if L == 0:
                    for h in range(2):
                        p.dma("sp", [tc], oqv[:, :, h * 64:(h + 1) * 64], wv5[:, :, h, 0, :])
                        p.dma("sp", [tc], okv[:, :, h * 64:(h + 1) * 64], wv5[:, :, h, 1, :])
                else:
                    p.dma("sp", [tc], oqv, wv4[:, :, 0, :])
                    p.dma("sp", [tc], okv, wv4[:, :, 1, :])
                tstores = [p.dma("sp", [tc], sc[f"wv{L}"].ap()[c], vall)]
                t_cast_prev = tstores[-1]
            for c in range(NC):
                buf = wf32[:, c % 2]
                tl = p.dma("sp" if c % 2 == 0 else "pool", [buf_free[c % 2], t_cast_prev],
                           buf[:, 0:D],
                           ow[c * 128:(c + 1) * 128, :],
                           stream=f"wf{c % 2}" if c % 2 == 0 else f"wfp{c % 2}")
                tc = p.op("dve", [tl] + (t_cast_prev if isinstance(t_cast_prev, list) else [t_cast_prev]), lambda e, b=buf: e.tensor_copy(
                    out=wb16[:, 0:D], in_=b[:, 0:D]))
                t_cast_prev = p.dma("sp", [tc], sc[f"wo{L}"].ap()[c], wb16[:, 0:D])
                buf_free[c % 2] = tc

        def prep_tern(nm, src):
            p.fence()
            W = ew[src].ap()
            rows, cols = W.tensor.shape
            nch = rows // 128
            SLAB = min(cols, 2048)
            nsl = cols // SLAB
            acc = stat[:, 0:1]
            tz = p.op("dve", [p.now("dve")], lambda e: e.memset(acc, 0.0))
            buf_free = [None, None]
            tacc = tz
            for c in range(nch):
                for sl in range(nsl):
                    i = (c * nsl + sl) % 2
                    buf = wf32[:, i, 0:SLAB]
                    tl = p.dma("sp" if i == 0 else "pool", [buf_free[i]], buf,
                               W[c * 128:(c + 1) * 128, sl * SLAB:(sl + 1) * SLAB],
                               stream=f"wf{i}" if i == 0 else f"wfp{i}")
                    tr = p.op("dve", [tl, tacc], lambda e, b=buf: e.tensor_reduce(
                        out=stat[:, 1:2], in_=b, axis=AX.X, op=ALU.add,
                        apply_absolute_value=True))
                    tacc = p.op("dve", [tr], lambda e: e.tensor_add(
                        out=acc, in0=acc, in1=stat[:, 1:2]))
                    buf_free[i] = tr
            tsum = p.op("pe", [tacc, pb[0].free], lambda e: e.matmul(
                pb[0].t[0:1, 0:1], ones_c_f[:], acc, start=True, stop=True))
            tm = p.op("dve", [tsum], lambda e: e.tensor_scalar(
                out=srow[0:1, 0:1], in0=pb[0].t[0:1, 0:1], scalar1=1.0 / float(rows * cols),
                scalar2=1e-5, op0=ALU.mult, op1=ALU.max))
            pb[0].free = tm
            trc = p.op("dve", [tm], lambda e: e.reciprocal(
                out=srow[0:1, 1:2], in_=srow[0:1, 0:1]))
            tb = p.op("pe", [trc, pb[1].free], lambda e: e.matmul(
                pb[1].t[:, 0:2], ones_r_f[:], srow[0:1, 0:2], start=True, stop=True))
            sidx = {"l0gT": 2, "l0uT": 4, "l0dT": 6}[nm]
            tbc = p.op("dve", [tb], lambda e: e.tensor_copy(
                out=stat[:, sidx:sidx + 2], in_=pb[1].t[:, 0:2]))
            pb[1].free = tbc
            wb_free = None
            clip_last = [None, None]
            for c in range(nch):
                for sl in range(nsl):
                    i = (c * nsl + sl) % 2
                    buf = wf32[:, i, 0:SLAB]
                    tl = p.dma("sp" if i == 0 else "pool", [buf_free[i]], buf,
                               W[c * 128:(c + 1) * 128, sl * SLAB:(sl + 1) * SLAB],
                               stream=f"wf{i}" if i == 0 else f"wfp{i}")
                    ti_last = None
                    for j in range(SLAB // 512):
                        cs = slice(sl * SLAB + j * 512, sl * SLAB + (j + 1) * 512)
                        jj = (j % 2)
                        IT = it1 if jj == 0 else it2
                        ti = p.op("dve", [tl, tbc, clip_last[jj]], lambda e, b=buf, j=j, IT=IT: e.tensor_scalar(
                            out=IT[:], in0=b[:, j * 512:(j + 1) * 512],
                            scalar1=stat[:, sidx + 1:sidx + 2],
                            scalar2=None, op0=ALU.mult))
                        ti_last = ti
                        clip_last[jj] = p.op("dve", [ti, wb_free], lambda e, cs=cs, IT=IT: e.tensor_scalar(
                            out=wb16[:, cs.start - sl * SLAB:cs.stop - sl * SLAB], in0=IT[:],
                            scalar1=1.0, scalar2=-1.0,
                            op0=ALU.min, op1=ALU.max))
                    tclip = clip_last[0 if (SLAB // 512) % 2 else 1]
                    tclip = p.now("dve")
                    if nm == "l0dT":
                        tst = p.dma("sp", [tclip], sc[nm].ap()[c][:, sl * SLAB:(sl + 1) * SLAB],
                                    wb16[:, 0:SLAB])
                    else:
                        nic = SLAB // 128
                        odst = sc[nm].ap()[sl * nic:(sl + 1) * nic, c].rearrange(
                            "ic p e -> p ic e")
                        tst = p.dma("sp", [tclip], odst,
                                    wb16[:, 0:SLAB].rearrange("p (ic e) -> p ic e", e=128))
                    buf_free[i] = ti_last
                    wb_free = tst

        def prep_lpbq(nm, src):
            p.fence()
            W = ew[src].ap()
            rows, cols = W.tensor.shape
            nch = rows // 128
            ngrp = cols // 32
            buf_free = [None, None]
            sq_slot = [None, None, None, None]
            tlast = None
            for c in range(nch):
                buf = wf32[:, c % 2][:, 0:cols]
                tl = p.dma("sp" if c % 2 == 0 else "pool", [buf_free[c % 2]], buf,
                           W[c * 128:(c + 1) * 128, :],
                           stream=f"wf{c % 2}" if c % 2 == 0 else f"wfp{c % 2}")
                bview = buf.rearrange("p (g k) -> p g k", k=32)
                ts1 = p.op("dve", [tl, tlast], lambda e, b=bview, n=ngrp: e.tensor_reduce(
                    out=ft1[:, 0:n], in_=b, axis=AX.X, op=ALU.max,
                    apply_absolute_value=True))
                ts2 = p.op("dve", [ts1], lambda e, n=ngrp: e.tensor_scalar(
                    out=ft2[:, 0:n], in0=ft1[:, 0:n], scalar1=1.0 / 7.0,
                    scalar2=1e-12, op0=ALU.mult, op1=ALU.max))
                ts3 = p.op("dve", [ts2], lambda e, n=ngrp: e.reciprocal(
                    out=ft3[:, 0:n], in_=ft2[:, 0:n]))
                tq = None
                for j in range(ngrp // 16):
                    gsl = slice(j * 16, (j + 1) * 16)
                    csl = slice(j * 512, (j + 1) * 512)
                    wv = bview[:, gsl, :]
                    rsb = bass.broadcast_tensor_aps(ft3[:, gsl][:, :, None], wv)[0]
                    sb_ = bass.broadcast_tensor_aps(ft2[:, gsl][:, :, None], wv)[0]
                    ti = p.op("dve", [ts3, tq], lambda e, wv=wv, rsb=rsb: e.tensor_tensor(
                        out=it1[:].rearrange("p (g k) -> p g k", k=32),
                        in0=wv, in1=rsb, op=ALU.mult))
                    tc1 = p.op("dve", [ti], lambda e: e.tensor_scalar(
                        out=ft4[:], in0=it1[:], scalar1=7.0, scalar2=-8.0,
                        op0=ALU.min, op1=ALU.max))
                    tq = p.op("dve", [tc1], lambda e, sb_=sb_, csl=csl: e.tensor_tensor(
                        out=wb16[:, csl].rearrange("p (g k) -> p g k", k=32),
                        in0=ft4[:].rearrange("p (g k) -> p g k", k=32),
                        in1=sb_, op=ALU.mult))
                njc = cols // 128
                gsz = min(12, njc)
                tev_grp = []
                for jc in range(njc):
                    b_ = pt[jc % 2]
                    ttr = p.op("pe", [tq, b_.free], lambda e, jc=jc, b_=b_: e.transpose(
                        b_.t[:], wb16[:, jc * 128:(jc + 1) * 128], identb[:]))
                    slot = jc % gsz
                    tev = p.op("dve", [ttr, sq_slot[0] if slot == 0 else None],
                               lambda e, b_=b_, slot=slot: e.tensor_copy(
                                   out=sqb[:, slot * 128:slot * 128 + 128], in_=b_.t[:]))
                    b_.free = tev
                    tev_grp.append(tev)
                    if slot == gsz - 1:
                        j0 = jc - gsz + 1
                        if nm == "l1dT":
                            odst = sc[nm].ap()[j0:j0 + gsz][:, :, c * 128:(c + 1) * 128]                                 .rearrange("t p e -> p t e")
                        else:
                            odst = sc[nm].ap()[c, j0:j0 + gsz].rearrange("t p e -> p t e")
                        tst = p.dma("sp", tev_grp, odst,
                                    sqb[:, 0:gsz * 128].rearrange("p (t e) -> p t e", e=128))
                        sq_slot[0] = tst
                        tev_grp = []
                tlast = tq
                buf_free[c % 2] = tq

        if UPTO >= 0:
            with nc.named_scope("prep_attn0"):
                prep_attn(0)
                p.fence()
        if UPTO >= 2:
            with nc.named_scope("prep_tern"):
                prep_tern("l0gT", "l0_wgT")
                prep_tern("l0uT", "l0_wuT")
                prep_tern("l0dT", "l0_wdT")
                p.fence()
        if UPTO >= 3:
            with nc.named_scope("prep_attn1"):
                prep_attn(1)
                p.fence()
        if UPTO >= 4:
            with nc.named_scope("prep_lpbq"):
                prep_lpbq("l1gT", "l1_wg")
                prep_lpbq("l1uT", "l1_wu")
                prep_lpbq("l1dT", "l1_wd")
                p.fence()

        def emit_ln_store(tt, tdep, L, xdst, prev_tok):
            ts1 = p.op("dve", [tdep], lambda e: e.reduce_sum(
                out=stat[:, 16:17], in_=rt[:], axis=AX.X))
            ts2 = p.op("act", [tdep], lambda e: e.activation(
                out=sqf[:], in_=rt[:], func=AF.Square, accum_out=stat[:, 17:18]))
            tm = p.op("dve", [ts1], lambda e: e.tensor_scalar_mul(
                out=stat[:, 18:19], in0=stat[:, 16:17], scalar1=1.0 / D))
            tv1 = p.op("dve", [ts2], lambda e: e.tensor_scalar_mul(
                out=stat[:, 19:20], in0=stat[:, 17:18], scalar1=1.0 / D))
            tm2 = p.op("dve", [tm], lambda e: e.tensor_mul(
                out=stat[:, 20:21], in0=stat[:, 18:19], in1=stat[:, 18:19]))
            tvv = p.op("dve", [tv1, tm2], lambda e: e.tensor_sub(
                out=stat[:, 21:22], in0=stat[:, 19:20], in1=stat[:, 20:21]))
            tsd = p.op("act", [tvv, t_eps], lambda e: e.activation(
                out=stat[:, 22:23], in_=stat[:, 21:22], func=AF.Sqrt, bias=epsc[:]))
            trs = p.op("dve", [tsd], lambda e: e.reciprocal(
                out=stat[:, 23:24], in_=stat[:, 22:23]))
            buf = xot[:, tt % 2]
            tx = p.op("dve", [trs, prev_tok], lambda e, b=buf: e.tensor_scalar(
                out=b, in0=rt[:], scalar1=stat[:, 18:19], scalar2=stat[:, 23:24],
                op0=ALU.subtract, op1=ALU.mult))
            tst = p.dma("sp", [tx], xdst[tt * 128:(tt + 1) * 128, :], buf, stream=f"xo{tt % 2}")
            tmax = None
            if L == 1:
                tmx = p.op("dve", [tx], lambda e, b=buf: e.tensor_reduce(
                    out=stat[:, 24:25], in_=b, axis=AX.X, op=ALU.max,
                    apply_absolute_value=True))
                tmax = p.op("dve", [tmx], lambda e: e.tensor_tensor(
                    out=stat[:, 25:26], in0=stat[:, 25:26], in1=stat[:, 24:25],
                    op=ALU.max))
            return tst, (tx if tmax is None else tmax)

        def emit_attn(L, seq, x_src, x_dst):
            wqk = sc[f"wqk{L}"].ap()
            wv = sc[f"wv{L}"].ap()
            wo = sc[f"wo{L}"].ap()

            if L == 1:
                if NO_TABLE:
                    p.op("dve", [], lambda e: e.memset(cosf[:], 1.0))
                    p.op("dve", [], lambda e: e.memset(sinf[:], 0.0))
                else:
                    p.dma("sp", [], cosf[:], consts["cosf"].ap())
                    p.dma("sp", [], sinf[:], consts["sinf"].ap())
            sc_a1 = nc.enter_named_scope(f"at{L}{seq}_a1", False)
            xin_free = [None, None]
            t_cast_prev = [None]
            for tt in range(16):
                buf = xin[:, tt % 2]
                tl = p.dma("sp", [xin_free[tt % 2]], buf, x_src[tt * 128:(tt + 1) * 128, :], stream=f"xin{tt % 2}")
                tc = p.op("dve", [tl] + (t_cast_prev if isinstance(t_cast_prev, list) else [t_cast_prev]), lambda e, b=buf: e.tensor_copy(
                    out=wb16[:, 0:D], in_=b))
                xin_free[tt % 2] = tc
                td_cast = None
                if DBG == ("cast", seq):
                    td_cast = p.dma("sp", [tc], dbg16.ap().rearrange(
                        "p (t d) -> p t d", t=16)[:, tt], wb16[:, 0:D], stream="dbgc")
                for c in range(NC):
                    b_ = pt[c % 2]
                    ttr = p.op("pe", [tc, b_.free], lambda e, c=c, b_=b_: e.transpose(
                        b_.t[:], wb16[:, c * 128:(c + 1) * 128], identb[:]))
                    tev = p.op("dve", [ttr], lambda e, c=c, tt=tt, b_=b_: e.tensor_copy(
                        out=xT[:, c, tt * 128:(tt + 1) * 128], in_=b_.t[:]))
                    b_.free = tev
                t_cast_prev = [p.now("pe"), td_cast]
            nc.leave_named_scope(f"at{L}{seq}_a1", sc_a1[0], False)
            sc_a2 = nc.enter_named_scope(f"at{L}{seq}_a2", False)
            t_xT = p.now("dve")
            if DBG == ("xT", seq):
                p.dma("sp", [t_xT], dbg16.ap().rearrange("p (c s) -> p c s", c=NC), xT[:])

            tv_w = [p.dma("sp", [p.now("dve")], woB[:, c, :], wv[c]) for c in range(NC)]
            bank = pb[4]
            for tt in range(16):
                vsl = vT[:, tt]
                tms = p.op("dve", [bank.free], lambda e, v=vsl: e.memset(v, 1.0))
                for half in range(2):
                    tmm = None
                    for c in range(NC):
                        tmm = p.op("pe", [t_xT, tv_w[c], bank.free], lambda e, c=c, tt=tt, h=half, bk=bank: e.matmul(
                            bk.t[:], xT[:, c, tt * 128:(tt + 1) * 128],
                            woB[:, c, h * 512:(h + 1) * 512],
                            start=(c == 0), stop=(c == NC - 1 and L != 0)))
                    if L == 0:
                        tmm = p.op("pe", [tmm, t_bvb], lambda e, h=half, bk=bank: e.matmul(
                            bk.t[:], ones_r_b[:], bvrow[0:1, h * 512:(h + 1) * 512],
                            start=False, stop=True))
                    vout = vsl[:, half * 8:(half + 1) * 8, 0:64]
                    tev = p.op("dve", [tmm, tms], lambda e, vo=vout, bk=bank: e.tensor_copy(
                        out=vo, in_=bk.t[:].rearrange("p (h e) -> p h e", h=8)))
                    bank.free = tev
            t_vdone = p.now("dve")
            nc.leave_named_scope(f"at{L}{seq}_a2", sc_a2[0], False)
            sc_a3 = nc.enter_named_scope(f"at{L}{seq}_a3", False)

            w_free = [None, None]
            rope_war = [None, None]
            norm_war = [None, None]
            norm_war2 = [None]
            rope_ev = [None]

            def qk_prod(which, dst):
                hbase = 0 if which == "q" else 8
                for hc in range(8):
                    wbuf = wstr[:, hc % 2]
                    tw = p.dma("sp", [w_free[hc % 2]], wbuf[:, 0:8, :],
                               wqk[hbase + hc].rearrange("c p e -> p c e"),
                               stream=f"ws{hc % 2}")
                    for c in range(NC):
                        for g in range(4):
                            bk = pb[g]
                            p.op("pe", [tw, t_xT, bk.free], lambda e, c=c, g=g, w=wbuf, bk=bk: e.matmul(
                                bk.t[:], w[:, c, :], xT[:, c, g * 512:(g + 1) * 512],
                                start=(c == 0), stop=(c == NC - 1)))
                    for g in range(4):
                        bk = pb[g]
                        tmm = p.now("pe")
                        if L == 0 or NO_ROPE:
                            bias_ap = bqk[:, hbase + hc:hbase + hc + 1]
                            if L == 0:
                                tev = p.op("act", [tmm], lambda e, g=g, hc=hc, b=bias_ap, bk=bk: e.activation(
                                    out=dst[:, hc, g * 512:(g + 1) * 512], in_=bk.t[:],
                                    func=AF.Identity, bias=b))
                            else:
                                tev = p.op("act", [tmm], lambda e, g=g, hc=hc, bk=bk: e.activation(
                                    out=dst[:, hc, g * 512:(g + 1) * 512], in_=bk.t[:],
                                    func=AF.Identity))
                            bk.free = tev
                        else:
                            tst_ = p.op("act", [tmm, rope_war[g % 2]], lambda e, g=g, bk=bk: e.activation(
                                out=sqb[:, (g % 2) * 512:(g % 2 + 1) * 512], in_=bk.t[:],
                                func=AF.Identity))
                            if ROPE_VAR == 1:
                                tt1 = p.op("dve", [tmm, rope_ev[0]], lambda e, g=g, bk=bk: e.tensor_mul(
                                    out=ft1[:], in0=bk.t[:],
                                    in1=cosf[:, g * 512:(g + 1) * 512]))
                                bk.free = tt1
                                tev = p.op("dve", [tt1], lambda e, g=g, hc=hc: e.tensor_copy(
                                    out=dst[:, hc, g * 512:(g + 1) * 512], in_=ft1[:]))
                                rope_ev[0] = tev
                                continue_flag = True
                            else:
                                continue_flag = False
                            if continue_flag:
                                toks_done = None
                                continue
                            tt1 = p.op("dve", [tmm, tst_, rope_ev[0]], lambda e, g=g, bk=bk: e.tensor_mul(
                                out=ft1[:], in0=bk.t[:],
                                in1=cosf[:, g * 512:(g + 1) * 512]))
                            bk.free = tt1
                            rbk = pb[4]
                            trot = p.op("pe", [tst_, rbk.free], lambda e, g=g, rbk=rbk: e.matmul(
                                rbk.t[:], rotTb[:], sqb[:, (g % 2) * 512:(g % 2 + 1) * 512],
                                start=True, stop=True))
                            rope_war[g % 2] = trot
                            tt2 = p.op("dve", [trot, rope_ev[0]], lambda e, g=g, rbk=rbk: e.tensor_mul(
                                out=ft2[:], in0=rbk.t[:],
                                in1=sinf[:, g * 512:(g + 1) * 512]))
                            rbk.free = tt2
                            tev = p.op("dve", [tt1, tt2], lambda e, g=g, hc=hc: e.tensor_add(
                                out=dst[:, hc, g * 512:(g + 1) * 512],
                                in0=ft1[:], in1=ft2[:]))
                            rope_ev[0] = tev
                    w_free[hc % 2] = p.now("pe")

            qk_prod("k", kT)
            t_k = [p.now("dve"), p.now("act"), p.now("pe"), p.now("pool")]
            qk_prod("q", qq)
            t_qk = [p.now("dve"), p.now("act"), p.now("pe"), p.now("pool")]

            nc.leave_named_scope(f"at{L}{seq}_a3", sc_a3[0], False)
            sc_b0 = nc.enter_named_scope(f"at{L}{seq}_b0", False)
            tzb = p.op("dve", t_k, lambda e: e.memset(b_qk2[0:1, 0:80], 0.0))
            sq_par = [None, None, None]
            red_prev = [tzb, tzb, tzb]
            unit = 0
            for which, src_t, base in (("k", kT, 8), ("q", qq, 0)):
                tdep = t_k if which == "k" else t_qk
                for c in range(NC):
                    for g in range(4):
                        par = unit % 3
                        unit += 1
                        tsq = p.op("act", tdep + [sq_par[par]], lambda e, c=c, g=g, s=src_t, par=par: e.activation(
                            out=sqb[:, par * 512:(par + 1) * 512],
                            in_=s[:, c, g * 512:(g + 1) * 512], func=AF.Square))
                        bk = pb[3 + par]
                        tmm = p.op("pe", [tsq, bk.free], lambda e, par=par, bk=bk: e.matmul(
                            bk.t[0:1, :], ones_c_b[:], sqb[:, par * 512:(par + 1) * 512],
                            start=True, stop=True))
                        sq_par[par] = tmm
                        idx = (base + c) * 4 + g
                        trd = p.op("dve", [tmm, red_prev[par]], lambda e, idx=idx, bk=bk: e.tensor_reduce(
                            out=b_qk2[0:1, idx:idx + 1], in_=bk.t[0:1, :], axis=AX.X,
                            op=ALU.max))
                        bk.free = trd
                        red_prev[par] = trd
            tprev = p.op("dve", red_prev, lambda e: e.tensor_reduce(
                out=b_qk2[0:1, 64:80], in_=b_qk2[0:1, 0:64].rearrange("a (c g) -> a c g", g=4),
                axis=AX.X, op=ALU.max))
            tb1 = p.op("dve", [tprev], lambda e: e.tensor_mul(
                out=srow[0:1, 8:16], in0=b_qk2[0:1, 64:72], in1=b_qk2[0:1, 72:80]))
            tb2 = p.op("act", [tb1], lambda e: e.activation(
                out=srow[0:1, 16:24], in_=srow[0:1, 8:16], func=AF.Sqrt))
            tb3 = p.op("dve", [tb2], lambda e: e.tensor_scalar_mul(
                out=srow[0:1, 24:32], in0=srow[0:1, 16:24], scalar1=-0.125))
            tb4 = p.op("pe", [tb3, pb[5].free], lambda e: e.matmul(
                pb[5].t[:, 0:8], ones_r_f[:], srow[0:1, 24:32], start=True, stop=True))
            tb5 = p.op("dve", [tb4], lambda e: e.tensor_copy(out=bbias[:], in_=pb[5].t[:, 0:8]))
            pb[5].free = tb5

            nc.leave_named_scope(f"at{L}{seq}_b0", sc_b0[0], False)
            sc_b2 = nc.enter_named_scope(f"at{L}{seq}_b2", False)
            t_wo = [p.dma("sp", [t_vdone], woB[:, c, :], wo[c]) for c in range(NC)]

            xo_prev = [None, None]
            rt_prev = None
            tst_last = None
            for qt in range(4):
                for hd in range(NH):
                    hc2 = hd // 2
                    rb = (hd % 2) * 64
                    kmax = 4 * (qt + 1)
                    abk = pb[2 + (hd % 2)]
                    t_pv = None
                    texs = [None] * (kmax + 2)
                    for kc in range(kmax + 2):
                      if kc < kmax:
                        o = max(0, kc * 128 - qt * 512)
                        n = 512 - o
                        sbk = pb[[0, 1, 4][kc % 3]]
                        tsc = p.op("pe", [sbk.free, tb5], lambda e, kc=kc, hc2=hc2, rb=rb, qt=qt, o=o, n=n, sbk=sbk: e.matmul(
                            sbk.t[:, o:o + n],
                            kT[rb:rb + 64, hc2, kc * 128:(kc + 1) * 128],
                            qq[rb:rb + 64, hc2, qt * 512 + o:qt * 512 + o + n],
                            start=True, stop=True))
                        slot = kc % 3
                        tex = p.op("act", [tsc, t_pv], lambda e, sbk=sbk, o=o, n=n, slot=slot, hc2=hc2: e.activation(
                            out=eT[:, slot, o:o + n], in_=sbk.t[:, o:o + n], func=AF.Exp,
                            bias=bbias[:, hc2:hc2 + 1], scale=0.125))
                        sbk.free = tex
                        if kc >= 4 * qt:
                            tex = p.op("pool", [tex], lambda e, slot=slot, o=o: e.tensor_mul(
                                out=eT[:, slot, o:o + 128], in0=eT[:, slot, o:o + 128],
                                in1=trib[:]))
                        texs[kc] = tex
                      if kc >= 2:
                        kp = kc - 2
                        o = max(0, kp * 128 - qt * 512)
                        n = 512 - o
                        slot = kp % 3
                        t_pv = p.op("pe", [texs[kp], t_vdone] + ([t_pv] if t_pv else [abk.free]),
                                    lambda e, kp=kp, hd=hd, o=o, n=n, slot=slot, kmax=kmax, abk=abk: e.matmul(
                                        abk.t[0:65, o:o + n], vT[:, kp, hd, 0:65],
                                        eT[:, slot, o:o + n],
                                        start=(kp == 0), stop=(kp == kmax - 1)))
                    trb = p.op("dve", [t_pv, norm_war[hd % 2]], lambda e, hd=hd, abk=abk: e.tensor_copy(
                        out=sqb[0:1, (hd % 2) * 512:(hd % 2) * 512 + 512],
                        in_=abk.t[64:65, 0:512]))
                    tbc = p.op("pe", [trb, pb[5].free], lambda e, hd=hd: e.matmul(
                        pb[5].t[0:64, :], ones_r_b[0:1, 0:64],
                        sqb[0:1, (hd % 2) * 512:(hd % 2) * 512 + 512],
                        start=True, stop=True))
                    trc = p.op("dve", [tbc, norm_war2[0]], lambda e: e.reciprocal(
                        out=ft1[0:64, :], in_=pb[5].t[0:64, :]))
                    pb[5].free = trc
                    tn = p.op("dve", [trc, t_pv], lambda e, hc2=hc2, rb=rb, abk=abk: e.tensor_tensor(
                        out=aT[rb:rb + 64, hc2, :], in0=abk.t[0:64, 0:512],
                        in1=ft1[0:64, :], op=ALU.mult))
                    abk.free = tn
                    norm_war2[0] = tn
                    norm_war[hd % 2] = tn
                t_aT = p.now("dve")
                if DBG == ("aT", seq):
                    p.dma("sp", [t_aT], dbg16.ap().rearrange("p (c q s) -> p c q s", c=NC, q=4)[:, :, qt], aT[:])
                for tti in range(4):
                    tt = qt * 4 + tti
                    txl = p.dma("pool", [xin_free[tt % 2]], xin[:, tt % 2],
                                x_src[tt * 128:(tt + 1) * 128, :], stream=f"xp{tt % 2}")
                    tmm_h = []
                    for half in range(2):
                        bk = pb[0 + half]
                        tmm = None
                        for c in range(NC):
                            tmm = p.op("pe", [t_aT, t_wo[c], bk.free], lambda e, c=c, tti=tti, h=half, bk=bk: e.matmul(
                                bk.t[:], aT[:, c, tti * 128:(tti + 1) * 128],
                                woB[:, c, h * 512:(h + 1) * 512],
                                start=(c == 0), stop=(c == NC - 1 and L != 0)))
                        if L == 0:
                            tmm = p.op("pe", [tmm, t_obb], lambda e, h=half, bk=bk: e.matmul(
                                bk.t[:], ones_r_b[:], obrow[0:1, h * 512:(h + 1) * 512],
                                start=False, stop=True))
                        tmm_h.append(tmm)
                    tr1 = p.op("dve", [tmm_h[0], txl, rt_prev], lambda e, tt=tt: e.tensor_add(
                        out=rt[:, 0:512], in0=xin[:, tt % 2, 0:512], in1=pb[0].t[:]))
                    pb[0].free = tr1
                    tr2 = p.op("dve", [tmm_h[1], tr1], lambda e, tt=tt: e.tensor_add(
                        out=rt[:, 512:1024], in0=xin[:, tt % 2, 512:1024], in1=pb[1].t[:]))
                    pb[1].free = tr2
                    xin_free[tt % 2] = tr2
                    if DBG == ("rt", seq):
                        tr2 = p.dma("sp", [tr2], dbgf.ap()[:, tt], rt[:])
                    tst_last, tln = emit_ln_store(tt, tr2, L, x_dst,
                                                  xo_prev[tt % 2])
                    rt_prev = tln
                    xo_prev[tt % 2] = tst_last
            nc.leave_named_scope(f"at{L}{seq}_b2", sc_b2[0], False)
            return tst_last

        def emit_ffn0(seq, x_src, x_dst):
            l0g = sc["l0gT"].ap()
            l0u = sc["l0uT"].ap()
            l0d = sc["l0dT"].ap()
            rt_prev = [None]
            xo_prev2 = [None, None]

            sc_f0 = nc.enter_named_scope(f"f0{seq}_xq", False)
            xin_free = [None, None]
            t_cast_prev = [None]
            for tt in range(16):
                buf = xin[:, tt % 2]
                tl = p.dma("sp", [xin_free[tt % 2]], buf,
                           x_src[tt * 128:(tt + 1) * 128, :], stream=f"xin{tt % 2}")
                tam = p.op("dve", [tl], lambda e, b=buf: e.tensor_reduce(
                    out=stat[:, 26:27], in_=b, axis=AX.X, op=ALU.max,
                    apply_absolute_value=True))
                tcl = p.op("dve", [tam], lambda e: e.tensor_scalar_max(
                    out=stat[:, 27:28], in0=stat[:, 26:27], scalar1=1e-5))
                trc = p.op("dve", [tcl], lambda e: e.reciprocal(
                    out=stat[:, 28:29], in_=stat[:, 27:28]))
                tsc_ = p.op("dve", [trc], lambda e: e.tensor_scalar_mul(
                    out=stat[:, 29:30], in0=stat[:, 28:29], scalar1=127.0))
                tsx = p.op("dve", [tcl], lambda e, tt=tt: e.tensor_scalar_mul(
                    out=stat[:, 32 + tt:33 + tt], in0=stat[:, 27:28],
                    scalar1=1.0 / 127.0))
                tq = t_cast_prev
                for h in range(2):
                    ti = p.op("dve", [tsc_, tl] + (tq if isinstance(tq, list) else [tq]),
                              lambda e, b=buf, h=h: e.tensor_scalar(
                                  out=it1[:], in0=b[:, h * 512:(h + 1) * 512],
                                  scalar1=stat[:, 29:30], scalar2=None, op0=ALU.mult))
                    tq = p.op("dve", [ti], lambda e, h=h: e.tensor_copy(
                        out=sqb[:, h * 512:(h + 1) * 512], in_=it1[:]))
                xin_free[tt % 2] = ("dve", p.cnt["dve"])
                for c in range(NC):
                    b_ = pt[c % 2]
                    ttr = p.op("pe", [tq, b_.free], lambda e, c=c, b_=b_: e.transpose(
                        b_.t[:], sqb[:, c * 128:(c + 1) * 128], identb[:]))
                    tev = p.op("dve", [ttr], lambda e, c=c, tt=tt, b_=b_: e.tensor_copy(
                        out=xT[:, c, tt * 128:(tt + 1) * 128], in_=b_.t[:]))
                    b_.free = tev
                t_cast_prev = [p.now("pe")]
            t_xq = p.now("dve")

            nc.leave_named_scope(f"f0{seq}_xq", sc_f0[0], False)
            sc_f1 = nc.enter_named_scope(f"f0{seq}_main", False)
            t_wd = []
            for ic in range(32):
                t_wd.append(p.dma("sp", [t_xq], wdT0[:, ic, :], l0d[ic]))

            mean_g = stat[:, 2:3]
            mean_u = stat[:, 4:5]
            mean_d = stat[:, 6:7]
            for g in range(4):
                tsxr = []
                for j in range(4):
                    tt = g * 4 + j
                    b_ = pb[5]
                    ttr = p.op("pe", [t_xq, b_.free], lambda e, tt=tt, b_=b_: e.transpose(
                        b_.t[0:1, 0:128], stat[:, 32 + tt:33 + tt], identf[:]))
                    tcp = p.op("dve", [ttr], lambda e, j=j, b_=b_: e.tensor_copy(
                        out=sxr[0:1, j * 128:(j + 1) * 128], in_=b_.t[0:1, 0:128]))
                    b_.free = tcp
                    tsxr.append(tcp)
                tbc = p.op("pe", [pb[4].free] + tsxr, lambda e: e.matmul(
                    pb[4].t[:], ones_r_f[:], sxr[0:1, :], start=True, stop=True))
                tsxb = p.op("dve", [tbc], lambda e: e.tensor_copy(
                    out=sxb[:], in_=pb[4].t[:]))
                pb[4].free = tsxb
                tmz = p.op("dve", [pb[0].free, pb[1].free], lambda e: e.memset(mtile[:], 0.0))
                w_free2 = [None, None]
                tgu_prev = None
                for ic in range(32):
                    twu = p.dma("sp", [w_free2[0]], wstr[:, 0],
                                l0u[ic].rearrange("c p e -> p c e"), stream="fu")
                    twg = p.dma("sp", [w_free2[1]], wstr[:, 1],
                                l0g[ic].rearrange("c p e -> p c e"), stream="fg")
                    for c in range(NC):
                        p.op("pe", [twu, t_xq, pb[0].free if c == 0 else None],
                             lambda e, c=c, g=g: e.matmul(
                                 pb[0].t[:], wstr[:, 0, c, :], xT[:, c, g * 512:(g + 1) * 512],
                                 start=(c == 0), stop=(c == NC - 1)))
                    tmmu = p.now("pe")
                    w_free2[0] = tmmu
                    for c in range(NC):
                        p.op("pe", [twg, t_xq, pb[1].free if c == 0 else None],
                             lambda e, c=c, g=g: e.matmul(
                                 pb[1].t[:], wstr[:, 1, c, :], xT[:, c, g * 512:(g + 1) * 512],
                                 start=(c == 0), stop=(c == NC - 1)))
                    tmmg = p.now("pe")
                    w_free2[1] = tmmg
                    tup = p.op("dve", [tmmu, tsxb], lambda e: e.scalar_tensor_tensor(
                        out=ft1[:], in0=pb[0].t[:], scalar=mean_u, in1=sxb[:],
                        op0=ALU.mult, op1=ALU.mult))
                    pb[0].free = tup
                    tga = p.op("dve", [tmmg, tsxb], lambda e: e.scalar_tensor_tensor(
                        out=ft2[:], in0=pb[1].t[:], scalar=mean_g, in1=sxb[:],
                        op0=ALU.mult, op1=ALU.mult))
                    pb[1].free = tga
                    tsi = p.op("act", [tga], lambda e: e.activation(
                        out=ft3[:], in_=ft2[:], func=AF.Silu))
                    tgu = p.op("dve", [tsi, tup, tgu_prev], lambda e, ic=ic: e.tensor_mul(
                        out=guh[:, ic, :], in0=ft1[:], in1=ft3[:]))
                    tm1 = p.op("dve", [tgu, tmz], lambda e, ic=ic: e.scalar_tensor_tensor(
                        out=mtile[:], in0=guh[:, ic, :], scalar=1.0, in1=mtile[:],
                        op0=ALU.mult, op1=ALU.max))
                    tgu_prev = p.op("dve", [tm1], lambda e, ic=ic: e.scalar_tensor_tensor(
                        out=mtile[:], in0=guh[:, ic, :], scalar=-1.0, in1=mtile[:],
                        op0=ALU.mult, op1=ALU.max))
                tscq = []
                for j in range(4):
                    tt = g * 4 + j
                    b_ = pb[5]
                    ttr = p.op("pe", [tgu_prev, b_.free], lambda e, j=j, b_=b_: e.transpose(
                        b_.t[0:128, 0:128], mtile[:, j * 128:(j + 1) * 128], identf[:]))
                    trd = p.op("dve", [ttr], lambda e, b_=b_: e.tensor_reduce(
                        out=stat[:, 26:27], in_=b_.t[0:128, 0:128], axis=AX.X, op=ALU.max))
                    b_.free = trd
                    tcl = p.op("dve", [trd], lambda e: e.tensor_scalar_max(
                        out=stat[:, 27:28], in0=stat[:, 26:27], scalar1=1e-5))
                    tds = p.op("dve", [tcl], lambda e, j=j: e.tensor_scalar_mul(
                        out=stat[:, 48 + j:49 + j], in0=stat[:, 27:28],
                        scalar1=1.0 / 127.0))
                    tds2 = p.op("dve", [tds], lambda e, j=j: e.tensor_mul(
                        out=stat[:, 48 + j:49 + j], in0=stat[:, 48 + j:49 + j],
                        in1=mean_d))
                    trc = p.op("dve", [tcl], lambda e: e.reciprocal(
                        out=stat[:, 28:29], in_=stat[:, 27:28]))
                    tsc_ = p.op("dve", [trc], lambda e: e.tensor_scalar_mul(
                        out=stat[:, 29:30], in0=stat[:, 28:29], scalar1=127.0))
                    b2 = pb[5]
                    ttr2 = p.op("pe", [tsc_, b2.free], lambda e, b2=b2: e.transpose(
                        b2.t[0:1, 0:128], stat[:, 29:30], identf[:]))
                    tcp = p.op("dve", [ttr2, tds2], lambda e, j=j, b2=b2: e.tensor_copy(
                        out=sxr[0:1, j * 128:(j + 1) * 128], in_=b2.t[0:1, 0:128]))
                    b2.free = tcp
                    tscq.append(tcp)
                tbc2 = p.op("pe", [pb[4].free] + tscq, lambda e: e.matmul(
                    pb[4].t[:], ones_r_f[:], sxr[0:1, :], start=True, stop=True))
                tsqb2 = p.op("dve", [tbc2], lambda e: e.tensor_copy(
                    out=sxb[:], in_=pb[4].t[:]))
                pb[4].free = tsqb2
                tqz = None
                for ic in range(32):
                    ti = p.op("dve", [tsqb2, tqz], lambda e, ic=ic: e.tensor_tensor(
                        out=it1[:], in0=guh[:, ic, :], in1=sxb[:], op=ALU.mult))
                    tqz = p.op("dve", [ti], lambda e, ic=ic: e.tensor_copy(
                        out=guh[:, ic, :], in_=it1[:]))
                for half_pass in range(2):
                    tmm = None
                    for ic in range(32):
                        for j2 in range(2):
                            tti = half_pass * 2 + j2
                            tt = g * 4 + tti
                            for h in range(2):
                                bk = pb[j2 * 2 + h]
                                tmm = p.op("pe", [tqz, t_wd[ic], bk.free],
                                           lambda e, ic=ic, tt=tt, h=h, bk=bk, g=g, tti=tti: e.matmul(
                                               bk.t[:], guh[:, ic, tti * 128:(tti + 1) * 128],
                                               wdT0[:, ic, h * 512:(h + 1) * 512],
                                               start=(ic == 0), stop=(ic == 31)))
                    for j2 in range(2):
                        tti = half_pass * 2 + j2
                        tt = g * 4 + tti
                        txl = p.dma("pool", [xin_free[tt % 2]], xin[:, tt % 2],
                                    x_src[tt * 128:(tt + 1) * 128, :], stream=f"xp{tt % 2}")
                        tr1 = p.op("dve", [tmm, txl, rt_prev[0]], lambda e, tt=tt, j2=j2, tti=tti: e.scalar_tensor_tensor(
                            out=rt[:, 0:512], in0=pb[j2 * 2].t[:],
                            scalar=stat[:, 48 + tti:49 + tti],
                            in1=xin[:, tt % 2, 0:512], op0=ALU.mult, op1=ALU.add))
                        pb[j2 * 2].free = tr1
                        tr2 = p.op("dve", [tr1], lambda e, tt=tt, j2=j2, tti=tti: e.scalar_tensor_tensor(
                            out=rt[:, 512:1024], in0=pb[j2 * 2 + 1].t[:],
                            scalar=stat[:, 48 + tti:49 + tti],
                            in1=xin[:, tt % 2, 512:1024], op0=ALU.mult, op1=ALU.add))
                        pb[j2 * 2 + 1].free = tr2
                        xin_free[tt % 2] = tr2
                        tst, tln = emit_ln_store(tt, tr2, 0, x_dst, xo_prev2[tt % 2])
                        rt_prev[0] = tln
                        xo_prev2[tt % 2] = tst
            nc.leave_named_scope(f"f0{seq}_main", sc_f1[0], False)

        def finish_gmax(col, slot, deps):
            b_ = pb[5]
            ttr = p.op("pe", deps + [b_.free], lambda e, col=col, b_=b_: e.transpose(
                b_.t[0:1, 0:128], stat[:, col:col + 1], identf[:]))
            trd = p.op("dve", [ttr], lambda e, slot=slot, b_=b_: e.tensor_reduce(
                out=mxrow[0:1, slot:slot + 1], in_=b_.t[0:1, 0:128], axis=AX.X,
                op=ALU.max))
            b_.free = trd
            return trd

        def do_cc(deps):
            tin = p.dma("pool", deps, sc["cc_in"].ap(), mxrow[0:1, 0:8], stream="ccin")
            tcc = p.cc_max([tin], sc["cc_in"].ap(), sc["cc_out"].ap())
            return p.dma("pool", [tcc], mxrow[0:1, 8:16], sc["cc_out"].ap(), stream="ccout")

        def derive_scale(tdep, gslot, gbcol):
            if not isinstance(tdep, list):
                tdep = [tdep]
            t1 = p.op("dve", tdep, lambda e: e.tensor_scalar(
                out=srow[0:1, 40:41], in0=mxrow[0:1, 8 + gslot:9 + gslot],
                scalar1=1.0 / 32767.0, scalar2=1e-12, op0=ALU.mult, op1=ALU.max))
            t2 = p.op("dve", [t1], lambda e: e.reciprocal(
                out=srow[0:1, 41:42], in_=srow[0:1, 40:41]))
            t3 = p.op("pe", [t2, pb[4].free], lambda e: e.matmul(
                pb[4].t[:, 0:2], ones_r_f[:], srow[0:1, 40:42], start=True, stop=True))
            t4 = p.op("dve", [t3], lambda e, gbcol=gbcol: e.tensor_copy(
                out=gb[:, gbcol:gbcol + 2], in_=pb[4].t[:, 0:2]))
            pb[4].free = t4
            return t4

        def emit_ffn1_AB(seq, x_src):
            l1g = sc["l1gT"].ap()
            l1u = sc["l1uT"].ap()
            spu = sc["sp_up"].ap()
            spg = sc["sp_gate"].ap()
            xin_free = [None, None]
            t_cast_prev = [None]
            for tt in range(16):
                buf = xin[:, tt % 2]
                tl = p.dma("sp", [xin_free[tt % 2]], buf,
                           x_src[tt * 128:(tt + 1) * 128, :], stream=f"xin{tt % 2}")
                tq = t_cast_prev
                for h in range(2):
                    ti = p.op("dve", [tl] + (tq if isinstance(tq, list) else [tq]),
                              lambda e, b=buf, h=h: e.tensor_scalar(
                                  out=it1[:], in0=b[:, h * 512:(h + 1) * 512],
                                  scalar1=gb[:, 1:2], scalar2=None, op0=ALU.mult))
                    tq = p.op("dve", [ti], lambda e, h=h: e.tensor_scalar(
                        out=sqb[:, h * 512:(h + 1) * 512], in0=it1[:],
                        scalar1=gb[:, 0:1], scalar2=None, op0=ALU.mult))
                xin_free[tt % 2] = ("dve", p.cnt["dve"])
                for c in range(NC):
                    b_ = pt[c % 2]
                    ttr = p.op("pe", [tq, b_.free], lambda e, c=c, b_=b_: e.transpose(
                        b_.t[:], sqb[:, c * 128:(c + 1) * 128], identb[:]))
                    tev = p.op("dve", [ttr], lambda e, c=c, tt=tt, b_=b_: e.tensor_copy(
                        out=xT[:, c, tt * 128:(tt + 1) * 128], in_=b_.t[:]))
                    b_.free = tev
                t_cast_prev = [p.now("pe")]
            t_xq = p.now("dve")
            tmz = p.op("dve", [t_xq], lambda e: e.memset(stat[:, 52:54], 0.0))
            w_free2 = [None, None]
            sp_free = [None, None]
            tmax_prev = tmz
            for mat, (wsrc, spdst, mcol, wslot) in enumerate(
                    [(l1u, spu, 52, 0), (l1g, spg, 53, 1)]):
                for ic in range(24):
                    tw = p.dma("sp", [w_free2[wslot]], wstr[:, wslot],
                               wsrc[ic].rearrange("c p e -> p c e"), stream=f"f{wslot}")
                    for g in range(4):
                        bk = pb[g]
                        tmm = None
                        for c in range(NC):
                            tmm = p.op("pe", [tw, t_xq, bk.free], lambda e, c=c, g=g, wslot=wslot: e.matmul(
                                bk.t[:], wstr[:, wslot, c, :], xT[:, c, g * 512:(g + 1) * 512],
                                start=(c == 0), stop=(c == NC - 1)))
                        sl = g % 2
                        tcp = p.op("act", [tmm, sp_free[sl]], lambda e, sl=sl, bk=bk: e.activation(
                            out=sp16[:, sl], in_=bk.t[:], func=AF.Identity))
                        bk.free = tcp
                        trd = p.op("dve", [tcp, tmax_prev], lambda e, sl=sl: e.tensor_reduce(
                            out=stat[:, 56:57], in_=sp16[:, sl], axis=AX.X, op=ALU.max,
                            apply_absolute_value=True))
                        tmax_prev = p.op("dve", [trd, tmax_prev], lambda e, mcol=mcol: e.tensor_tensor(
                            out=stat[:, mcol:mcol + 1], in0=stat[:, mcol:mcol + 1],
                            in1=stat[:, 56:57], op=ALU.max))
                        sp_free[sl] = p.dma("pool", [trd], spdst[seq, ic][:, g * 512:(g + 1) * 512],
                                            sp16[:, sl], stream=f"sp{sl}")
                    w_free2[wslot] = p.now("pe")
            return tmax_prev

        def emit_ffn1_C1(seq):
            spg = sc["sp_gate"].ap()
            sp_free = [None, None]
            tmax_prev = p.op("dve", [], lambda e: e.memset(stat[:, 54:55], 0.0))                 if seq == 0 else None
            for ic in range(24):
                for g in range(4):
                    sl = g % 2
                    SP = sp16 if sl == 0 else sp16b
                    IT = it1 if sl == 0 else it2
                    FA = ft1 if sl == 0 else ftA
                    FB = ft2 if sl == 0 else ftB
                    tl = p.dma("sp", [sp_free[sl]], SP[:, 0],
                               spg[seq, ic][:, g * 512:(g + 1) * 512], stream=f"cg{sl}")
                    t1 = p.op("dve", [tl], lambda e, SP=SP, IT=IT: e.tensor_scalar(
                        out=IT[:], in0=SP[:, 0], scalar1=gb[:, 5:6], scalar2=None,
                        op0=ALU.mult))
                    t2 = p.op("dve", [t1], lambda e, IT=IT, FA=FA: e.tensor_scalar(
                        out=FA[:], in0=IT[:], scalar1=gb[:, 4:5], scalar2=None,
                        op0=ALU.mult))
                    pass
                    t3 = p.op("act", [t2], lambda e, FA=FA, FB=FB: e.activation(
                        out=FB[:], in_=FA[:], func=AF.Sigmoid))
                    t4 = p.op("dve", [t3, t2], lambda e, FB=FB, IT=IT: e.tensor_scalar(
                        out=IT[:], in0=FB[:], scalar1=65536.0, scalar2=None,
                        op0=ALU.mult))
                    t5 = p.op("dve", [t4], lambda e, IT=IT, FB=FB: e.tensor_scalar(
                        out=FB[:], in0=IT[:], scalar1=65535.0, scalar2=1.0 / 65536.0,
                        op0=ALU.min, op1=ALU.mult))
                    t6 = p.op("dve", [t5], lambda e, SP=SP, FA=FA, FB=FB: e.tensor_mul(
                        out=SP[:, 1], in0=FA[:], in1=FB[:]))
                    trd = p.op("dve", [t6, tmax_prev], lambda e, SP=SP: e.tensor_reduce(
                        out=stat[:, 56:57], in_=SP[:, 1], axis=AX.X, op=ALU.max,
                        apply_absolute_value=True))
                    tmax_prev = p.op("dve", [trd], lambda e: e.tensor_tensor(
                        out=stat[:, 54:55], in0=stat[:, 54:55], in1=stat[:, 56:57],
                        op=ALU.max))
                    sp_free[sl] = p.dma("pool", [trd], spg[seq, ic][:, g * 512:(g + 1) * 512],
                                        SP[:, 1], stream=f"sp{sl}")
            return tmax_prev

        def emit_ffn1_C2(seq):
            spg = sc["sp_gate"].ap()
            spu = sc["sp_up"].ap()
            sp_free = [None, None]
            spu_free = [None, None]
            tmax_prev = None
            if seq == 0:
                tmax_prev = p.op("dve", [], lambda e: e.memset(stat[:, 55:56], 0.0))
            for ic in range(24):
                for g in range(4):
                    sl = g % 2
                    SP = sp16 if sl == 0 else sp16b
                    IT = it1 if sl == 0 else it2
                    FA = ft1 if sl == 0 else ftA
                    FB = ft2 if sl == 0 else ftB
                    US = spu16[:, sl]
                    tlg = p.dma("sp", [sp_free[sl]], SP[:, 0],
                                spg[seq, ic][:, g * 512:(g + 1) * 512], stream=f"cg{sl}")
                    tlu = p.dma("sp", [spu_free[sl]], US,
                                spu[seq, ic][:, g * 512:(g + 1) * 512], stream=f"su{sl}")
                    t1 = p.op("dve", [tlg], lambda e, SP=SP, IT=IT: e.tensor_scalar(
                        out=IT[:], in0=SP[:, 0], scalar1=gb[:, 7:8], scalar2=None,
                        op0=ALU.mult))
                    t2 = p.op("dve", [t1], lambda e, IT=IT, FA=FA: e.tensor_scalar(
                        out=FA[:], in0=IT[:], scalar1=gb[:, 6:7], scalar2=None,
                        op0=ALU.mult))
                    pass
                    t3 = p.op("dve", [tlu, t2], lambda e, US=US, IT=IT: e.tensor_scalar(
                        out=IT[:], in0=US, scalar1=gb[:, 3:4], scalar2=None,
                        op0=ALU.mult))
                    t4 = p.op("dve", [t3], lambda e, IT=IT, FB=FB: e.tensor_scalar(
                        out=FB[:], in0=IT[:], scalar1=gb[:, 2:3], scalar2=None,
                        op0=ALU.mult))
                    t5 = p.op("dve", [t4], lambda e, SP=SP, FA=FA, FB=FB: e.tensor_mul(
                        out=SP[:, 1], in0=FA[:], in1=FB[:]))
                    trd = p.op("dve", [t5, tmax_prev], lambda e, SP=SP: e.tensor_reduce(
                        out=stat[:, 56:57], in_=SP[:, 1], axis=AX.X, op=ALU.max,
                        apply_absolute_value=True))
                    tmax_prev = p.op("dve", [trd], lambda e: e.tensor_tensor(
                        out=stat[:, 55:56], in0=stat[:, 55:56], in1=stat[:, 56:57],
                        op=ALU.max))
                    spu_free[sl] = t3
                    sp_free[sl] = p.dma("pool", [trd],
                                        spu[seq, ic][:, g * 512:(g + 1) * 512],
                                        SP[:, 1], stream=f"sp{sl}")
            return tmax_prev

        def emit_ffn1_C3(seq, x_src, y_dst, t_wd1):
            spu = sc["sp_up"].ap()
            sp_free = [None, None]
            rt_prev = [None]
            xo_prev2 = [None, None]
            xin_free = [None, None]
            for g in range(4):
                toq_all = []
                toq = None
                for ic in range(24):
                    sl = ic % 2
                    SP = sp16 if sl == 0 else sp16b
                    IT = it1 if sl == 0 else it2
                    tl = p.dma("sp", [sp_free[sl]], SP[:, 0],
                               spu[seq, ic][:, g * 512:(g + 1) * 512], stream=f"cg{sl}")
                    t1 = p.op("dve", [tl, toq], lambda e, SP=SP, IT=IT: e.tensor_scalar(
                        out=IT[:], in0=SP[:, 0], scalar1=gb[:, 9:10], scalar2=None,
                        op0=ALU.mult))
                    sp_free[sl] = t1
                    toq = p.op("dve", [t1], lambda e, ic=ic, IT=IT: e.tensor_scalar(
                        out=oq1[:, ic, :], in0=IT[:], scalar1=gb[:, 8:9], scalar2=None,
                        op0=ALU.mult))
                    toq_all.append(toq)
                for half_pass in range(2):
                    tmm = None
                    for ic in range(24):
                        for j2 in range(2):
                            tti = half_pass * 2 + j2
                            for h in range(2):
                                bk = pb[j2 * 2 + h]
                                tmm = p.op("pe", [toq_all[ic], t_wd1[ic], bk.free],
                                           lambda e, ic=ic, tti=tti, h=h, bk=bk: e.matmul(
                                               bk.t[:], oq1[:, ic, tti * 128:(tti + 1) * 128],
                                               wdT1[:, ic, h * 512:(h + 1) * 512],
                                               start=(ic == 0), stop=(ic == 23)))
                    for j2 in range(2):
                        tti = half_pass * 2 + j2
                        tt = g * 4 + tti
                        txl = p.dma("sp", [xin_free[tt % 2]], xin[:, tt % 2],
                                    x_src[tt * 128:(tt + 1) * 128, :], stream=f"xin{tt % 2}")
                        tr1 = p.op("dve", [tmm, txl, rt_prev[0]], lambda e, tt=tt, j2=j2: e.tensor_add(
                            out=rt[:, 0:512], in0=pb[j2 * 2].t[:], in1=xin[:, tt % 2, 0:512]))
                        pb[j2 * 2].free = tr1
                        tr2 = p.op("dve", [tr1], lambda e, tt=tt, j2=j2: e.tensor_add(
                            out=rt[:, 512:1024], in0=pb[j2 * 2 + 1].t[:],
                            in1=xin[:, tt % 2, 512:1024]))
                        pb[j2 * 2 + 1].free = tr2
                        xin_free[tt % 2] = tr2
                        tst, tln = emit_ln_store(tt, tr2, 0, y_dst, xo_prev2[tt % 2])
                        rt_prev[0] = tln
                        xo_prev2[tt % 2] = tst

        x0 = x_in.ap()
        xc = sc["xcur"].ap()
        t_mx_init = p.op("dve", [], lambda e: e.memset(stat[:, 25:26], 0.0))

        if UPTO >= 1:
            for seq in range(NSEQ):
                with nc.named_scope(f"l0a_s{seq}"):
                    emit_attn(0, seq, x0[seq], xc[seq])
                    p.fence()
                if UPTO >= 2:
                    with nc.named_scope(f"l0f_s{seq}"):
                        emit_ffn0(seq, xc[seq], xc[seq])
                        p.fence()
                if UPTO >= 3 and not SKIP_L1A:
                    with nc.named_scope(f"l1a_s{seq}"):
                        emit_attn(1, seq, xc[seq], xc[seq])
                        p.fence()
        if UPTO >= 4:
            with nc.named_scope("cc1"):
                tg = finish_gmax(25, 0, [p.now("dve")])
                tcc = do_cc([tg])
                derive_scale(tcc, 0, 0)
                p.fence()
            for seq in range(NSEQ):
                with nc.named_scope(f"l1fAB_s{seq}"):
                    emit_ffn1_AB(seq, xc[seq])
                    p.fence()
            with nc.named_scope("cc2"):
                tg1 = finish_gmax(52, 1, [])
                tg2 = finish_gmax(53, 2, [tg1])
                tcc = do_cc([tg1, tg2])
                td1 = derive_scale(tcc, 1, 2)
                derive_scale([tcc, td1], 2, 4)
                p.fence()
            for seq in range(NSEQ):
                with nc.named_scope(f"l1fC1_s{seq}"):
                    emit_ffn1_C1(seq)
                    p.fence()
            with nc.named_scope("cc3"):
                tg3 = finish_gmax(54, 3, [])
                tcc = do_cc([tg3])
                derive_scale(tcc, 3, 6)
                p.fence()
            for seq in range(NSEQ):
                with nc.named_scope(f"l1fC2_s{seq}"):
                    emit_ffn1_C2(seq)
                    p.fence()
            with nc.named_scope("cc4"):
                tg4 = finish_gmax(55, 4, [])
                tcc = do_cc([tg4])
                derive_scale(tcc, 4, 8)
                p.fence()
            t_wd1 = [p.dma("sp", [], wdT1[:, ic, :], sc["l1dT"].ap()[ic])
                     for ic in range(24)]
            for seq in range(NSEQ):
                with nc.named_scope(f"l1fC3_s{seq}"):
                    emit_ffn1_C3(seq, xc[seq], y_out.ap()[seq], t_wd1)
                    p.fence()

        p.fence()
        if UPTO < 4:
            tlast = None
            for seq in range(NSEQ):
                for tt in range(16):
                    tlast = p.dma("sp", [tlast], y_out.ap()[seq][tt * 128:(tt + 1) * 128, :],
                                  xc[seq][tt * 128:(tt + 1) * 128, :])
            p._wait("sp", [tlast])

    return nc


def _host_arrange(inputs):
    f = np.float32
    A = np.ascontiguousarray
    b = np.asarray(inputs["L0_attn_attn_b"], f)
    bqk = np.zeros((128, 16), f)
    for hc in range(8):
        qcols = np.concatenate([np.arange(384 * hc, 384 * hc + 64),
                                np.arange(384 * hc + 192, 384 * hc + 256)])
        bqk[:, hc] = b[qcols]
        bqk[:, 8 + hc] = b[qcols + 64]
    vcols = np.concatenate([np.arange(192 * h + 128, 192 * h + 192) for h in range(16)])
    return {
        "l0_aw": A(np.asarray(inputs["L0_attn_attn_W"], f)),
        "l0_bqk": bqk,
        "l0_bv": A(b[vcols][None, :]),
        "l0_ow": A(np.asarray(inputs["L0_attn_out_W"], f)),
        "l0_ob": A(np.asarray(inputs["L0_attn_out_b"], f)[None, :]),
        "l0_wgT": A(np.asarray(inputs["L0_ffn_w_gate"], f).T),
        "l0_wuT": A(np.asarray(inputs["L0_ffn_w_up"], f).T),
        "l0_wdT": A(np.asarray(inputs["L0_ffn_w_down"], f).T),
        "l1_aw": A(np.asarray(inputs["L1_attn_w_qkv"], f)),
        "l1_ow": A(np.asarray(inputs["L1_attn_w_out"], f)),
        "l1_wg": A(np.asarray(inputs["L1_ffn_w_gate"], f)),
        "l1_wu": A(np.asarray(inputs["L1_ffn_w_up"], f)),
        "l1_wd": A(np.asarray(inputs["L1_ffn_w_down"], f)),
    }


def kernel(**inputs):
    x = np.asarray(inputs["L0_attn_encodings"], dtype=np.float32)
    w = _host_arrange(inputs)
    nc = build_program()
    in_maps = []
    for c in range(NCORES):
        m = dict(w)
        m["x_in"] = np.ascontiguousarray(x[c * NSEQ:(c + 1) * NSEQ])
        in_maps.append(m)
    res = run_bass_kernel_spmd(nc, in_maps, core_ids=list(range(NCORES)))
    return np.concatenate([r["y_out"] for r in res.results], axis=0)
